# version 12
# baseline (speedup 1.0000x reference)
import numpy as np
import ml_dtypes
from contextlib import ExitStack

import concourse.bass as bass
import concourse.mybir as mybir
from concourse.bass_utils import run_bass_kernel_spmd

F32 = mybir.dt.float32
BF16 = mybir.dt.bfloat16
AF = mybir.ActivationFunctionType
ALU = mybir.AluOpType
AX = mybir.AxisListType

NCORES = 8
D = 1024
C = 64
DFF = 2736
NE = 8
HME = 2048
GS = 512
EPS_LN = 1e-5
EPS_RMS = 1e-6
EPS_N = 1e-12
DFF_HALF_CHUNKS = [[128] * 11, [128] * 10 + [48]]
DFF_HALF_W = [sum(h) for h in DFF_HALF_CHUNKS]
DFF_HALF_OFF = [0, 1408]


class Sem:
    def __init__(self, stack, nc, name):
        self.h = stack.enter_context(nc.semaphore(name))
        self.n = 0


class _StopBuild(Exception):
    pass


SERIAL_B = False


def build_nc(TC=2048, stop_after=None):
    nt = TC // 128
    ng = TC // GS
    nch = TC // C
    NCHG = GS // C
    NTG = GS // 128
    n_th = 2 if TC >= 2048 else 1
    n_bh = 2 if TC >= 2048 else 1
    NG_BH_C = (TC // GS) // n_bh
    THS = TC // n_th
    nt_th = THS // 128
    ng_th = THS // GS

    nc = bass.Bass("TRN2", target_bir_lowering=False, debug=False)

    def din(name, shape, dt=F32):
        return nc.dram_tensor(name, shape, dt, kind="ExternalInput").ap()

    x_in = din("x", [TC, D])
    Wq = din("Wq", [D, D]); Wk = din("Wk", [D, D])
    Wv = din("Wv", [D, D]); Wg = din("Wg", [D, D])
    Wo = din("Wo", [D, D])
    w1 = din("w1", [D, DFF]); w2 = din("w2", [D, DFF]); w3 = din("w3", [DFF, D])
    g_attn = din("g_attn", [1, D]); g_ffn = din("g_ffn", [1, D])
    Wr = din("Wr", [D, NE]); br = din("br", [1, NE])
    W1 = din("W1", [NE, D, HME]); b1r = din("b1r", [NE, 128, HME // 128])
    W2 = din("W2", [NE, HME, D]); b2 = din("b2", [NE, D])
    ones_b = din("ones_b", [1, 128], BF16)
    ones_f = din("ones_f", [1, 128], F32)
    ones2_f = din("ones2_f", [128, 2], F32)
    bcast2_f = din("bcast2_f", [2, 128], F32)
    mask_tri = din("mask_tri", [128, C], F32)
    id_b = din("id_b", [128, 128], BF16)
    id_f = din("id_f", [128, 128], F32)
    rmask = din("rmask", [128, NCORES], F32)
    rmaskbar = din("rmaskbar", [128, NCORES], F32)

    out_t = nc.dram_tensor("out", [TC, D], F32, kind="ExternalOutput").ap()

    def dsc(name, shape, dt=F32):
        return nc.dram_tensor(name, shape, dt, kind="Internal").ap()

    sc_xn32 = dsc("sc_xn32", [TC, D])
    sc_xnb = dsc("sc_xnb", [TC, D], BF16)
    sc_xnbg = dsc("sc_xnbg", [TC, D], BF16)
    sc_rms = dsc("sc_rms", [1, TC])
    sc_qt = dsc("sc_qt", [8, 128, TC], BF16)
    sc_kt = dsc("sc_kt", [8, 128, TC], BF16)
    sc_v = dsc("sc_v", [nch * C, D], BF16)
    sc_sloc = dsc("sc_sloc", [128, 8 * nch * C], BF16)
    sc_s1 = dsc("sc_s1", [TC, D])
    sc_h2 = dsc("sc_h2", [TC, D], BF16)
    sc_g = dsc("sc_g", [TC, D])
    cc_inM = dsc("cc_inM", [8 * 128, C])
    cc_outM = dsc("cc_outM", [NCORES * 8 * 128, C])
    cc_inD = dsc("cc_inD", [128, 8])
    sc_gates = dsc("sc_gates", [1, NE * TC], BF16)
    cc_outD = dsc("cc_outD", [NCORES * 128, 8])

    with ExitStack() as st:
        scopes = {}

        def sbt(name, shape, dt=F32, scope=None, tdest=False):
            tgt = st if scope is None else scopes.setdefault(scope, ExitStack())
            return tgt.enter_context(
                nc.sbuf_tensor(name, shape, dt, is_transpose_dest=tdest))

        def close_scope(name):
            if name in scopes:
                scopes.pop(name).close()

        def pst(name, shape, dt=F32):
            return st.enter_context(nc.psum_tensor(name, shape, dt))

        onesb = sbt("onesb", [1, 128], BF16)
        onesf = sbt("onesf", [1, 128], F32)
        ones2f = sbt("ones2f", [128, 2], F32)
        bc2f = sbt("bc2f", [2, 128], F32)
        maskA = sbt("maskA", [128, C], F32)
        idb = sbt("idb", [128, 128], BF16)
        idf = sbt("idf", [128, 128], F32)
        rmk = sbt("rmk", [128, NCORES], F32)
        rmkb = sbt("rmkb", [128, NCORES], F32)
        garow = sbt("garow", [1, D], F32)
        gfrow = sbt("gfrow", [1, D], F32)
        gfbc = sbt("gfbc", [128, D], F32)
        brf = sbt("brf", [1, NE], F32)
        b2s = sbt("b2s", [NE, D], F32)
        rmsr = sbt("rmsr", [128, nt], F32)
        zeros64 = sbt("zeros64", [128, C], F32)
        constv = sbt("constv", [128, 4], F32)
        junk1 = sbt("junk1", [128, 1], F32)
        gam = sbt("gam", [128, 8 * nch], F32)

        stat = sbt("stat", [128, 8])
        rstd_c = sbt("rstd_c", [128, 1])





        P_A = pst("P_A", [128, 512])
        P_B = pst("P_B", [128, 512])
        P_P0 = pst("P_P0", [128, 512])
        P_P1 = pst("P_P1", [128, 512])
        P_N = pst("P_N", [128, 512])
        P_BC = pst("P_BC", [128, 512])
        P_TRB = pst("P_TRB", [128, 1024], BF16)
        P_TRF = P_TRB.ap().bitcast(F32)
        P_M = pst("P_M", [128, 512])

        LD = Sem(st, nc, "LD")
        GD = Sem(st, nc, "GD")
        GD1 = Sem(st, nc, "GD1")
        GD2 = Sem(st, nc, "GD2")
        PE = Sem(st, nc, "PE")
        DV = Sem(st, nc, "DV")
        AC = Sem(st, nc, "AC")
        CCS = Sem(st, nc, "CCS")
        OUT = Sem(st, nc, "OUT")
        BAR = Sem(st, nc, "BAR")

        bankfree = {}

        def bfree(name, sem):
            bankfree[name] = (sem.h, sem.n)

        def bwait(eng, *names):
            for nm in names:
                if nm in bankfree:
                    h, n = bankfree[nm]
                    eng.wait_ge(h, n)

        block = st.enter_context(nc.Block())
        nbar = [0]

        def maybe_stop(tag, dumps=()):
            if stop_after == tag:
                @block.gpsimd
                def _(gq):
                    gq.wait_ge(CCS.h, CCS.n)
                    gq.wait_ge(LD.h, LD.n)
                    gq.wait_ge(GD.h, GD.n)
                    for nm, src in dumps:
                        dt_ = src.dtype if hasattr(src, "dtype") else F32
                        dd = nc.dram_tensor("dbg_" + nm, list(src.shape), dt_,
                                            kind="ExternalOutput").ap()
                        gq.dma_start(out=dd, in_=src).then_inc(OUT.h, 16)
                        OUT.n += 16
                    gq.wait_ge(OUT.h, OUT.n)
                barrier()
                for k in list(scopes.keys())[::-1]:
                    scopes.pop(k).close()
                raise _StopBuild()

        def barrier():
            nbar[0] += 1
            tgt = 5 * nbar[0]
            for deco in (block.sync, block.gpsimd, block.tensor,
                         block.vector, block.scalar):
                @deco
                def _(q):
                    q.sem_inc(BAR.h, 1)
                    q.wait_ge(BAR.h, tgt)

        try:
            @block.sync
            def _(q):
                for dst, src in ((onesb, ones_b), (onesf, ones_f),
                                 (ones2f, ones2_f), (bc2f, bcast2_f),
                                 (maskA, mask_tri), (idb, id_b), (idf, id_f),
                                 (rmk, rmask), (rmkb, rmaskbar),
                                 (garow, g_attn), (gfrow, g_ffn), (brf, br),
                                 (b2s, b2)):
                    q.dma_start(out=dst[:], in_=src).then_inc(LD.h, 16)
                    LD.n += 16

            @block.vector
            def _(v):
                v.wait_ge(LD.h, LD.n)
                v.memset(zeros64[:], 0.0)
                v.memset(constv[:, 0:1], EPS_LN)
                v.memset(constv[:, 1:2], EPS_RMS)
                v.memset(constv[:, 2:3], -0.5)
                v.memset(constv[:, 3:4], 1.0 / D).then_inc(DV.h, 1)
                DV.n += 1

            barrier()

            gabc = sbt("gabc", [128, D], F32, scope="A1")
            xt = [sbt(f"xt{i}", [128, D], scope="A1") for i in range(2)]
            xnt = sbt("xnt", [128, D], scope="A1")
            xnbt = sbt("xnbt", [128, D], BF16, scope="A1")
            xnbgt = sbt("xnbgt", [128, D], BF16, scope="A1")
            sq = sbt("sq", [128, D], scope="A1")

            maybe_stop("p0")
            @block.tensor
            def _(t):
                t.wait_ge(LD.h, LD.n)
                t.matmul(P_A[:], onesf[:], garow[0:1, 0:512], start=True, stop=True)
                t.matmul(P_B[:], onesf[:], garow[0:1, 512:1024], start=True, stop=True)
                t.matmul(P_P0[:], onesf[:], gfrow[0:1, 0:512], start=True, stop=True)
                t.matmul(P_P1[:], onesf[:], gfrow[0:1, 512:1024],
                         start=True, stop=True).then_inc(PE.h, 1)
                PE.n += 1

            @block.vector
            def _(v):
                v.wait_ge(PE.h, PE.n)
                v.tensor_copy(gabc[:, 0:512], P_A[:])
                v.tensor_copy(gabc[:, 512:1024], P_B[:])
                v.tensor_copy(gfbc[:, 0:512], P_P0[:])
                v.tensor_copy(gfbc[:, 512:1024], P_P1[:]).then_inc(DV.h, 1)
                DV.n += 1

            x_ld = {}

            @block.sync
            def _(q):
                for t in range(min(2, nt)):
                    q.dma_start(out=xt[t % 2][:],
                                in_=x_in[t * 128:(t + 1) * 128, :]).then_inc(LD.h, 16)
                    LD.n += 16
                    x_ld[t] = LD.n

            dv_done = {}
            ac_done = {}
            spill_done = {}
            for t in range(nt):
                xb = xt[t % 2]

                @block.vector
                def _(v, t=t, xb=xb):
                    v.wait_ge(LD.h, LD.n)
                    v.reduce_sum(stat[:, 0:1], xb[:], axis=AX.X)
                    v.tensor_copy(zeros64[:], zeros64[:])
                    v.tensor_scalar_mul(stat[:, 1:2], stat[:, 0:1], 1.0 / D)
                    v.tensor_copy(zeros64[:], zeros64[:])
                    v.tensor_mul(stat[:, 2:3], stat[:, 1:2], stat[:, 1:2]
                                 ).then_inc(DV.h, 1)
                    DV.n += 1

                @block.scalar
                def _(s, t=t, xb=xb):
                    s.wait_ge(LD.h, LD.n)
                    s.activation(sq[:], xb[:], AF.Square,
                                 accum_out=stat[:, 3:4]).then_inc(AC.h, 1)
                    AC.n += 1

                @block.vector
                def _(v, t=t):
                    v.wait_ge(AC.h, AC.n)
                    v.tensor_scalar_mul(stat[:, 4:5], stat[:, 3:4], 1.0 / D)
                    v.tensor_copy(zeros64[:], zeros64[:])
                    v.tensor_sub(stat[:, 5:6], stat[:, 4:5], stat[:, 2:3]
                                 ).then_inc(DV.h, 1)
                    DV.n += 1

                @block.scalar
                def _(s, t=t):
                    s.wait_ge(DV.h, DV.n)
                    s.activation(stat[:, 6:7], stat[:, 5:6], AF.Ln, bias=constv[:, 0:1])
                    s.activation(rstd_c[:], stat[:, 6:7], AF.Exp,
                                 scale=constv[:, 2:3]).then_inc(AC.h, 1)
                    AC.n += 1

                @block.vector
                def _(v, t=t, xb=xb):
                    v.wait_ge(AC.h, AC.n)
                    if t >= 1:
                        v.wait_ge(LD.h, LD.n)
                    v.tensor_scalar(xnt[:], xb[:], stat[:, 1:2], rstd_c[:],
                                    ALU.subtract, ALU.mult)
                    v.tensor_tensor(xnbgt[:], xnt[:], gabc[:], ALU.mult
                                    ).then_inc(DV.h, 1)
                    DV.n += 1
                    dv_done[t] = DV.n

                @block.scalar
                def _(s, t=t):
                    s.wait_ge(DV.h, dv_done[t])
                    if t >= 1:
                        s.wait_ge(LD.h, LD.n)
                    s.copy(xnbt[:], xnt[:])
                    s.activation(sq[:], xnt[:], AF.Square, accum_out=stat[:, 7:8])
                    s.activation(stat[:, 6:7], stat[:, 7:8], AF.Ln,
                                 bias=constv[:, 1:2], scale=constv[:, 3:4])
                    s.activation(rmsr[:, t:t + 1], stat[:, 6:7], AF.Exp,
                                 scale=constv[:, 2:3]).then_inc(AC.h, 1)
                    AC.n += 1
                    ac_done[t] = AC.n

                @block.sync
                def _(q, t=t):
                    q.wait_ge(DV.h, dv_done[t])
                    q.dma_start(out=sc_xn32[t * 128:(t + 1) * 128, :],
                                in_=xnt[:]).then_inc(LD.h, 16)
                    q.dma_start(out=sc_xnbg[t * 128:(t + 1) * 128, :],
                                in_=xnbgt[:]).then_inc(LD.h, 16)
                    LD.n += 32
                    q.wait_ge(AC.h, ac_done[t])
                    q.dma_start(out=sc_xnb[t * 128:(t + 1) * 128, :],
                                in_=xnbt[:]).then_inc(LD.h, 16)
                    q.dma_start(out=sc_rms[0:1, t * 128:(t + 1) * 128]
                                .rearrange("a p -> p a"),
                                in_=rmsr[:, t:t + 1]).then_inc(LD.h, 16)
                    LD.n += 32
                    if t + 2 < nt:
                        q.dma_start(out=xt[t % 2][:],
                                    in_=x_in[(t + 2) * 128:(t + 3) * 128, :]
                                    ).then_inc(LD.h, 16)
                        LD.n += 16
                        x_ld[t + 2] = LD.n

                spill_done[t] = LD.n

            a1_spills_done = LD.n
            barrier()

            close_scope("A1")
            xn_fs = [sbt(f"xn_f{i}", [128, TC], BF16, scope="AB", tdest=True)
                 for i in range(8)]
            l_run = sbt("l_run", [128, 8 * C], F32, scope="A2")
            gam_run = sbt("gam_run", [128, 8], F32, scope="A2")
            s_local = sbt("s_local", [128, 8 * nch * C], BF16, scope="A2")
            wqb = sbt("wqb", [128, 8 * D], BF16, scope="A2")
            wkb = sbt("wkb", [128, 8 * D], BF16, scope="A2")
            wvb = sbt("wvb", [128, 8 * D], BF16, scope="A2")
            wgb = sbt("wgb", [128, 8 * D], BF16, scope="A2")
            rstd_row = sbt("rstd_row", [1, TC], F32, scope="A2")
            rstdb = sbt("rstdb", [128, GS], F32, scope="A2")
            a_oc = sbt("a_oc", [128, GS], F32, scope="A2")
            ra_oc = sbt("ra_oc", [128, GS], F32, scope="A2")
            az = sbt("az", [128, GS], F32, scope="A2")
            b64 = [sbt(f"b64_{i}", [128, C], scope="A2") for i in range(NCHG)]
            rb64 = [sbt(f"rb64_{i}", [128, C], scope="A2") for i in range(NCHG)]
            sqf = sbt("sqf", [128, GS], F32, scope="A2")
            nn2 = sbt("nn2", [2, GS], F32, scope="A2")
            invn2 = sbt("invn2", [2, GS], F32, scope="A2")
            invb = sbt("invb", [128, GS], F32, scope="A2")
            qn_oc = sbt("qn_oc", [128, GS], F32, scope="A2")
            qtb_oc = sbt("qtb_oc", [128, GS], BF16, scope="A2")
            kt32 = sbt("kt32", [128, C], F32, scope="A2")
            khb = sbt("khb", [128, C], BF16, scope="A2")
            khtok = sbt("khtok", [128, 128], BF16, scope="A2")
            vtile = [sbt(f"vtile{i}", [128, D], BF16, scope="A2") for i in range(NTG)]

            maybe_stop("a1", [("xn32", sc_xn32), ("xnb", sc_xnb), ("xnbg", sc_xnbg), ("rms", sc_rms)])
            @block.sync
            def _(q):
                q.wait_ge(LD.h, LD.n)
                for fc in range(8):
                    q.dma_start(out=xn_fs[fc][:],
                                in_=sc_xnbg[:, fc * 128:(fc + 1) * 128],
                                transpose=True).then_inc(LD.h, 16)
                    LD.n += 16
                q.dma_start(out=rstd_row[:], in_=sc_rms).then_inc(LD.h, 16)
                LD.n += 16

            @block.gpsimd
            def _(g):
                for wdst, wsrc in ((wqb, Wq), (wkb, Wk), (wvb, Wv), (wgb, Wg)):
                    for fc in range(8):
                        g.dma_start(out=wdst[:, fc * D:(fc + 1) * D],
                                    in_=wsrc[fc * 128:(fc + 1) * 128, :]
                                    ).then_inc(GD.h, 16)
                        GD.n += 16

            a2_ld = LD.n
            a2_gd = GD.n

            @block.vector
            def _(v):
                v.memset(l_run[:], 0.0)
                v.memset(gam_run[:], 1.0).then_inc(DV.h, 1)
                DV.n += 1
            maybe_stop("a2m")

            @block.sync
            def _(q):
                q.wait_ge(LD.h, LD.n)
            maybe_stop("a2t")

            @block.gpsimd
            def _(gq):
                gq.wait_ge(GD.h, GD.n)
            maybe_stop("a2w")

            def a2_proj(g, oc, PP, wsb, PPn):
                @block.tensor
                def _(t):
                    bwait(t, PPn)
                    last = None
                    for fc in range(8):
                        last = t.matmul(
                            PP[:],
                            wsb[:, fc * D + oc * 128: fc * D + (oc + 1) * 128],
                            xn_fs[fc][:, g * GS:(g + 1) * GS],
                            start=(fc == 0), stop=(fc == 7))
                    last.then_inc(PE.h, 1)
                    PE.n += 1

            def a2_l2norm(PP, PPn):
                @block.scalar
                def _(s):
                    s.wait_ge(PE.h, PE.n)
                    s.activation(sqf[:], PP[:], AF.Square).then_inc(AC.h, 1)
                    AC.n += 1

                @block.tensor
                def _(t):
                    t.wait_ge(AC.h, AC.n)
                    t.matmul(P_N[0:2, :], ones2f[:], sqf[:], start=True, stop=True
                             ).then_inc(PE.h, 1)
                    PE.n += 1

                @block.scalar
                def _(s):
                    s.wait_ge(PE.h, PE.n)
                    s.activation(nn2[:], P_N[0:2, :], AF.Sqrt).then_inc(AC.h, 1)
                    AC.n += 1

                @block.vector
                def _(v):
                    v.wait_ge(AC.h, AC.n)
                    v.tensor_scalar_max(nn2[:], nn2[:], EPS_N)
                    v.reciprocal(invn2[:], nn2[:]).then_inc(DV.h, 1)
                    DV.n += 1

                @block.tensor
                def _(t):
                    t.wait_ge(DV.h, DV.n)
                    t.matmul(P_BC[:], bc2f[:], invn2[:], start=True, stop=True
                             ).then_inc(PE.h, 1)
                    PE.n += 1

                @block.vector
                def _(v):
                    v.wait_ge(PE.h, PE.n)
                    v.tensor_copy(invb[:], P_BC[:])
                    v.tensor_tensor(qn_oc[:], PP[:], invb[:], ALU.mult
                                    ).then_inc(DV.h, 1)
                    DV.n += 1
                bfree(PPn, DV)

            vspill = {}
            kspill_ld = [LD.n]
            a2_ld_now = [LD.n]
            a2_gd_now = [GD.n]
            for g in range(ng):
                @block.tensor
                def _(t, g=g):
                    t.wait_ge(LD.h, a2_ld_now[0])
                    t.wait_ge(GD.h, a2_gd_now[0])
                    t.wait_ge(DV.h, DV.n)
                    t.matmul(P_BC[:], onesf[:], rstd_row[0:1, g * GS:(g + 1) * GS],
                             start=True, stop=True).then_inc(PE.h, 1)
                    PE.n += 1

                @block.vector
                def _(v, g=g):
                    v.wait_ge(PE.h, PE.n)
                    v.tensor_copy(rstdb[:], P_BC[:]).then_inc(DV.h, 1)
                    DV.n += 1
                if g == 0:
                    maybe_stop("a2r")

                for tl in range(NTG):
                    tt = g * NTG + tl

                    @block.tensor
                    def _(t, tt=tt):
                        bwait(t, "P_A", "P_B")
                        last = None
                        for fc in range(8):
                            lhs = xn_fs[fc][:, tt * 128:(tt + 1) * 128]
                            t.matmul(P_A[:], lhs, wvb[:, fc * D:fc * D + 512],
                                     start=(fc == 0), stop=(fc == 7))
                            last = t.matmul(P_B[:], lhs,
                                            wvb[:, fc * D + 512:(fc + 1) * D],
                                            start=(fc == 0), stop=(fc == 7))
                        last.then_inc(PE.h, 1)
                        PE.n += 1

                    if g == 0 and tl == 0:
                        maybe_stop("a2vm")

                    @block.vector
                    def _(v, tl=tl, tt=tt, g=g):
                        v.wait_ge(PE.h, PE.n)
                        if g > 0:
                            v.wait_ge(LD.h, LD.n)
                        v.tensor_scalar_mul(vtile[tl][:, 0:512], P_A[:],
                                            rmsr[:, tt:tt + 1])
                        v.tensor_scalar_mul(vtile[tl][:, 512:1024], P_B[:],
                                            rmsr[:, tt:tt + 1]).then_inc(DV.h, 1)
                        DV.n += 1
                    bfree("P_A", DV)
                    bfree("P_B", DV)

                    if g == 0 and tl == 0:
                        maybe_stop("a2ve")

                    @block.sync
                    def _(q, tl=tl, tt=tt):
                        q.wait_ge(DV.h, DV.n)
                        for half in range(2):
                            q.dma_start(out=sc_v[(tt * 2 + half) * C:
                                                 (tt * 2 + half + 1) * C, :],
                                        in_=vtile[tl][half * 64:(half + 1) * 64, :]
                                        ).then_inc(LD.h, 16)
                            LD.n += 16
                    vspill[tl] = LD.n
                if g == 0:
                    maybe_stop("a2v")

                for oc in range(8):
                    PP = P_P0 if oc % 2 == 0 else P_P1
                    PPn = 'P_P0' if oc % 2 == 0 else 'P_P1'

                    a2_proj(g, oc, PP, wgb, PPn)

                    @block.vector
                    def _(v, oc=oc, PP=PP):
                        v.wait_ge(PE.h, PE.n)
                        v.tensor_tensor(az[:], PP[:], rstdb[:], ALU.mult
                                        ).then_inc(DV.h, 1)
                        DV.n += 1
                    bfree(PPn, DV)

                    @block.scalar
                    def _(s, oc=oc):
                        s.wait_ge(DV.h, DV.n)
                        s.activation(a_oc[:], az[:], AF.Sigmoid).then_inc(AC.h, 1)
                        AC.n += 1

                    @block.vector
                    def _(v, oc=oc):
                        v.wait_ge(AC.h, AC.n)
                        v.reciprocal(ra_oc[:], a_oc[:])
                        last = None
                        for c in range(NCHG):
                            v.tensor_tensor_scan(b64[c][:], a_oc[:, c * C:(c + 1) * C],
                                                 zeros64[:], 1.0, ALU.mult, ALU.add)
                            last = v.tensor_tensor_scan(
                                rb64[c][:], ra_oc[:, c * C:(c + 1) * C],
                                zeros64[:], 1.0, ALU.mult, ALU.add)
                        last.then_inc(DV.h, 1)
                        DV.n += 1

                    if g == 0 and oc == 0:
                        maybe_stop("a2zg")
                    a2_proj(g, oc, PP, wqb, PPn)
                    a2_l2norm(PP, PPn)

                    @block.vector
                    def _(v, g=g, oc=oc):
                        v.wait_ge(LD.h, LD.n)
                        last = None
                        for c in range(NCHG):
                            last = v.tensor_tensor(qtb_oc[:, c * C:(c + 1) * C],
                                                   qn_oc[:, c * C:(c + 1) * C],
                                                   b64[c][:], ALU.mult)
                        last.then_inc(DV.h, 1)
                        DV.n += 1

                    @block.sync
                    def _(q, g=g, oc=oc):
                        q.wait_ge(DV.h, DV.n)
                        q.dma_start(out=sc_qt[oc, :, g * GS:(g + 1) * GS],
                                    in_=qtb_oc[:]).then_inc(LD.h, 16)
                        LD.n += 16
                    q_spill_ld = LD.n

                    if g == 0 and oc == 0:
                        maybe_stop("a2q")
                    a2_proj(g, oc, PP, wkb, PPn)
                    a2_l2norm(PP, PPn)

                    for c in range(NCHG):
                        n_glob = g * NCHG + c
                        bb = (c % 2) * 64

                        @block.vector
                        def _(v, g=g, oc=oc, c=c):
                            if c == 0:
                                v.wait_ge(LD.h, LD.n)
                            v.tensor_tensor(kt32[:], qn_oc[:, c * C:(c + 1) * C],
                                            rb64[c][:], ALU.mult)
                            v.tensor_copy(qtb_oc[:, c * C:(c + 1) * C], kt32[:])
                            v.tensor_scalar_mul(khb[:], kt32[:], b64[c][:, C - 1:C]
                                                ).then_inc(DV.h, 1)
                            DV.n += 1

                        @block.tensor
                        def _(t, c=c):
                            t.wait_ge(DV.h, DV.n)
                            t.transpose(P_TRB[0:64, 0:128], khb[:], idb[:]).then_inc(PE.h, 1)
                            PE.n += 1

                        @block.vector
                        def _(v, c=c, bb=bb):
                            v.wait_ge(PE.h, PE.n)
                            v.tensor_copy(khtok[bb:bb + 64, :], P_TRB[0:64, 0:128]
                                          ).then_inc(DV.h, 1)
                            DV.n += 1

                        @block.tensor
                        def _(t, c=c, bb=bb, oc=oc):
                            t.wait_ge(DV.h, DV.n)
                            vt = vtile[c // 2]
                            t.matmul(P_M[0:64, 0:64], khtok[bb:bb + 64, 0:64],
                                     vt[bb:bb + 64, oc * 128:oc * 128 + 64],
                                     start=True, stop=True, tile_position=(bb, 0))
                            t.matmul(P_BC[64:128, 0:64], khtok[bb:bb + 64, 64:128],
                                     vt[bb:bb + 64, oc * 128 + 64:(oc + 1) * 128],
                                     start=True, stop=True, tile_position=(bb, 64)
                                     ).then_inc(PE.h, 1)
                            PE.n += 1

                        @block.vector
                        def _(v, oc=oc, c=c, n_glob=n_glob):
                            v.wait_ge(PE.h, PE.n)
                            v.tensor_copy(
                                s_local[:, (oc * nch + n_glob) * C:
                                        (oc * nch + n_glob + 1) * C],
                                l_run[:, oc * C:(oc + 1) * C])
                            v.tensor_scalar_mul(l_run[:, oc * C:(oc + 1) * C],
                                                l_run[:, oc * C:(oc + 1) * C],
                                                b64[c][:, C - 1:C])
                            v.tensor_copy(gam[:, oc * nch + n_glob:
                                              oc * nch + n_glob + 1],
                                          gam_run[:, oc:oc + 1])
                            v.tensor_tensor(l_run[0:64, oc * C:(oc + 1) * C],
                                            P_M[0:64, 0:64],
                                            l_run[0:64, oc * C:(oc + 1) * C],
                                            ALU.add)
                            v.tensor_tensor(l_run[64:128, oc * C:(oc + 1) * C],
                                            P_BC[64:128, 0:64],
                                            l_run[64:128, oc * C:(oc + 1) * C],
                                            ALU.add)
                            v.tensor_scalar_mul(gam_run[:, oc:oc + 1],
                                                gam_run[:, oc:oc + 1],
                                                b64[c][:, C - 1:C]).then_inc(DV.h, 1)
                            DV.n += 1

                    @block.sync
                    def _(q, g=g, oc=oc):
                        q.wait_ge(DV.h, DV.n)
                        q.dma_start(out=sc_kt[oc, :, g * GS:(g + 1) * GS],
                                    in_=qtb_oc[:]).then_inc(LD.h, 16)
                        LD.n += 16
                    kspill_ld[0] = LD.n
                    if g == 0 and oc == 0:
                        maybe_stop("a2k")

            @block.sync
            def _(q):
                q.wait_ge(DV.h, DV.n)
                q.dma_start(out=sc_sloc, in_=s_local[:]).then_inc(LD.h, 16)
                LD.n += 16
            a2_spills_done = LD.n

            barrier()

            maybe_stop("a2", [("qt", sc_qt), ("kt", sc_kt), ("v", sc_v), ("sloc", sc_sloc), ("lrun", l_run[:]), ("gam", gam[:]), ("gamrun", gam_run[:])])
            @block.gpsimd
            def _(gq):
                for p in range(8):
                    gq.dma_start(out=cc_inM[p * 128:(p + 1) * 128, :],
                                 in_=l_run[:, p * C:(p + 1) * C]).then_inc(GD.h, 16)
                    GD.n += 16
                gq.dma_start(out=cc_inD, in_=gam_run[:]).then_inc(GD.h, 16)
                GD.n += 16
                gq.wait_ge(GD.h, GD.n)
                gq.collective_compute(
                    "AllGather", ALU.bypass,
                    replica_groups=[list(range(NCORES))],
                    ins=[cc_inM.opt()], outs=[cc_outM.opt()],
                ).then_inc(CCS.h, 1)
                gq.collective_compute(
                    "AllGather", ALU.bypass,
                    replica_groups=[list(range(NCORES))],
                    ins=[cc_inD.opt()], outs=[cc_outD.opt()],
                ).then_inc(CCS.h, 1)
                CCS.n += 2

            close_scope("A2")
            gates_rows = sbt("gates_rows", [NE, TC], F32, scope="B")
            gates_rows_b = sbt("gates_rows_b", [NE, TC], BF16, scope="B")
            gflbuf = sbt("gflbuf", [1, GS], BF16, scope="B")
            wrb = sbt("wrb", [128, 8 * NE], BF16, scope="B")
            lg = sbt("lg", [128, 2 * NE], scope="B")
            srt = sbt("srt", [128, 2 * NE], scope="B")
            sel = sbt("sel", [128, 2 * NE], scope="B")
            ega = sbt("ega", [128, 2 * NE], scope="B")
            den2 = sbt("den2", [128, 2], scope="B")
            di2 = sbt("di2", [128, 2], scope="B")
            gat2 = sbt("gat2", [128, 2 * NE], scope="B")
            w1b = sbt("w1b", [128, 8 * HME], BF16, scope="B")
            w2b = sbt("w2b", [128, 16 * D], BF16, scope="B")
            b1s = sbt("b1s", [128, NE * (HME // 128)], F32, scope="B")
            gbc = sbt("gbc", [128, GS], BF16, scope="B")
            htmp = [sbt(f"htmp{i}", [128, GS], F32, scope="B") for i in range(2)]
            hhat = sbt("hhat", [128, NG_BH_C * 16 * GS], BF16, scope="B")
            macc = sbt("macc", [128, (nt // n_bh) * D], F32, scope="B")
            silt = sbt("silt", [128, D], F32, scope="B")

            maybe_stop("x", [("ccM", cc_outM), ("ccD", cc_outD)])
            @block.sync
            def _(q):
                for fc in range(8):
                    q.dma_start(out=xn_fs[fc][:],
                                in_=sc_xnb[:, fc * 128:(fc + 1) * 128],
                                transpose=True).then_inc(LD.h, 16)
                    LD.n += 16

            @block.gpsimd
            def _(gq):
                for fc in range(8):
                    gq.dma_start(out=wrb[:, fc * NE:(fc + 1) * NE],
                                 in_=Wr[fc * 128:(fc + 1) * 128, :]).then_inc(GD.h, 16)
                    GD.n += 16
                for e in range(NE):
                    gq.dma_start(out=b1s[:, e * 16:(e + 1) * 16],
                                 in_=b1r[e]).then_inc(GD.h, 16)
                    GD.n += 16

            b_ld = LD.n
            b_gd = GD.n

            for tp in range(0, nt, 2):
                @block.tensor
                def _(t, tp=tp):
                    t.wait_ge(LD.h, LD.n)
                    t.wait_ge(GD.h, GD.n)
                    t.wait_ge(DV.h, DV.n)
                    last = None
                    for j in range(2):
                        tt = tp + j
                        PP = P_P0 if j == 0 else P_P1
                        for fc in range(8):
                            t.matmul(PP[:, 0:NE],
                                     xn_fs[fc][:, tt * 128:(tt + 1) * 128],
                                     wrb[:, fc * NE:(fc + 1) * NE],
                                     start=(fc == 0), stop=False)
                        last = t.matmul(PP[:, 0:NE], onesf[:], brf[:],
                                        start=False, stop=True)
                    last.then_inc(PE.h, 1)
                    PE.n += 1

                @block.vector
                def _(v, tp=tp):
                    v.wait_ge(PE.h, PE.n)
                    v.tensor_copy(lg[:, 0:NE], P_P0[:, 0:NE])
                    v.tensor_copy(lg[:, NE:2 * NE], P_P1[:, 0:NE])
                    v.tensor_copy(zeros64[:], zeros64[:])
                    v.tensor_copy(zeros64[:], zeros64[:])
                    v.max(srt[:, 0:NE], lg[:, 0:NE])
                    v.max(srt[:, NE:2 * NE], lg[:, NE:2 * NE])
                    v.tensor_copy(zeros64[:], zeros64[:])
                    v.tensor_copy(zeros64[:], zeros64[:])
                    v.tensor_scalar(sel[:, 0:NE], lg[:, 0:NE], srt[:, 1:2], 0.0,
                                    ALU.is_ge, ALU.add)
                    v.tensor_scalar(sel[:, NE:2 * NE], lg[:, NE:2 * NE],
                                    srt[:, NE + 1:NE + 2], 0.0, ALU.is_ge, ALU.add)
                    v.tensor_scalar(ega[:, 0:NE], lg[:, 0:NE], srt[:, 0:1], 0.0,
                                    ALU.subtract, ALU.add)
                    v.tensor_scalar(ega[:, NE:2 * NE], lg[:, NE:2 * NE],
                                    srt[:, NE:NE + 1], 0.0, ALU.subtract, ALU.add
                                    ).then_inc(DV.h, 1)
                    DV.n += 1

                @block.scalar
                def _(s, tp=tp):
                    s.wait_ge(DV.h, DV.n)
                    s.activation(ega[:, 0:2 * NE], ega[:, 0:2 * NE], AF.Exp
                                 ).then_inc(AC.h, 1)
                    AC.n += 1

                @block.vector
                def _(v, tp=tp):
                    v.wait_ge(AC.h, AC.n)
                    v.tensor_mul(ega[:, 0:2 * NE], ega[:, 0:2 * NE],
                                 sel[:, 0:2 * NE])
                    v.tensor_copy(zeros64[:], zeros64[:])
                    v.reduce_sum(den2[:, 0:1], ega[:, 0:NE], axis=AX.X)
                    v.reduce_sum(den2[:, 1:2], ega[:, NE:2 * NE], axis=AX.X)
                    v.tensor_copy(zeros64[:], zeros64[:])
                    v.tensor_copy(zeros64[:], zeros64[:])
                    v.reciprocal(di2[:], den2[:])
                    v.tensor_copy(zeros64[:], zeros64[:])
                    v.tensor_copy(zeros64[:], zeros64[:])
                    v.tensor_copy(junk1[:], di2[:, 0:1])
                    v.tensor_scalar_mul(gat2[:, 0:NE], ega[:, 0:NE], di2[:, 0:1])
                    v.tensor_scalar_mul(gat2[:, NE:2 * NE], ega[:, NE:2 * NE],
                                        di2[:, 1:2]).then_inc(DV.h, 1)
                    DV.n += 1

                @block.tensor
                def _(t, tp=tp):
                    t.wait_ge(DV.h, DV.n)
                    t.transpose(P_BC[0:NE, 0:128], gat2[:, 0:NE], idf[:])
                    t.transpose(P_A[0:NE, 0:128], gat2[:, NE:2 * NE], idf[:]
                                ).then_inc(PE.h, 1)
                    PE.n += 1

                @block.vector
                def _(v, tp=tp):
                    v.wait_ge(PE.h, PE.n)
                    v.tensor_copy(gates_rows[:, tp * 128:(tp + 1) * 128],
                                  P_BC[0:NE, 0:128])
                    v.tensor_copy(gates_rows[:, (tp + 1) * 128:(tp + 2) * 128],
                                  P_A[0:NE, 0:128])
                    v.tensor_copy(gates_rows_b[:, tp * 128:(tp + 1) * 128],
                                  P_BC[0:NE, 0:128])
                    v.tensor_copy(gates_rows_b[:, (tp + 1) * 128:(tp + 2) * 128],
                                  P_A[0:NE, 0:128]).then_inc(DV.h, 1)
                    DV.n += 1

            @block.sync
            def _(q):
                q.wait_ge(DV.h, DV.n)
                q.dma_start(out=sc_gates.rearrange("a (e t) -> (a e) t", e=NE),
                            in_=gates_rows_b[:]).then_inc(LD.h, 16)
                LD.n += 16
            gfl_ld = LD.n

            NT_BH = nt // n_bh
            NG_BH = ng // n_bh
            silt_sp = [LD.n]
            pe_h_done = [PE.n]
            pe_w2_done = [PE.n]

            @block.gpsimd
            def _(gq):
                for fc in range(8):
                    gq.dma_start(out=w1b[:, fc * HME:(fc + 1) * HME],
                                 in_=W1[0, fc * 128:(fc + 1) * 128, :]
                                 ).then_inc(GD1.h, 16)
                    GD1.n += 16

            eseq = [(bh, e) for bh in range(n_bh) for e in range(NE)]
            for idx, (bh, e) in enumerate(eseq):
                @block.gpsimd
                def _(gq, bh=bh, e=e):
                    gq.wait_ge(PE.h, PE.n if SERIAL_B else pe_w2_done[0])
                    for hc in range(16):
                        gq.dma_start(out=w2b[:, hc * D:(hc + 1) * D],
                                     in_=W2[e, hc * 128:(hc + 1) * 128, :]
                                     ).then_inc(GD2.h, 16)
                        GD2.n += 16

                for gl in range(NG_BH):
                    g = bh * NG_BH + gl

                    @block.sync
                    def _(q, e=e, g=g):
                        q.wait_ge(PE.h, PE.n)
                        q.wait_ge(LD.h, LD.n)
                        q.dma_start(
                            out=gflbuf[:],
                            in_=sc_gates[0:1, e * TC + g * GS:
                                         e * TC + (g + 1) * GS]
                            ).then_inc(LD.h, 16)
                        LD.n += 16

                    @block.tensor
                    def _(t, e=e, g=g):
                        t.wait_ge(GD1.h, GD1.n)
                        t.wait_ge(LD.h, LD.n)
                        bwait(t, "P_BC")
                        t.matmul(P_BC[:], onesb[:], gflbuf[:],
                                 start=True, stop=True).then_inc(PE.h, 1)
                        PE.n += 1

                    @block.vector
                    def _(v):
                        v.wait_ge(PE.h, PE.n)
                        v.tensor_copy(gbc[:], P_BC[:]).then_inc(DV.h, 1)
                        DV.n += 1
                    bfree("P_BC", DV)

                    for hc in range(16):
                        PPn = "P_A" if hc % 2 == 0 else "P_B"
                        PP = P_A if hc % 2 == 0 else P_B
                        HT = htmp[hc % 2]

                        @block.tensor
                        def _(t, e=e, g=g, hc=hc, PP=PP, PPn=PPn):
                            bwait(t, PPn)
                            last = None
                            for fc in range(8):
                                last = t.matmul(
                                    PP[:],
                                    w1b[:, fc * HME + hc * 128:
                                         fc * HME + (hc + 1) * 128],
                                    xn_fs[fc][:, g * GS:(g + 1) * GS],
                                    start=(fc == 0), stop=(fc == 7))
                            last.then_inc(PE.h, 1)
                            PE.n += 1

                        @block.vector
                        def _(v, e=e, gl=gl, hc=hc, PP=PP, HT=HT):
                            v.wait_ge(PE.h, PE.n)
                            bwait(v, "HT%d" % (hc % 2))
                            v.tensor_scalar_add(HT[:], PP[:],
                                                b1s[:, e * 16 + hc:
                                                    e * 16 + hc + 1])
                            v.scalar_tensor_tensor(
                                hhat[:, (gl * 16 + hc) * GS:
                                     (gl * 16 + hc + 1) * GS],
                                HT[:], 0.0, gbc[:],
                                ALU.max, ALU.mult).then_inc(DV.h, 1)
                            DV.n += 1
                        bfree(PPn, DV)
                        bfree("HT%d" % (hc % 2), DV)
                pe_h_done[0] = PE.n
                h_done_dv = DV.n

                if idx + 1 < len(eseq):
                    bhn, en = eseq[idx + 1]

                    @block.gpsimd
                    def _(gq, en=en):
                        gq.wait_ge(PE.h, PE.n if SERIAL_B else pe_h_done[0])
                        for fc in range(8):
                            gq.dma_start(out=w1b[:, fc * HME:(fc + 1) * HME],
                                         in_=W1[en, fc * 128:(fc + 1) * 128, :]
                                         ).then_inc(GD1.h, 16)
                            GD1.n += 16

                for gl in range(NG_BH):
                    g = bh * NG_BH + gl
                    for tl in range(NTG):
                        tt = g * NTG + tl
                        ttl = tt - bh * NT_BH
                        par = (gl * NTG + tl) % 2
                        Q0, Q1 = (P_P0, P_P1) if par == 0 else (P_M, P_BC)
                        Q0n, Q1n = ("P_P0", "P_P1") if par == 0 else ("P_M", "P_BC")

                        @block.tensor
                        def _(t, bh=bh, e=e, g=g, gl=gl, tl=tl, tt=tt,
                              Q0=Q0, Q1=Q1, Q0n=Q0n, Q1n=Q1n,
                              h_done_dv=h_done_dv):
                            t.wait_ge(GD2.h, GD2.n)
                            t.wait_ge(DV.h, h_done_dv)
                            bwait(t, Q0n, Q1n)
                            last_is_b2 = (e == 0)
                            last = None
                            for hc in range(16):
                                lhs = hhat[:, (gl * 16 + hc) * GS + tl * 128:
                                           (gl * 16 + hc) * GS + (tl + 1) * 128]
                                stp = (hc == 15) and not last_is_b2
                                t.matmul(Q0[:], lhs,
                                         w2b[:, hc * D:hc * D + 512],
                                         start=(hc == 0), stop=stp)
                                last = t.matmul(Q1[:], lhs,
                                                w2b[:, hc * D + 512:(hc + 1) * D],
                                                start=(hc == 0), stop=stp)
                            if last_is_b2:
                                t.matmul(Q0[:],
                                         gates_rows[0:NE, tt * 128:(tt + 1) * 128],
                                         b2s[:, 0:512], start=False, stop=True)
                                last = t.matmul(
                                    Q1[:],
                                    gates_rows[0:NE, tt * 128:(tt + 1) * 128],
                                    b2s[:, 512:1024], start=False, stop=True)
                            last.then_inc(PE.h, 1)
                            PE.n += 1

                        @block.vector
                        def _(v, e=e, ttl=ttl, Q0=Q0, Q1=Q1):
                            v.wait_ge(PE.h, PE.n)
                            if e == 0:
                                v.tensor_copy(macc[:, ttl * D:ttl * D + 512],
                                              Q0[:])
                                v.tensor_copy(macc[:, ttl * D + 512:(ttl + 1) * D],
                                              Q1[:]).then_inc(DV.h, 1)
                            else:
                                v.tensor_tensor(macc[:, ttl * D:ttl * D + 512],
                                                Q0[:],
                                                macc[:, ttl * D:ttl * D + 512],
                                                ALU.add)
                                v.tensor_tensor(
                                    macc[:, ttl * D + 512:(ttl + 1) * D],
                                    Q1[:],
                                    macc[:, ttl * D + 512:(ttl + 1) * D],
                                    ALU.add).then_inc(DV.h, 1)
                            DV.n += 1
                        bfree(Q0n, DV)
                        bfree(Q1n, DV)
                pe_w2_done[0] = PE.n


                for ttl in range(NT_BH):
                    tt = bh * NT_BH + ttl

                    @block.scalar
                    def _(s2, ttl=ttl, tt=tt):
                        s2.wait_ge(DV.h, DV.n)
                        s2.wait_ge(LD.h, LD.n)
                        s2.activation(silt[:], macc[:, ttl * D:(ttl + 1) * D],
                                      AF.Silu).then_inc(AC.h, 1)
                        AC.n += 1

                    @block.sync
                    def _(q, tt=tt):
                        q.wait_ge(AC.h, AC.n)
                        q.dma_start(out=sc_g[tt * 128:(tt + 1) * 128, :],
                                    in_=silt[:]).then_inc(LD.h, 16)
                        LD.n += 16
                    silt_sp[0] = LD.n
            b_spills_done = LD.n

            barrier()

            close_scope("B")
            close_scope("AB")
            s_start = sbt("s_start", [128, 8 * C], F32, scope="C")
            d_r = sbt("d_r", [128, NE], F32, scope="C")
            m_r = sbt("m_r", [128, 8 * C], F32, scope="C")
            dadj = sbt("dadj", [128, NE], F32, scope="C")
            qtf = sbt("qtf", [128, 8 * TC], BF16, scope="C")
            sloc2 = sbt("sloc2", [128, 8 * nch * C], BF16, scope="C")
            ktf = sbt("ktf", [128, 8 * TC], BF16, scope="C")
            wob = sbt("wob", [128, 8 * D], BF16, scope="C")
            vA = [sbt(f"vA{i}", [128, D], BF16, scope="C") for i in range(NTG)]
            vB = [sbt(f"vB{i}", [128, D], BF16, scope="C") for i in range(NTG)]
            stmp = sbt("stmp", [128, C], F32, scope="C")
            sl32 = sbt("sl32", [128, C], F32, scope="C")
            stb = [sbt(f"stb{i}", [128, C], BF16, scope="C") for i in range(2)]
            at_sb = sbt("at_sb", [128, C], BF16, scope="C")
            o_sb = sbt("o_sb", [128, 8 * GS], BF16, scope="C")
            xn32t = sbt("xn32t", [128, D], F32, scope="C")
            s1t = sbt("s1t", [128, D], F32, scope="C")

            maybe_stop("b", [("g", sc_g), ("gates", sc_gates)])
            @block.sync
            def _(q):
                q.wait_ge(LD.h, LD.n)
                q.wait_ge(CCS.h, CCS.n)
                for oc in range(8):
                    q.dma_start(out=qtf[:, oc * TC:(oc + 1) * TC],
                                in_=sc_qt[oc]).then_inc(LD.h, 16)
                    q.dma_start(out=ktf[:, oc * TC:(oc + 1) * TC],
                                in_=sc_kt[oc]).then_inc(LD.h, 16)
                    LD.n += 32
                q.dma_start(out=sloc2[:], in_=sc_sloc).then_inc(LD.h, 16)
                LD.n += 16

            @block.gpsimd
            def _(gq):
                for p in range(8):
                    gq.dma_start(out=wob[:, p * D:(p + 1) * D],
                                 in_=Wo[p * 128:(p + 1) * 128, :]).then_inc(GD.h, 16)
                    GD.n += 16
            c_gd = GD.n

            @block.vector
            def _(v):
                v.wait_ge(LD.h, LD.n)
                v.memset(s_start[:], 0.0).then_inc(DV.h, 1)
                DV.n += 1

            for r in range(NCORES):
                @block.sync
                def _(q, r=r):
                    q.wait_ge(DV.h, DV.n)
                    q.dma_start(out=d_r[:, 0:8],
                                in_=cc_outD[r * 128:(r + 1) * 128, :]
                                ).then_inc(LD.h, 16)
                    LD.n += 16
                    for p in range(8):
                        q.dma_start(out=m_r[:, p * C:(p + 1) * C],
                                    in_=cc_outM[(r * 8 + p) * 128:
                                                (r * 8 + p + 1) * 128, :]
                                    ).then_inc(LD.h, 16)
                        LD.n += 16

                @block.vector
                def _(v, r=r):
                    v.wait_ge(LD.h, LD.n)
                    v.tensor_scalar(dadj[:], d_r[:], rmk[:, r:r + 1],
                                    rmkb[:, r:r + 1], ALU.mult, ALU.add)
                    v.tensor_scalar_mul(m_r[:], m_r[:], rmk[:, r:r + 1])
                    v.tensor_copy(zeros64[:], zeros64[:])
                    v.tensor_copy(zeros64[:], zeros64[:])
                    for p in range(8):
                        v.tensor_scalar_mul(s_start[:, p * C:(p + 1) * C],
                                            s_start[:, p * C:(p + 1) * C],
                                            dadj[:, p:p + 1])
                        v.tensor_tensor(s_start[:, p * C:(p + 1) * C],
                                        m_r[:, p * C:(p + 1) * C],
                                        s_start[:, p * C:(p + 1) * C], ALU.add)
                    v.tensor_copy(junk1[:], s_start[:, 0:1]).then_inc(DV.h, 1)
                    DV.n += 1

            s1t_sp = [LD.n]
            for g in range(ng):
                c_ld_now = [0]
                @block.sync
                def _(q, g=g):
                    q.wait_ge(PE.h, PE.n)
                    q.wait_ge(LD.h, LD.n)
                    for m in range(NTG):
                        n0 = g * NCHG + 2 * m
                        q.dma_start(out=vA[m][:],
                                    in_=sc_v[n0 * C:(n0 + 2) * C, :]
                                    ).then_inc(LD.h, 16)
                        q.dma_start(out=vB[m][0:64, :],
                                    in_=sc_v[(n0 + 1) * C:(n0 + 2) * C, :]
                                    ).then_inc(LD.h, 16)
                        q.dma_start(out=vB[m][64:128, :],
                                    in_=sc_v[n0 * C:(n0 + 1) * C, :]
                                    ).then_inc(LD.h, 16)
                        LD.n += 48
                c_ld_now[0] = LD.n

                for n in range(NCHG):
                    n_glob = g * NCHG + n
                    m = n // 2
                    bb = (n % 2) * 64
                    t0 = n_glob * C

                    for p in range(8):
                        ip = (n * 8 + p) % 2
                        PO0, PO1 = ((P_P0, P_P1) if ip == 0 else (P_N, P_TRF))
                        PO0n, PO1n = (("P_P0", "P_P1") if ip == 0
                                      else ("P_N", "P_TRF"))
                        STB = stb[ip]

                        @block.vector
                        def _(v, p=p, n_glob=n_glob, STB=STB, ip=ip):
                            bwait(v, "STB%d" % ip)
                            v.scalar_tensor_tensor(
                                STB[:], s_start[:, p * C:(p + 1) * C],
                                gam[:, p * nch + n_glob:
                                    p * nch + n_glob + 1],
                                sloc2[:, (p * nch + n_glob) * C:
                                      (p * nch + n_glob + 1) * C],
                                ALU.mult, ALU.add).then_inc(DV.h, 1)
                            DV.n += 1

                        @block.tensor
                        def _(t, p=p, t0=t0, g=g):
                            t.wait_ge(LD.h, c_ld_now[0])
                            t.wait_ge(GD.h, GD.n)
                            bwait(t, "P_M", "P_BC")
                            t.matmul(P_M[0:64, 0:64],
                                     ktf[0:64, p * TC + t0:p * TC + t0 + C],
                                     qtf[0:64, p * TC + t0:p * TC + t0 + C],
                                     start=True, stop=True, tile_position=(0, 0))
                            t.matmul(P_BC[64:128, 0:64],
                                     ktf[64:128, p * TC + t0:p * TC + t0 + C],
                                     qtf[64:128, p * TC + t0:p * TC + t0 + C],
                                     start=True, stop=True, tile_position=(64, 64)
                                     ).then_inc(PE.h, 1)
                            PE.n += 1

                        @block.vector
                        def _(v, p=p):
                            v.wait_ge(PE.h, PE.n)
                            v.tensor_tensor(at_sb[0:64, :], P_M[0:64, 0:64],
                                            maskA[0:64, :], ALU.mult)
                            v.tensor_tensor(at_sb[64:128, :], P_BC[64:128, 0:64],
                                            maskA[64:128, :], ALU.mult
                                            ).then_inc(DV.h, 1)
                            DV.n += 1
                        bfree("P_M", DV)
                        bfree("P_BC", DV)

                        @block.tensor
                        def _(t, p=p, n=n, m=m, bb=bb, t0=t0, PO0=PO0,
                              PO1=PO1, PO0n=PO0n, PO1n=PO1n, STB=STB):
                            t.wait_ge(DV.h, DV.n)
                            bwait(t, PO0n, PO1n)
                            t.matmul(PO0[0:64, 0:64], STB[0:64, :],
                                     qtf[0:64, p * TC + t0:p * TC + t0 + C],
                                     start=True, stop=False, tile_position=(0, 0))
                            t.matmul(PO1[64:128, 0:64],
                                     STB[64:128, :],
                                     qtf[64:128, p * TC + t0:p * TC + t0 + C],
                                     start=True, stop=False, tile_position=(64, 64))
                            va = vA[m] if n % 2 == 0 else vB[m]
                            vb_ = vB[m] if n % 2 == 0 else vA[m]
                            t.matmul(PO0[0:64, 0:64], va[0:64, p * 128:p * 128 + 64],
                                     at_sb[0:64, :], start=False, stop=True,
                                     tile_position=(0, 0))
                            t.matmul(PO1[64:128, 0:64],
                                     vb_[64:128, p * 128 + 64:(p + 1) * 128],
                                     at_sb[64:128, :], start=False, stop=True,
                                     tile_position=(64, 64)).then_inc(PE.h, 1)
                            PE.n += 1
                        bfree("STB%d" % ip, PE)

                        @block.vector
                        def _(v, p=p, n=n, PO0=PO0, PO1=PO1):
                            v.wait_ge(PE.h, PE.n)
                            v.tensor_copy(
                                o_sb[0:64, p * GS + n * C:p * GS + (n + 1) * C],
                                PO0[0:64, 0:64])
                            v.tensor_copy(
                                o_sb[64:128, p * GS + n * C:p * GS + (n + 1) * C],
                                PO1[64:128, 0:64]).then_inc(DV.h, 1)
                            DV.n += 1
                        bfree(PO0n, DV)
                        bfree(PO1n, DV)

                for tl in range(NTG):
                    tt = g * NTG + tl

                    @block.sync
                    def _(q, tt=tt):
                        q.wait_ge(DV.h, DV.n)
                        q.dma_start(out=xn32t[:],
                                    in_=sc_xn32[tt * 128:(tt + 1) * 128, :]
                                    ).then_inc(LD.h, 16)
                        LD.n += 16
                    xn_ld = LD.n

                    @block.tensor
                    def _(t, tl=tl):
                        t.wait_ge(DV.h, DV.n)
                        t.wait_ge(AC.h, AC.n)
                        last = None
                        for p in range(8):
                            lhs = o_sb[:, p * GS + tl * 128:p * GS + (tl + 1) * 128]
                            t.matmul(P_A[:], lhs, wob[:, p * D:p * D + 512],
                                     start=(p == 0), stop=(p == 7))
                            last = t.matmul(P_B[:], lhs,
                                            wob[:, p * D + 512:(p + 1) * D],
                                            start=(p == 0), stop=(p == 7))
                        last.then_inc(PE.h, 1)
                        PE.n += 1

                    @block.vector
                    def _(v, tt=tt, xn_ld=xn_ld):
                        v.wait_ge(PE.h, PE.n)
                        v.wait_ge(LD.h, LD.n)
                        v.tensor_tensor(s1t[:, 0:512], P_A[:], xn32t[:, 0:512],
                                        ALU.add)
                        v.tensor_tensor(s1t[:, 512:1024], P_B[:],
                                        xn32t[:, 512:1024], ALU.add
                                        ).then_inc(DV.h, 1)
                        DV.n += 1

                    @block.sync
                    def _(q, tt=tt):
                        q.wait_ge(DV.h, DV.n)
                        q.dma_start(out=sc_s1[tt * 128:(tt + 1) * 128, :],
                                    in_=s1t[:]).then_inc(LD.h, 16)
                        LD.n += 16
                    s1t_sp[0] = LD.n
            c_spills_done = LD.n

            barrier()

            close_scope("C")
            s1_sb = sbt("s1_sb", [128, nt_th * D], F32, scope="D")
            h2fs = [sbt(f"h2f{i}", [128, THS], BF16, scope="D", tdest=True)
                 for i in range(8)]
            w1d = sbt("w1d", [128, 8 * 1408], BF16, scope="D")
            w2d = sbt("w2d", [128, 8 * 1408], BF16, scope="D")
            w3d = sbt("w3d", [128, 11 * D], BF16, scope="D")
            zbuf = sbt("zbuf", [128, 11 * GS], BF16, scope="D")
            zsil = [sbt(f"zsil{i}", [128, GS], F32, scope="D") for i in range(2)]
            rstd2c = sbt("rstd2c", [128, 1], F32, scope="D")
            h2bt = sbt("h2bt", [128, D], BF16, scope="D")
            gxt = sbt("gxt", [128, D], F32, scope="D")
            xrt = sbt("xrt", [128, D], F32, scope="D")
            rt = sbt("rt", [128, D], F32, scope="D")

            maybe_stop("c", [("s1", sc_s1)])
            h2sp = [LD.n]
            for th in range(n_th):
                toff = th * THS

                for tl in range(nt_th):
                    tt = toff // 128 + tl

                    @block.sync
                    def _(q, th=th, tl=tl, tt=tt):
                        if th == 0 and tl == 0:
                            q.wait_ge(LD.h, LD.n)
                        q.wait_ge(DV.h, DV.n)
                        q.dma_start(out=s1_sb[:, tl * D:(tl + 1) * D],
                                    in_=sc_s1[tt * 128:(tt + 1) * 128, :]
                                    ).then_inc(LD.h, 16)
                        LD.n += 16
                    s1_ld = LD.n

                    @block.scalar
                    def _(s, tl=tl, s1_ld=s1_ld):
                        s.wait_ge(LD.h, LD.n)
                        s.activation(gxt[:], s1_sb[:, tl * D:(tl + 1) * D],
                                     AF.Square, accum_out=stat[:, 7:8])
                        s.activation(stat[:, 6:7], stat[:, 7:8], AF.Ln,
                                     bias=constv[:, 1:2], scale=constv[:, 3:4])
                        s.activation(rstd2c[:], stat[:, 6:7], AF.Exp,
                                     scale=constv[:, 2:3]).then_inc(AC.h, 1)
                        AC.n += 1

                    @block.vector
                    def _(v, tl=tl):
                        v.wait_ge(AC.h, AC.n)
                        v.wait_ge(LD.h, LD.n)
                        v.scalar_tensor_tensor(h2bt[:], s1_sb[:, tl * D:(tl + 1) * D],
                                               rstd2c[:], gfbc[:],
                                               ALU.mult, ALU.mult).then_inc(DV.h, 1)
                        DV.n += 1

                    @block.sync
                    def _(q, tt=tt):
                        q.wait_ge(DV.h, DV.n)
                        q.dma_start(out=sc_h2[tt * 128:(tt + 1) * 128, :],
                                    in_=h2bt[:]).then_inc(LD.h, 16)
                        LD.n += 16
                    h2sp[0] = LD.n
                h2_spilled = LD.n

                @block.sync
                def _(q, th=th, toff=toff):
                    q.wait_ge(LD.h, LD.n)
                    q.wait_ge(PE.h, PE.n)
                    for fc in range(8):
                        q.dma_start(out=h2fs[fc][:],
                                    in_=sc_h2[toff:toff + THS,
                                              fc * 128:(fc + 1) * 128],
                                    transpose=True).then_inc(LD.h, 16)
                        LD.n += 16
                h2f_ld = LD.n

                for dh in range(2):
                    chunks = DFF_HALF_CHUNKS[dh]
                    w_half = DFF_HALF_W[dh]
                    off = DFF_HALF_OFF[dh]

                    @block.gpsimd
                    def _(gq, dh=dh, off=off, w_half=w_half):
                        gq.wait_ge(PE.h, PE.n)
                        for fc in range(8):
                            gq.dma_start(out=w1d[:, fc * 1408:fc * 1408 + w_half],
                                         in_=w1[fc * 128:(fc + 1) * 128,
                                                off:off + w_half]).then_inc(GD.h, 16)
                            gq.dma_start(out=w2d[:, fc * 1408:fc * 1408 + w_half],
                                         in_=w2[fc * 128:(fc + 1) * 128,
                                                off:off + w_half]).then_inc(GD.h, 16)
                            GD.n += 32
                        poff = 0
                        for ci, cw in enumerate(chunks):
                            gq.dma_start(out=w3d[0:cw, ci * D:(ci + 1) * D],
                                         in_=w3[off + poff:off + poff + cw, :]
                                         ).then_inc(GD.h, 16)
                            GD.n += 16
                            poff += cw
                    d_gd_now = [GD.n]
                    d_ld_now = [LD.n]

                    for g in range(ng_th):
                        poff = 0
                        for ci, cw in enumerate(chunks):
                            par = ci % 2
                            Z0, Z1 = (P_P0, P_P1) if par == 0 else (P_M, P_BC)
                            Z0n, Z1n = (("P_P0", "P_P1") if par == 0
                                        else ("P_M", "P_BC"))
                            ZS = zsil[par]

                            @block.tensor
                            def _(t, g=g, ci=ci, cw=cw, poff=poff,
                                  Z0=Z0, Z1=Z1, Z0n=Z0n, Z1n=Z1n):
                                t.wait_ge(GD.h, d_gd_now[0])
                                t.wait_ge(LD.h, d_ld_now[0])
                                bwait(t, Z0n, Z1n)
                                last = None
                                for fc in range(8):
                                    rhs = h2fs[fc][:, g * GS:(g + 1) * GS]
                                    t.matmul(Z0[0:cw, :],
                                             w1d[:, fc * 1408 + poff:
                                                  fc * 1408 + poff + cw],
                                             rhs, start=(fc == 0), stop=(fc == 7))
                                    last = t.matmul(
                                        Z1[0:cw, :],
                                        w2d[:, fc * 1408 + poff:
                                             fc * 1408 + poff + cw],
                                        rhs, start=(fc == 0), stop=(fc == 7))
                                last.then_inc(PE.h, 1)
                                PE.n += 1

                            @block.scalar
                            def _(s, ci=ci, cw=cw, Z0=Z0, ZS=ZS, par=par):
                                s.wait_ge(PE.h, PE.n)
                                bwait(s, "ZS%d" % par)
                                s.activation(ZS[0:cw, :], Z0[0:cw, :], AF.Silu
                                             ).then_inc(AC.h, 1)
                                AC.n += 1
                            bfree(Z0n, AC)

                            @block.vector
                            def _(v, ci=ci, cw=cw, Z1=Z1, ZS=ZS):
                                v.wait_ge(AC.h, AC.n)
                                v.tensor_tensor(zbuf[0:cw, ci * GS:(ci + 1) * GS],
                                                Z1[0:cw, :], ZS[0:cw, :],
                                                ALU.mult).then_inc(DV.h, 1)
                                DV.n += 1
                            bfree(Z1n, DV)
                            bfree("ZS%d" % par, DV)
                            poff += cw

                        z_done_dv = DV.n
                        for tl in range(NTG):
                            tlg = g * NTG + tl
                            par = tl % 2
                            W0, W1_ = (P_A, P_B) if par == 0 else (P_N, P_TRF)
                            W0n, W1n = (("P_A", "P_B") if par == 0
                                        else ("P_N", "P_TRF"))

                            @block.tensor
                            def _(t, g=g, tl=tl, chunks=chunks,
                                  W0=W0, W1_=W1_, W0n=W0n, W1n=W1n,
                                  z_done_dv=z_done_dv):
                                t.wait_ge(DV.h, z_done_dv)
                                bwait(t, W0n, W1n)
                                nchk = len(chunks)
                                last = None
                                for ci, cw in enumerate(chunks):
                                    lhs = zbuf[0:cw, ci * GS + tl * 128:
                                               ci * GS + (tl + 1) * 128]
                                    t.matmul(W0[:, 0:512], lhs,
                                             w3d[0:cw, ci * D:ci * D + 512],
                                             start=(ci == 0), stop=(ci == nchk - 1))
                                    last = t.matmul(
                                        W1_[:, 0:512], lhs,
                                        w3d[0:cw, ci * D + 512:(ci + 1) * D],
                                        start=(ci == 0), stop=(ci == nchk - 1))
                                last.then_inc(PE.h, 1)
                                PE.n += 1

                            @block.vector
                            def _(v, tlg=tlg, W0=W0, W1_=W1_):
                                v.wait_ge(PE.h, PE.n)
                                v.tensor_tensor(s1_sb[:, tlg * D:tlg * D + 512],
                                                W0[:, 0:512],
                                                s1_sb[:, tlg * D:tlg * D + 512],
                                                ALU.add)
                                v.tensor_tensor(s1_sb[:, tlg * D + 512:(tlg + 1) * D],
                                                W1_[:, 0:512],
                                                s1_sb[:, tlg * D + 512:(tlg + 1) * D],
                                                ALU.add).then_inc(DV.h, 1)
                                DV.n += 1
                            bfree(W0n, DV)
                            bfree(W1n, DV)

                for tl in range(nt_th):
                    tt = toff // 128 + tl

                    @block.sync
                    def _(q, tt=tt, th=th, tl=tl):
                        if th == 0 and tl == 0:
                            q.wait_ge(LD.h, LD.n)
                        q.wait_ge(DV.h, DV.n)
                        q.dma_start(out=gxt[:], in_=sc_g[tt * 128:(tt + 1) * 128, :]
                                    ).then_inc(LD.h, 16)
                        q.dma_start(out=xrt[:], in_=x_in[tt * 128:(tt + 1) * 128, :]
                                    ).then_inc(LD.h, 16)
                        LD.n += 32
                    e_ld = LD.n

                    @block.vector
                    def _(v, tl=tl, e_ld=e_ld):
                        v.wait_ge(LD.h, LD.n)
                        v.wait_ge(OUT.h, OUT.n)
                        v.tensor_mul(rt[:], s1_sb[:, tl * D:(tl + 1) * D], gxt[:])
                        v.tensor_add(rt[:], rt[:], xrt[:]).then_inc(DV.h, 1)
                        DV.n += 1

                    @block.gpsimd
                    def _(gq, tt=tt):
                        gq.wait_ge(DV.h, DV.n)
                        gq.dma_start(out=out_t[tt * 128:(tt + 1) * 128, :],
                                     in_=rt[:]).then_inc(OUT.h, 16)
                        OUT.n += 16

            @block.gpsimd
            def _(gq):
                gq.wait_ge(OUT.h, OUT.n)

            barrier()
            close_scope("D")
        except _StopBuild:
            pass


    return nc


_MASK_CACHE = {}


def _consts(core_id):
    if core_id in _MASK_CACHE:
        return _MASK_CACHE[core_id]
    ones_b = np.ones((1, 128), ml_dtypes.bfloat16)
    ones_f = np.ones((1, 128), np.float32)
    ones2_f = np.zeros((128, 2), np.float32)
    ones2_f[0:64, 0] = 1.0
    ones2_f[64:128, 1] = 1.0
    bcast2_f = np.zeros((2, 128), np.float32)
    bcast2_f[0, 0:64] = 1.0
    bcast2_f[1, 64:128] = 1.0
    p = np.arange(128)[:, None] % 64
    q = np.arange(C)[None, :]
    mask_tri = (q >= p).astype(np.float32)
    id_b = np.eye(128, dtype=ml_dtypes.bfloat16)
    id_f = np.eye(128, dtype=np.float32)
    rmask = np.zeros((128, NCORES), np.float32)
    rmask[:, :core_id] = 1.0
    rmaskbar = 1.0 - rmask
    d = dict(ones_b=ones_b, ones_f=ones_f, ones2_f=ones2_f,
             bcast2_f=bcast2_f, mask_tri=mask_tri, id_b=id_b, id_f=id_f,
             rmask=rmask, rmaskbar=rmaskbar)
    _MASK_CACHE[core_id] = d
    return d


def kernel(**inputs):
    x = np.ascontiguousarray(np.asarray(inputs["L0_attn_x"])[0])
    T = x.shape[0]
    TC = T // NCORES
    w = {k: np.ascontiguousarray(np.asarray(v)) for k, v in inputs.items()}
    b1 = w["L0_ffn_b1"]
    b1r = np.ascontiguousarray(
        b1.reshape(NE, HME // 128, 128).transpose(0, 2, 1))

    base = dict(
        Wq=w["L0_attn_Wq"], Wk=w["L0_attn_Wk"], Wv=w["L0_attn_Wv"],
        Wg=w["L0_attn_Wg"], Wo=w["L0_attn_Wo"],
        w1=w["L0_attn_w1"], w2=w["L0_attn_w2"], w3=w["L0_attn_w3"],
        g_attn=w["L0_attn_g_attn"].reshape(1, D),
        g_ffn=w["L0_attn_g_ffn"].reshape(1, D),
        Wr=w["L0_ffn_Wr"], br=w["L0_ffn_br"].reshape(1, NE),
        W1=w["L0_ffn_W1"], b1r=b1r, W2=w["L0_ffn_W2"], b2=w["L0_ffn_b2"],
    )

    nc = build_nc(TC=TC, stop_after=globals().get("_STOP_AFTER"))
    in_maps = []
    for c in range(NCORES):
        m = dict(base)
        m["x"] = np.ascontiguousarray(x[c * TC:(c + 1) * TC])
        m.update(_consts(c))
        in_maps.append(m)

    res = run_bass_kernel_spmd(nc, in_maps, core_ids=list(range(NCORES)))
    globals()["_LAST"] = res.results
    out = np.concatenate([np.asarray(r["out"]) for r in res.results], axis=0)
    return out[None]


# version 13
# speedup vs baseline: 1.0950x; 1.0950x over previous
import numpy as np
import ml_dtypes
from contextlib import ExitStack

import concourse.bass as bass
import concourse.mybir as mybir
from concourse.bass_utils import run_bass_kernel_spmd

F32 = mybir.dt.float32
BF16 = mybir.dt.bfloat16
AF = mybir.ActivationFunctionType
ALU = mybir.AluOpType
AX = mybir.AxisListType

NCORES = 8
D = 1024
C = 64
DFF = 2736
NE = 8
HME = 2048
GS = 512
EPS_LN = 1e-5
EPS_RMS = 1e-6
EPS_N = 1e-12
DFF_HALF_CHUNKS = [[128] * 11, [128] * 10 + [48]]
DFF_HALF_W = [sum(h) for h in DFF_HALF_CHUNKS]
DFF_HALF_OFF = [0, 1408]


class Sem:
    def __init__(self, stack, nc, name):
        self.h = stack.enter_context(nc.semaphore(name))
        self.n = 0


class _StopBuild(Exception):
    pass


SERIAL_B = False


def build_nc(TC=2048, stop_after=None):
    nt = TC // 128
    ng = TC // GS
    nch = TC // C
    NCHG = GS // C
    NTG = GS // 128
    n_th = 2 if TC >= 2048 else 1
    n_bh = 2 if TC >= 2048 else 1
    NG_BH_C = (TC // GS) // n_bh
    THS = TC // n_th
    nt_th = THS // 128
    ng_th = THS // GS

    nc = bass.Bass("TRN2", target_bir_lowering=False, debug=False)

    def din(name, shape, dt=F32):
        return nc.dram_tensor(name, shape, dt, kind="ExternalInput").ap()

    x_in = din("x", [TC, D])
    Wq = din("Wq", [D, D]); Wk = din("Wk", [D, D])
    Wv = din("Wv", [D, D]); Wg = din("Wg", [D, D])
    Wo = din("Wo", [D, D])
    w1 = din("w1", [D, DFF]); w2 = din("w2", [D, DFF]); w3 = din("w3", [DFF, D])
    g_attn = din("g_attn", [1, D]); g_ffn = din("g_ffn", [1, D])
    Wr = din("Wr", [D, NE]); br = din("br", [1, NE])
    W1 = din("W1", [NE, D, HME]); b1r = din("b1r", [NE, 128, HME // 128])
    W2 = din("W2", [NE, HME, D]); b2 = din("b2", [NE, D])
    ones_b = din("ones_b", [1, 128], BF16)
    ones_f = din("ones_f", [1, 128], F32)
    ones2_f = din("ones2_f", [128, 2], F32)
    bcast2_f = din("bcast2_f", [2, 128], F32)
    mask_tri = din("mask_tri", [128, C], F32)
    id_b = din("id_b", [128, 128], BF16)
    id_f = din("id_f", [128, 128], F32)
    rmask = din("rmask", [128, NCORES], F32)
    rmaskbar = din("rmaskbar", [128, NCORES], F32)

    out_t = nc.dram_tensor("out", [TC, D], F32, kind="ExternalOutput").ap()

    def dsc(name, shape, dt=F32):
        return nc.dram_tensor(name, shape, dt, kind="Internal").ap()

    sc_xn32 = dsc("sc_xn32", [TC, D])
    sc_xnb = dsc("sc_xnb", [TC, D], BF16)
    sc_xnbg = dsc("sc_xnbg", [TC, D], BF16)
    sc_rms = dsc("sc_rms", [1, TC])
    sc_qt = dsc("sc_qt", [8, 128, TC], BF16)
    sc_kt = dsc("sc_kt", [8, 128, TC], BF16)
    sc_v = dsc("sc_v", [nch * C, D], BF16)
    sc_sloc = dsc("sc_sloc", [128, 8 * nch * C], BF16)
    sc_s1 = dsc("sc_s1", [TC, D])
    sc_h2 = dsc("sc_h2", [TC, D], BF16)
    sc_g = dsc("sc_g", [TC, D])
    cc_inM = dsc("cc_inM", [8 * 128, C])
    cc_outM = dsc("cc_outM", [NCORES * 8 * 128, C])
    cc_inD = dsc("cc_inD", [128, 8])
    sc_gates = dsc("sc_gates", [1, NE * TC], BF16)
    cc_outD = dsc("cc_outD", [NCORES * 128, 8])

    with ExitStack() as st:
        scopes = {}

        def sbt(name, shape, dt=F32, scope=None, tdest=False):
            tgt = st if scope is None else scopes.setdefault(scope, ExitStack())
            return tgt.enter_context(
                nc.sbuf_tensor(name, shape, dt, is_transpose_dest=tdest))

        def close_scope(name):
            if name in scopes:
                scopes.pop(name).close()

        def pst(name, shape, dt=F32):
            return st.enter_context(nc.psum_tensor(name, shape, dt))

        onesb = sbt("onesb", [1, 128], BF16)
        onesf = sbt("onesf", [1, 128], F32)
        ones2f = sbt("ones2f", [128, 2], F32)
        bc2f = sbt("bc2f", [2, 128], F32)
        maskA = sbt("maskA", [128, C], F32)
        idb = sbt("idb", [128, 128], BF16)
        idf = sbt("idf", [128, 128], F32)
        rmk = sbt("rmk", [128, NCORES], F32)
        rmkb = sbt("rmkb", [128, NCORES], F32)
        garow = sbt("garow", [1, D], F32)
        gfrow = sbt("gfrow", [1, D], F32)
        gfbc = sbt("gfbc", [128, D], F32)
        brf = sbt("brf", [1, NE], F32)
        b2s = sbt("b2s", [NE, D], F32)
        rmsr = sbt("rmsr", [128, nt], F32)
        zeros64 = sbt("zeros64", [128, C], F32)
        constv = sbt("constv", [128, 4], F32)
        junk1 = sbt("junk1", [128, 1], F32)
        gam = sbt("gam", [128, 8 * nch], F32)

        stat = sbt("stat", [128, 8])
        rstd_c = sbt("rstd_c", [128, 1])





        P_A = pst("P_A", [128, 512])
        P_B = pst("P_B", [128, 512])
        P_P0 = pst("P_P0", [128, 512])
        P_P1 = pst("P_P1", [128, 512])
        P_N = pst("P_N", [128, 512])
        P_BC = pst("P_BC", [128, 512])
        P_TRB = pst("P_TRB", [128, 1024], BF16)
        P_TRF = P_TRB.ap().bitcast(F32)
        P_M = pst("P_M", [128, 512])

        LD = Sem(st, nc, "LD")
        GD = Sem(st, nc, "GD")
        GD1 = Sem(st, nc, "GD1")
        GD2 = Sem(st, nc, "GD2")
        PE = Sem(st, nc, "PE")
        DV = Sem(st, nc, "DV")
        AC = Sem(st, nc, "AC")
        CCS = Sem(st, nc, "CCS")
        OUT = Sem(st, nc, "OUT")
        BAR = Sem(st, nc, "BAR")

        bankfree = {}

        def bfree(name, sem):
            bankfree[name] = (sem.h, sem.n)

        def bwait(eng, *names):
            for nm in names:
                if nm in bankfree:
                    h, n = bankfree[nm]
                    eng.wait_ge(h, n)

        block = st.enter_context(nc.Block())
        nbar = [0]

        def maybe_stop(tag, dumps=()):
            if stop_after == tag:
                @block.gpsimd
                def _(gq):
                    gq.wait_ge(CCS.h, CCS.n)
                    gq.wait_ge(LD.h, LD.n)
                    gq.wait_ge(GD.h, GD.n)
                    for nm, src in dumps:
                        dt_ = src.dtype if hasattr(src, "dtype") else F32
                        dd = nc.dram_tensor("dbg_" + nm, list(src.shape), dt_,
                                            kind="ExternalOutput").ap()
                        gq.dma_start(out=dd, in_=src).then_inc(OUT.h, 16)
                        OUT.n += 16
                    gq.wait_ge(OUT.h, OUT.n)
                barrier()
                for k in list(scopes.keys())[::-1]:
                    scopes.pop(k).close()
                raise _StopBuild()

        def barrier():
            nbar[0] += 1
            tgt = 5 * nbar[0]
            for deco in (block.sync, block.gpsimd, block.tensor,
                         block.vector, block.scalar):
                @deco
                def _(q):
                    q.sem_inc(BAR.h, 1)
                    q.wait_ge(BAR.h, tgt)

        try:
            @block.sync
            def _(q):
                for dst, src in ((onesb, ones_b), (onesf, ones_f),
                                 (ones2f, ones2_f), (bc2f, bcast2_f),
                                 (maskA, mask_tri), (idb, id_b), (idf, id_f),
                                 (rmk, rmask), (rmkb, rmaskbar),
                                 (garow, g_attn), (gfrow, g_ffn), (brf, br),
                                 (b2s, b2)):
                    q.dma_start(out=dst[:], in_=src).then_inc(LD.h, 16)
                    LD.n += 16

            @block.vector
            def _(v):
                v.wait_ge(LD.h, LD.n)
                v.memset(zeros64[:], 0.0)
                v.memset(constv[:, 0:1], EPS_LN)
                v.memset(constv[:, 1:2], EPS_RMS)
                v.memset(constv[:, 2:3], -0.5)
                v.memset(constv[:, 3:4], 1.0 / D).then_inc(DV.h, 1)
                DV.n += 1

            barrier()

            gabc = sbt("gabc", [128, D], F32, scope="A1")
            xt = [sbt(f"xt{i}", [128, D], scope="A1") for i in range(2)]
            xnt = sbt("xnt", [128, D], scope="A1")
            xnbt = sbt("xnbt", [128, D], BF16, scope="A1")
            xnbgt = sbt("xnbgt", [128, D], BF16, scope="A1")
            sq = sbt("sq", [128, D], scope="A1")

            maybe_stop("p0")
            @block.tensor
            def _(t):
                t.wait_ge(LD.h, LD.n)
                t.matmul(P_A[:], onesf[:], garow[0:1, 0:512], start=True, stop=True)
                t.matmul(P_B[:], onesf[:], garow[0:1, 512:1024], start=True, stop=True)
                t.matmul(P_P0[:], onesf[:], gfrow[0:1, 0:512], start=True, stop=True)
                t.matmul(P_P1[:], onesf[:], gfrow[0:1, 512:1024],
                         start=True, stop=True).then_inc(PE.h, 1)
                PE.n += 1

            @block.vector
            def _(v):
                v.wait_ge(PE.h, PE.n)
                v.tensor_copy(gabc[:, 0:512], P_A[:])
                v.tensor_copy(gabc[:, 512:1024], P_B[:])
                v.tensor_copy(gfbc[:, 0:512], P_P0[:])
                v.tensor_copy(gfbc[:, 512:1024], P_P1[:]).then_inc(DV.h, 1)
                DV.n += 1

            x_ld = {}

            @block.sync
            def _(q):
                for t in range(min(2, nt)):
                    q.dma_start(out=xt[t % 2][:],
                                in_=x_in[t * 128:(t + 1) * 128, :]).then_inc(LD.h, 16)
                    LD.n += 16
                    x_ld[t] = LD.n

            dv_done = {}
            ac_done = {}
            spill_done = {}
            for t in range(nt):
                xb = xt[t % 2]

                @block.vector
                def _(v, t=t, xb=xb):
                    v.wait_ge(LD.h, LD.n)
                    v.reduce_sum(stat[:, 0:1], xb[:], axis=AX.X)
                    v.tensor_copy(zeros64[:], zeros64[:])
                    v.tensor_scalar_mul(stat[:, 1:2], stat[:, 0:1], 1.0 / D)
                    v.tensor_copy(zeros64[:], zeros64[:])
                    v.tensor_mul(stat[:, 2:3], stat[:, 1:2], stat[:, 1:2]
                                 ).then_inc(DV.h, 1)
                    DV.n += 1

                @block.scalar
                def _(s, t=t, xb=xb):
                    s.wait_ge(LD.h, LD.n)
                    s.activation(sq[:], xb[:], AF.Square,
                                 accum_out=stat[:, 3:4]).then_inc(AC.h, 1)
                    AC.n += 1

                @block.vector
                def _(v, t=t):
                    v.wait_ge(AC.h, AC.n)
                    v.tensor_scalar_mul(stat[:, 4:5], stat[:, 3:4], 1.0 / D)
                    v.tensor_copy(zeros64[:], zeros64[:])
                    v.tensor_sub(stat[:, 5:6], stat[:, 4:5], stat[:, 2:3]
                                 ).then_inc(DV.h, 1)
                    DV.n += 1

                @block.scalar
                def _(s, t=t):
                    s.wait_ge(DV.h, DV.n)
                    s.activation(stat[:, 6:7], stat[:, 5:6], AF.Ln, bias=constv[:, 0:1])
                    s.activation(rstd_c[:], stat[:, 6:7], AF.Exp,
                                 scale=constv[:, 2:3]).then_inc(AC.h, 1)
                    AC.n += 1

                @block.vector
                def _(v, t=t, xb=xb):
                    v.wait_ge(AC.h, AC.n)
                    if t >= 1:
                        v.wait_ge(LD.h, LD.n)
                    v.tensor_scalar(xnt[:], xb[:], stat[:, 1:2], rstd_c[:],
                                    ALU.subtract, ALU.mult)
                    v.tensor_tensor(xnbgt[:], xnt[:], gabc[:], ALU.mult
                                    ).then_inc(DV.h, 1)
                    DV.n += 1
                    dv_done[t] = DV.n

                @block.scalar
                def _(s, t=t):
                    s.wait_ge(DV.h, dv_done[t])
                    if t >= 1:
                        s.wait_ge(LD.h, LD.n)
                    s.copy(xnbt[:], xnt[:])
                    s.activation(sq[:], xnt[:], AF.Square, accum_out=stat[:, 7:8])
                    s.activation(stat[:, 6:7], stat[:, 7:8], AF.Ln,
                                 bias=constv[:, 1:2], scale=constv[:, 3:4])
                    s.activation(rmsr[:, t:t + 1], stat[:, 6:7], AF.Exp,
                                 scale=constv[:, 2:3]).then_inc(AC.h, 1)
                    AC.n += 1
                    ac_done[t] = AC.n

                @block.sync
                def _(q, t=t):
                    q.wait_ge(DV.h, dv_done[t])
                    q.dma_start(out=sc_xn32[t * 128:(t + 1) * 128, :],
                                in_=xnt[:]).then_inc(LD.h, 16)
                    q.dma_start(out=sc_xnbg[t * 128:(t + 1) * 128, :],
                                in_=xnbgt[:]).then_inc(LD.h, 16)
                    LD.n += 32
                    q.wait_ge(AC.h, ac_done[t])
                    q.dma_start(out=sc_xnb[t * 128:(t + 1) * 128, :],
                                in_=xnbt[:]).then_inc(LD.h, 16)
                    q.dma_start(out=sc_rms[0:1, t * 128:(t + 1) * 128]
                                .rearrange("a p -> p a"),
                                in_=rmsr[:, t:t + 1]).then_inc(LD.h, 16)
                    LD.n += 32
                    if t + 2 < nt:
                        q.dma_start(out=xt[t % 2][:],
                                    in_=x_in[(t + 2) * 128:(t + 3) * 128, :]
                                    ).then_inc(LD.h, 16)
                        LD.n += 16
                        x_ld[t + 2] = LD.n

                spill_done[t] = LD.n

            a1_spills_done = LD.n
            barrier()

            close_scope("A1")
            xn_fs = [sbt(f"xn_f{i}", [128, TC], BF16, scope="AB", tdest=True)
                 for i in range(8)]
            l_run = sbt("l_run", [128, 8 * C], F32, scope="A2")
            gam_run = sbt("gam_run", [128, 8], F32, scope="A2")
            s_local = sbt("s_local", [128, 8 * nch * C], BF16, scope="A2")
            wqb = sbt("wqb", [128, 8 * D], BF16, scope="A2")
            wkb = sbt("wkb", [128, 8 * D], BF16, scope="A2")
            wvb = sbt("wvb", [128, 8 * D], BF16, scope="A2")
            wgb = sbt("wgb", [128, 8 * D], BF16, scope="A2")
            rstd_row = sbt("rstd_row", [1, TC], F32, scope="A2")
            rstdb = sbt("rstdb", [128, GS], F32, scope="A2")
            a_oc = sbt("a_oc", [128, GS], F32, scope="A2")
            ra_oc = sbt("ra_oc", [128, GS], F32, scope="A2")
            az = sbt("az", [128, GS], F32, scope="A2")
            b64 = [sbt(f"b64_{i}", [128, C], scope="A2") for i in range(NCHG)]
            rb64 = [sbt(f"rb64_{i}", [128, C], scope="A2") for i in range(NCHG)]
            sqf = sbt("sqf", [128, GS], F32, scope="A2")
            nn2 = sbt("nn2", [2, GS], F32, scope="A2")
            invn2 = sbt("invn2", [2, GS], F32, scope="A2")
            invb = sbt("invb", [128, GS], F32, scope="A2")
            qn_oc = sbt("qn_oc", [128, GS], F32, scope="A2")
            qtb_oc = sbt("qtb_oc", [128, GS], BF16, scope="A2")
            kt32 = sbt("kt32", [128, C], F32, scope="A2")
            khb = sbt("khb", [128, C], BF16, scope="A2")
            khtok = sbt("khtok", [128, 128], BF16, scope="A2")
            vtile = [sbt(f"vtile{i}", [128, D], BF16, scope="A2") for i in range(NTG)]

            maybe_stop("a1", [("xn32", sc_xn32), ("xnb", sc_xnb), ("xnbg", sc_xnbg), ("rms", sc_rms)])
            @block.sync
            def _(q):
                q.wait_ge(LD.h, LD.n)
                for fc in range(8):
                    q.dma_start(out=xn_fs[fc][:],
                                in_=sc_xnbg[:, fc * 128:(fc + 1) * 128],
                                transpose=True).then_inc(LD.h, 16)
                    LD.n += 16
                q.dma_start(out=rstd_row[:], in_=sc_rms).then_inc(LD.h, 16)
                LD.n += 16

            @block.gpsimd
            def _(g):
                for wdst, wsrc in ((wqb, Wq), (wkb, Wk), (wvb, Wv), (wgb, Wg)):
                    for fc in range(8):
                        g.dma_start(out=wdst[:, fc * D:(fc + 1) * D],
                                    in_=wsrc[fc * 128:(fc + 1) * 128, :]
                                    ).then_inc(GD.h, 16)
                        GD.n += 16

            a2_ld = LD.n
            a2_gd = GD.n

            @block.vector
            def _(v):
                v.memset(l_run[:], 0.0)
                v.memset(gam_run[:], 1.0).then_inc(DV.h, 1)
                DV.n += 1
            maybe_stop("a2m")

            @block.sync
            def _(q):
                q.wait_ge(LD.h, LD.n)
            maybe_stop("a2t")

            @block.gpsimd
            def _(gq):
                gq.wait_ge(GD.h, GD.n)
            maybe_stop("a2w")

            def a2_proj(g, oc, PP, wsb, PPn):
                @block.tensor
                def _(t):
                    bwait(t, PPn)
                    last = None
                    for fc in range(8):
                        last = t.matmul(
                            PP[:],
                            wsb[:, fc * D + oc * 128: fc * D + (oc + 1) * 128],
                            xn_fs[fc][:, g * GS:(g + 1) * GS],
                            start=(fc == 0), stop=(fc == 7))
                    last.then_inc(PE.h, 1)
                    PE.n += 1

            def a2_l2norm(PP, PPn):
                @block.scalar
                def _(s):
                    s.wait_ge(PE.h, PE.n)
                    s.activation(sqf[:], PP[:], AF.Square).then_inc(AC.h, 1)
                    AC.n += 1

                @block.tensor
                def _(t):
                    t.wait_ge(AC.h, AC.n)
                    t.matmul(P_N[0:2, :], ones2f[:], sqf[:], start=True, stop=True
                             ).then_inc(PE.h, 1)
                    PE.n += 1

                @block.scalar
                def _(s):
                    s.wait_ge(PE.h, PE.n)
                    s.activation(nn2[:], P_N[0:2, :], AF.Sqrt).then_inc(AC.h, 1)
                    AC.n += 1

                @block.vector
                def _(v):
                    v.wait_ge(AC.h, AC.n)
                    v.tensor_scalar_max(nn2[:], nn2[:], EPS_N)
                    v.reciprocal(invn2[:], nn2[:]).then_inc(DV.h, 1)
                    DV.n += 1

                @block.tensor
                def _(t):
                    t.wait_ge(DV.h, DV.n)
                    t.matmul(P_BC[:], bc2f[:], invn2[:], start=True, stop=True
                             ).then_inc(PE.h, 1)
                    PE.n += 1

                @block.vector
                def _(v):
                    v.wait_ge(PE.h, PE.n)
                    v.tensor_copy(invb[:], P_BC[:])
                    v.tensor_tensor(qn_oc[:], PP[:], invb[:], ALU.mult
                                    ).then_inc(DV.h, 1)
                    DV.n += 1
                bfree(PPn, DV)

            vspill = {}
            kspill_ld = [LD.n]
            a2_ld_now = [LD.n]
            a2_gd_now = [GD.n]
            for g in range(ng):
                @block.tensor
                def _(t, g=g):
                    t.wait_ge(LD.h, a2_ld_now[0])
                    t.wait_ge(GD.h, a2_gd_now[0])
                    t.wait_ge(DV.h, DV.n)
                    t.matmul(P_BC[:], onesf[:], rstd_row[0:1, g * GS:(g + 1) * GS],
                             start=True, stop=True).then_inc(PE.h, 1)
                    PE.n += 1

                @block.vector
                def _(v, g=g):
                    v.wait_ge(PE.h, PE.n)
                    v.tensor_copy(rstdb[:], P_BC[:]).then_inc(DV.h, 1)
                    DV.n += 1
                if g == 0:
                    maybe_stop("a2r")

                for tl in range(NTG):
                    tt = g * NTG + tl

                    @block.tensor
                    def _(t, tt=tt):
                        bwait(t, "P_A", "P_B")
                        last = None
                        for fc in range(8):
                            lhs = xn_fs[fc][:, tt * 128:(tt + 1) * 128]
                            t.matmul(P_A[:], lhs, wvb[:, fc * D:fc * D + 512],
                                     start=(fc == 0), stop=(fc == 7))
                            last = t.matmul(P_B[:], lhs,
                                            wvb[:, fc * D + 512:(fc + 1) * D],
                                            start=(fc == 0), stop=(fc == 7))
                        last.then_inc(PE.h, 1)
                        PE.n += 1

                    if g == 0 and tl == 0:
                        maybe_stop("a2vm")

                    @block.vector
                    def _(v, tl=tl, tt=tt, g=g):
                        v.wait_ge(PE.h, PE.n)
                        if g > 0:
                            v.wait_ge(LD.h, LD.n)
                        v.tensor_scalar_mul(vtile[tl][:, 0:512], P_A[:],
                                            rmsr[:, tt:tt + 1])
                        v.tensor_scalar_mul(vtile[tl][:, 512:1024], P_B[:],
                                            rmsr[:, tt:tt + 1]).then_inc(DV.h, 1)
                        DV.n += 1
                    bfree("P_A", DV)
                    bfree("P_B", DV)

                    if g == 0 and tl == 0:
                        maybe_stop("a2ve")

                    @block.sync
                    def _(q, tl=tl, tt=tt):
                        q.wait_ge(DV.h, DV.n)
                        for half in range(2):
                            q.dma_start(out=sc_v[(tt * 2 + half) * C:
                                                 (tt * 2 + half + 1) * C, :],
                                        in_=vtile[tl][half * 64:(half + 1) * 64, :]
                                        ).then_inc(LD.h, 16)
                            LD.n += 16
                    vspill[tl] = LD.n
                if g == 0:
                    maybe_stop("a2v")

                for oc in range(8):
                    PP = P_P0 if oc % 2 == 0 else P_P1
                    PPn = 'P_P0' if oc % 2 == 0 else 'P_P1'

                    a2_proj(g, oc, PP, wgb, PPn)

                    @block.vector
                    def _(v, oc=oc, PP=PP):
                        v.wait_ge(PE.h, PE.n)
                        v.tensor_tensor(az[:], PP[:], rstdb[:], ALU.mult
                                        ).then_inc(DV.h, 1)
                        DV.n += 1
                    bfree(PPn, DV)

                    @block.scalar
                    def _(s, oc=oc):
                        s.wait_ge(DV.h, DV.n)
                        s.activation(a_oc[:], az[:], AF.Sigmoid).then_inc(AC.h, 1)
                        AC.n += 1

                    @block.vector
                    def _(v, oc=oc):
                        v.wait_ge(AC.h, AC.n)
                        v.reciprocal(ra_oc[:], a_oc[:])
                        last = None
                        for c in range(NCHG):
                            v.tensor_tensor_scan(b64[c][:], a_oc[:, c * C:(c + 1) * C],
                                                 zeros64[:], 1.0, ALU.mult, ALU.add)
                            last = v.tensor_tensor_scan(
                                rb64[c][:], ra_oc[:, c * C:(c + 1) * C],
                                zeros64[:], 1.0, ALU.mult, ALU.add)
                        last.then_inc(DV.h, 1)
                        DV.n += 1

                    if g == 0 and oc == 0:
                        maybe_stop("a2zg")
                    a2_proj(g, oc, PP, wqb, PPn)
                    a2_l2norm(PP, PPn)

                    @block.vector
                    def _(v, g=g, oc=oc):
                        v.wait_ge(LD.h, LD.n)
                        last = None
                        for c in range(NCHG):
                            last = v.tensor_tensor(qtb_oc[:, c * C:(c + 1) * C],
                                                   qn_oc[:, c * C:(c + 1) * C],
                                                   b64[c][:], ALU.mult)
                        last.then_inc(DV.h, 1)
                        DV.n += 1

                    @block.sync
                    def _(q, g=g, oc=oc):
                        q.wait_ge(DV.h, DV.n)
                        q.dma_start(out=sc_qt[oc, :, g * GS:(g + 1) * GS],
                                    in_=qtb_oc[:]).then_inc(LD.h, 16)
                        LD.n += 16
                    q_spill_ld = LD.n

                    if g == 0 and oc == 0:
                        maybe_stop("a2q")
                    a2_proj(g, oc, PP, wkb, PPn)
                    a2_l2norm(PP, PPn)

                    for c in range(NCHG):
                        n_glob = g * NCHG + c
                        bb = (c % 2) * 64

                        @block.vector
                        def _(v, g=g, oc=oc, c=c):
                            if c == 0:
                                v.wait_ge(LD.h, LD.n)
                            v.tensor_tensor(kt32[:], qn_oc[:, c * C:(c + 1) * C],
                                            rb64[c][:], ALU.mult)
                            v.tensor_copy(qtb_oc[:, c * C:(c + 1) * C], kt32[:])
                            v.tensor_scalar_mul(khb[:], kt32[:], b64[c][:, C - 1:C]
                                                ).then_inc(DV.h, 1)
                            DV.n += 1

                        @block.tensor
                        def _(t, c=c):
                            t.wait_ge(DV.h, DV.n)
                            t.transpose(P_TRB[0:64, 0:128], khb[:], idb[:]).then_inc(PE.h, 1)
                            PE.n += 1

                        @block.vector
                        def _(v, c=c, bb=bb):
                            v.wait_ge(PE.h, PE.n)
                            v.tensor_copy(khtok[bb:bb + 64, :], P_TRB[0:64, 0:128]
                                          ).then_inc(DV.h, 1)
                            DV.n += 1

                        @block.tensor
                        def _(t, c=c, bb=bb, oc=oc):
                            t.wait_ge(DV.h, DV.n)
                            vt = vtile[c // 2]
                            t.matmul(P_M[0:64, 0:64], khtok[bb:bb + 64, 0:64],
                                     vt[bb:bb + 64, oc * 128:oc * 128 + 64],
                                     start=True, stop=True, tile_position=(bb, 0))
                            t.matmul(P_BC[64:128, 0:64], khtok[bb:bb + 64, 64:128],
                                     vt[bb:bb + 64, oc * 128 + 64:(oc + 1) * 128],
                                     start=True, stop=True, tile_position=(bb, 64)
                                     ).then_inc(PE.h, 1)
                            PE.n += 1

                        @block.vector
                        def _(v, oc=oc, c=c, n_glob=n_glob):
                            v.wait_ge(PE.h, PE.n)
                            v.tensor_copy(
                                s_local[:, (oc * nch + n_glob) * C:
                                        (oc * nch + n_glob + 1) * C],
                                l_run[:, oc * C:(oc + 1) * C])
                            v.tensor_scalar_mul(l_run[:, oc * C:(oc + 1) * C],
                                                l_run[:, oc * C:(oc + 1) * C],
                                                b64[c][:, C - 1:C])
                            v.tensor_copy(gam[:, oc * nch + n_glob:
                                              oc * nch + n_glob + 1],
                                          gam_run[:, oc:oc + 1])
                            v.tensor_tensor(l_run[0:64, oc * C:(oc + 1) * C],
                                            P_M[0:64, 0:64],
                                            l_run[0:64, oc * C:(oc + 1) * C],
                                            ALU.add)
                            v.tensor_tensor(l_run[64:128, oc * C:(oc + 1) * C],
                                            P_BC[64:128, 0:64],
                                            l_run[64:128, oc * C:(oc + 1) * C],
                                            ALU.add)
                            v.tensor_scalar_mul(gam_run[:, oc:oc + 1],
                                                gam_run[:, oc:oc + 1],
                                                b64[c][:, C - 1:C]).then_inc(DV.h, 1)
                            DV.n += 1

                    @block.sync
                    def _(q, g=g, oc=oc):
                        q.wait_ge(DV.h, DV.n)
                        q.dma_start(out=sc_kt[oc, :, g * GS:(g + 1) * GS],
                                    in_=qtb_oc[:]).then_inc(LD.h, 16)
                        LD.n += 16
                    kspill_ld[0] = LD.n
                    if g == 0 and oc == 0:
                        maybe_stop("a2k")

            @block.sync
            def _(q):
                q.wait_ge(DV.h, DV.n)
                q.dma_start(out=sc_sloc, in_=s_local[:]).then_inc(LD.h, 16)
                LD.n += 16
            a2_spills_done = LD.n

            barrier()

            maybe_stop("a2", [("qt", sc_qt), ("kt", sc_kt), ("v", sc_v), ("sloc", sc_sloc), ("lrun", l_run[:]), ("gam", gam[:]), ("gamrun", gam_run[:])])
            @block.gpsimd
            def _(gq):
                for p in range(8):
                    gq.dma_start(out=cc_inM[p * 128:(p + 1) * 128, :],
                                 in_=l_run[:, p * C:(p + 1) * C]).then_inc(GD.h, 16)
                    GD.n += 16
                gq.dma_start(out=cc_inD, in_=gam_run[:]).then_inc(GD.h, 16)
                GD.n += 16
                gq.wait_ge(GD.h, GD.n)
                gq.collective_compute(
                    "AllGather", ALU.bypass,
                    replica_groups=[list(range(NCORES))],
                    ins=[cc_inM.opt()], outs=[cc_outM.opt()],
                ).then_inc(CCS.h, 1)
                gq.collective_compute(
                    "AllGather", ALU.bypass,
                    replica_groups=[list(range(NCORES))],
                    ins=[cc_inD.opt()], outs=[cc_outD.opt()],
                ).then_inc(CCS.h, 1)
                CCS.n += 2

            close_scope("A2")
            gates_rows = sbt("gates_rows", [NE, TC], F32, scope="B")
            gates_rows_b = sbt("gates_rows_b", [NE, TC], BF16, scope="B")
            gflbuf = sbt("gflbuf", [1, GS], BF16, scope="B")
            wrb = sbt("wrb", [128, 8 * NE], BF16, scope="B")
            lg = sbt("lg", [128, 2 * NE], scope="B")
            srt = sbt("srt", [128, 2 * NE], scope="B")
            sel = sbt("sel", [128, 2 * NE], scope="B")
            ega = sbt("ega", [128, 2 * NE], scope="B")
            den2 = sbt("den2", [128, 2], scope="B")
            di2 = sbt("di2", [128, 2], scope="B")
            gat2 = sbt("gat2", [128, 2 * NE], scope="B")
            w1b = sbt("w1b", [128, 8 * HME], BF16, scope="B")
            w2b = sbt("w2b", [128, 16 * D], BF16, scope="B")
            b1s = sbt("b1s", [128, NE * (HME // 128)], F32, scope="B")
            gbc = sbt("gbc", [128, GS], BF16, scope="B")
            htmp = [sbt(f"htmp{i}", [128, GS], F32, scope="B") for i in range(2)]
            hhat = sbt("hhat", [128, NG_BH_C * 16 * GS], BF16, scope="B")
            macc = sbt("macc", [128, (nt // n_bh) * D], F32, scope="B")
            silt = [sbt(f"silt{i}", [128, D], F32, scope="B") for i in range(2)]

            maybe_stop("x", [("ccM", cc_outM), ("ccD", cc_outD)])
            @block.sync
            def _(q):
                for fc in range(8):
                    q.dma_start(out=xn_fs[fc][:],
                                in_=sc_xnb[:, fc * 128:(fc + 1) * 128],
                                transpose=True).then_inc(LD.h, 16)
                    LD.n += 16

            @block.gpsimd
            def _(gq):
                for fc in range(8):
                    gq.dma_start(out=wrb[:, fc * NE:(fc + 1) * NE],
                                 in_=Wr[fc * 128:(fc + 1) * 128, :]).then_inc(GD.h, 16)
                    GD.n += 16
                for e in range(NE):
                    gq.dma_start(out=b1s[:, e * 16:(e + 1) * 16],
                                 in_=b1r[e]).then_inc(GD.h, 16)
                    GD.n += 16

            b_ld = LD.n
            b_gd = GD.n

            for tp in range(0, nt, 2):
                @block.tensor
                def _(t, tp=tp):
                    t.wait_ge(LD.h, LD.n)
                    t.wait_ge(GD.h, GD.n)
                    t.wait_ge(DV.h, DV.n)
                    last = None
                    for j in range(2):
                        tt = tp + j
                        PP = P_P0 if j == 0 else P_P1
                        for fc in range(8):
                            t.matmul(PP[:, 0:NE],
                                     xn_fs[fc][:, tt * 128:(tt + 1) * 128],
                                     wrb[:, fc * NE:(fc + 1) * NE],
                                     start=(fc == 0), stop=False)
                        last = t.matmul(PP[:, 0:NE], onesf[:], brf[:],
                                        start=False, stop=True)
                    last.then_inc(PE.h, 1)
                    PE.n += 1

                @block.vector
                def _(v, tp=tp):
                    v.wait_ge(PE.h, PE.n)
                    v.tensor_copy(lg[:, 0:NE], P_P0[:, 0:NE])
                    v.tensor_copy(lg[:, NE:2 * NE], P_P1[:, 0:NE])
                    v.tensor_copy(zeros64[:], zeros64[:])
                    v.tensor_copy(zeros64[:], zeros64[:])
                    v.max(srt[:, 0:NE], lg[:, 0:NE])
                    v.max(srt[:, NE:2 * NE], lg[:, NE:2 * NE])
                    v.tensor_copy(zeros64[:], zeros64[:])
                    v.tensor_copy(zeros64[:], zeros64[:])
                    v.tensor_scalar(sel[:, 0:NE], lg[:, 0:NE], srt[:, 1:2], 0.0,
                                    ALU.is_ge, ALU.add)
                    v.tensor_scalar(sel[:, NE:2 * NE], lg[:, NE:2 * NE],
                                    srt[:, NE + 1:NE + 2], 0.0, ALU.is_ge, ALU.add)
                    v.tensor_scalar(ega[:, 0:NE], lg[:, 0:NE], srt[:, 0:1], 0.0,
                                    ALU.subtract, ALU.add)
                    v.tensor_scalar(ega[:, NE:2 * NE], lg[:, NE:2 * NE],
                                    srt[:, NE:NE + 1], 0.0, ALU.subtract, ALU.add
                                    ).then_inc(DV.h, 1)
                    DV.n += 1

                @block.scalar
                def _(s, tp=tp):
                    s.wait_ge(DV.h, DV.n)
                    s.activation(ega[:, 0:2 * NE], ega[:, 0:2 * NE], AF.Exp
                                 ).then_inc(AC.h, 1)
                    AC.n += 1

                @block.vector
                def _(v, tp=tp):
                    v.wait_ge(AC.h, AC.n)
                    v.tensor_mul(ega[:, 0:2 * NE], ega[:, 0:2 * NE],
                                 sel[:, 0:2 * NE])
                    v.tensor_copy(zeros64[:], zeros64[:])
                    v.reduce_sum(den2[:, 0:1], ega[:, 0:NE], axis=AX.X)
                    v.reduce_sum(den2[:, 1:2], ega[:, NE:2 * NE], axis=AX.X)
                    v.tensor_copy(zeros64[:], zeros64[:])
                    v.tensor_copy(zeros64[:], zeros64[:])
                    v.reciprocal(di2[:], den2[:])
                    v.tensor_copy(zeros64[:], zeros64[:])
                    v.tensor_copy(zeros64[:], zeros64[:])
                    v.tensor_copy(junk1[:], di2[:, 0:1])
                    v.tensor_scalar_mul(gat2[:, 0:NE], ega[:, 0:NE], di2[:, 0:1])
                    v.tensor_scalar_mul(gat2[:, NE:2 * NE], ega[:, NE:2 * NE],
                                        di2[:, 1:2]).then_inc(DV.h, 1)
                    DV.n += 1

                @block.tensor
                def _(t, tp=tp):
                    t.wait_ge(DV.h, DV.n)
                    t.transpose(P_BC[0:NE, 0:128], gat2[:, 0:NE], idf[:])
                    t.transpose(P_A[0:NE, 0:128], gat2[:, NE:2 * NE], idf[:]
                                ).then_inc(PE.h, 1)
                    PE.n += 1

                @block.vector
                def _(v, tp=tp):
                    v.wait_ge(PE.h, PE.n)
                    v.tensor_copy(gates_rows[:, tp * 128:(tp + 1) * 128],
                                  P_BC[0:NE, 0:128])
                    v.tensor_copy(gates_rows[:, (tp + 1) * 128:(tp + 2) * 128],
                                  P_A[0:NE, 0:128])
                    v.tensor_copy(gates_rows_b[:, tp * 128:(tp + 1) * 128],
                                  P_BC[0:NE, 0:128])
                    v.tensor_copy(gates_rows_b[:, (tp + 1) * 128:(tp + 2) * 128],
                                  P_A[0:NE, 0:128]).then_inc(DV.h, 1)
                    DV.n += 1

            @block.sync
            def _(q):
                q.wait_ge(DV.h, DV.n)
                q.dma_start(out=sc_gates.rearrange("a (e t) -> (a e) t", e=NE),
                            in_=gates_rows_b[:]).then_inc(LD.h, 16)
                LD.n += 16
            gfl_ld = LD.n

            NT_BH = nt // n_bh
            NG_BH = ng // n_bh
            silt_sp = [LD.n]
            pe_h_done = [PE.n]
            pe_w2_done = [PE.n]

            @block.gpsimd
            def _(gq):
                for fc in range(8):
                    gq.dma_start(out=w1b[:, fc * HME:(fc + 1) * HME],
                                 in_=W1[0, fc * 128:(fc + 1) * 128, :]
                                 ).then_inc(GD1.h, 16)
                    GD1.n += 16

            eseq = [(bh, e) for bh in range(n_bh) for e in range(NE)]
            for idx, (bh, e) in enumerate(eseq):
                @block.gpsimd
                def _(gq, bh=bh, e=e):
                    gq.wait_ge(PE.h, PE.n if SERIAL_B else pe_w2_done[0])
                    for hc in range(16):
                        gq.dma_start(out=w2b[:, hc * D:(hc + 1) * D],
                                     in_=W2[e, hc * 128:(hc + 1) * 128, :]
                                     ).then_inc(GD2.h, 16)
                        GD2.n += 16

                for gl in range(NG_BH):
                    g = bh * NG_BH + gl

                    @block.sync
                    def _(q, e=e, g=g):
                        q.wait_ge(PE.h, PE.n)
                        q.wait_ge(LD.h, LD.n)
                        q.dma_start(
                            out=gflbuf[:],
                            in_=sc_gates[0:1, e * TC + g * GS:
                                         e * TC + (g + 1) * GS]
                            ).then_inc(LD.h, 16)
                        LD.n += 16

                    @block.tensor
                    def _(t, e=e, g=g):
                        t.wait_ge(GD1.h, GD1.n)
                        t.wait_ge(LD.h, LD.n)
                        bwait(t, "P_BC")
                        t.matmul(P_BC[:], onesb[:], gflbuf[:],
                                 start=True, stop=True).then_inc(PE.h, 1)
                        PE.n += 1

                    @block.vector
                    def _(v):
                        v.wait_ge(PE.h, PE.n)
                        v.tensor_copy(gbc[:], P_BC[:]).then_inc(DV.h, 1)
                        DV.n += 1
                    bfree("P_BC", DV)

                    for hc in range(16):
                        PPn = "P_A" if hc % 2 == 0 else "P_B"
                        PP = P_A if hc % 2 == 0 else P_B
                        HT = htmp[hc % 2]

                        @block.tensor
                        def _(t, e=e, g=g, hc=hc, PP=PP, PPn=PPn):
                            bwait(t, PPn)
                            last = None
                            for fc in range(8):
                                last = t.matmul(
                                    PP[:],
                                    w1b[:, fc * HME + hc * 128:
                                         fc * HME + (hc + 1) * 128],
                                    xn_fs[fc][:, g * GS:(g + 1) * GS],
                                    start=(fc == 0), stop=(fc == 7))
                            last.then_inc(PE.h, 1)
                            PE.n += 1

                        @block.vector
                        def _(v, e=e, gl=gl, hc=hc, PP=PP, HT=HT):
                            v.wait_ge(PE.h, PE.n)
                            bwait(v, "HT%d" % (hc % 2))
                            v.tensor_scalar_add(HT[:], PP[:],
                                                b1s[:, e * 16 + hc:
                                                    e * 16 + hc + 1])
                            v.scalar_tensor_tensor(
                                hhat[:, (gl * 16 + hc) * GS:
                                     (gl * 16 + hc + 1) * GS],
                                HT[:], 0.0, gbc[:],
                                ALU.max, ALU.mult).then_inc(DV.h, 1)
                            DV.n += 1
                        bfree(PPn, DV)
                        bfree("HT%d" % (hc % 2), DV)
                pe_h_done[0] = PE.n
                h_done_dv = DV.n

                if idx + 1 < len(eseq):
                    bhn, en = eseq[idx + 1]

                    @block.gpsimd
                    def _(gq, en=en):
                        gq.wait_ge(PE.h, PE.n if SERIAL_B else pe_h_done[0])
                        for fc in range(8):
                            gq.dma_start(out=w1b[:, fc * HME:(fc + 1) * HME],
                                         in_=W1[en, fc * 128:(fc + 1) * 128, :]
                                         ).then_inc(GD1.h, 16)
                            GD1.n += 16

                for gl in range(NG_BH):
                    g = bh * NG_BH + gl
                    for tl in range(NTG):
                        tt = g * NTG + tl
                        ttl = tt - bh * NT_BH
                        par = (gl * NTG + tl) % 2
                        Q0, Q1 = (P_P0, P_P1) if par == 0 else (P_M, P_BC)
                        Q0n, Q1n = ("P_P0", "P_P1") if par == 0 else ("P_M", "P_BC")

                        @block.tensor
                        def _(t, bh=bh, e=e, g=g, gl=gl, tl=tl, tt=tt,
                              Q0=Q0, Q1=Q1, Q0n=Q0n, Q1n=Q1n,
                              h_done_dv=h_done_dv):
                            t.wait_ge(GD2.h, GD2.n)
                            t.wait_ge(DV.h, h_done_dv)
                            bwait(t, Q0n, Q1n)
                            last_is_b2 = (e == 0)
                            last = None
                            for hc in range(16):
                                lhs = hhat[:, (gl * 16 + hc) * GS + tl * 128:
                                           (gl * 16 + hc) * GS + (tl + 1) * 128]
                                stp = (hc == 15) and not last_is_b2
                                t.matmul(Q0[:], lhs,
                                         w2b[:, hc * D:hc * D + 512],
                                         start=(hc == 0), stop=stp)
                                last = t.matmul(Q1[:], lhs,
                                                w2b[:, hc * D + 512:(hc + 1) * D],
                                                start=(hc == 0), stop=stp)
                            if last_is_b2:
                                t.matmul(Q0[:],
                                         gates_rows[0:NE, tt * 128:(tt + 1) * 128],
                                         b2s[:, 0:512], start=False, stop=True)
                                last = t.matmul(
                                    Q1[:],
                                    gates_rows[0:NE, tt * 128:(tt + 1) * 128],
                                    b2s[:, 512:1024], start=False, stop=True)
                            last.then_inc(PE.h, 1)
                            PE.n += 1

                        @block.vector
                        def _(v, e=e, ttl=ttl, Q0=Q0, Q1=Q1):
                            v.wait_ge(PE.h, PE.n)
                            if e == 0:
                                v.tensor_copy(macc[:, ttl * D:ttl * D + 512],
                                              Q0[:])
                                v.tensor_copy(macc[:, ttl * D + 512:(ttl + 1) * D],
                                              Q1[:]).then_inc(DV.h, 1)
                            else:
                                v.tensor_tensor(macc[:, ttl * D:ttl * D + 512],
                                                Q0[:],
                                                macc[:, ttl * D:ttl * D + 512],
                                                ALU.add)
                                v.tensor_tensor(
                                    macc[:, ttl * D + 512:(ttl + 1) * D],
                                    Q1[:],
                                    macc[:, ttl * D + 512:(ttl + 1) * D],
                                    ALU.add).then_inc(DV.h, 1)
                            DV.n += 1
                        bfree(Q0n, DV)
                        bfree(Q1n, DV)
                pe_w2_done[0] = PE.n


                if e == NE - 1:
                    silt_ld = {}
                    for ttl in range(NT_BH):
                        tt = bh * NT_BH + ttl
                        ld_need = silt_ld.get(ttl - 2, LD.n)

                        @block.scalar
                        def _(s2, ttl=ttl, tt=tt, ld_need=ld_need):
                            s2.wait_ge(DV.h, DV.n)
                            s2.wait_ge(LD.h, ld_need)
                            s2.activation(silt[ttl % 2][:],
                                          macc[:, ttl * D:(ttl + 1) * D],
                                          AF.Silu).then_inc(AC.h, 1)
                            AC.n += 1

                        @block.sync
                        def _(q, tt=tt, ttl=ttl):
                            q.wait_ge(AC.h, AC.n)
                            q.dma_start(out=sc_g[tt * 128:(tt + 1) * 128, :],
                                        in_=silt[ttl % 2][:]).then_inc(LD.h, 16)
                            LD.n += 16
                        silt_ld[ttl] = LD.n
                    silt_sp[0] = LD.n
            b_spills_done = LD.n

            barrier()

            close_scope("B")
            close_scope("AB")
            s_start = sbt("s_start", [128, 8 * C], F32, scope="C")
            d_r = sbt("d_r", [128, NE], F32, scope="C")
            m_r = sbt("m_r", [128, 8 * C], F32, scope="C")
            dadj = sbt("dadj", [128, NE], F32, scope="C")
            qtf = sbt("qtf", [128, 8 * TC], BF16, scope="C")
            sloc2 = sbt("sloc2", [128, 8 * nch * C], BF16, scope="C")
            ktf = sbt("ktf", [128, 8 * TC], BF16, scope="C")
            wob = sbt("wob", [128, 8 * D], BF16, scope="C")
            vA = [sbt(f"vA{i}", [128, D], BF16, scope="C") for i in range(NTG)]
            vB = [sbt(f"vB{i}", [128, D], BF16, scope="C") for i in range(NTG)]
            stmp = sbt("stmp", [128, C], F32, scope="C")
            sl32 = sbt("sl32", [128, C], F32, scope="C")
            stb = [sbt(f"stb{i}", [128, C], BF16, scope="C") for i in range(2)]
            at_sb = sbt("at_sb", [128, C], BF16, scope="C")
            o_sb = sbt("o_sb", [128, 8 * GS], BF16, scope="C")
            xn32t = sbt("xn32t", [128, D], F32, scope="C")
            s1t = sbt("s1t", [128, D], F32, scope="C")

            maybe_stop("b", [("g", sc_g), ("gates", sc_gates)])
            @block.sync
            def _(q):
                q.wait_ge(LD.h, LD.n)
                q.wait_ge(CCS.h, CCS.n)
                for oc in range(8):
                    q.dma_start(out=qtf[:, oc * TC:(oc + 1) * TC],
                                in_=sc_qt[oc]).then_inc(LD.h, 16)
                    q.dma_start(out=ktf[:, oc * TC:(oc + 1) * TC],
                                in_=sc_kt[oc]).then_inc(LD.h, 16)
                    LD.n += 32
                q.dma_start(out=sloc2[:], in_=sc_sloc).then_inc(LD.h, 16)
                LD.n += 16

            @block.gpsimd
            def _(gq):
                for p in range(8):
                    gq.dma_start(out=wob[:, p * D:(p + 1) * D],
                                 in_=Wo[p * 128:(p + 1) * 128, :]).then_inc(GD.h, 16)
                    GD.n += 16
            c_gd = GD.n

            @block.vector
            def _(v):
                v.wait_ge(LD.h, LD.n)
                v.memset(s_start[:], 0.0).then_inc(DV.h, 1)
                DV.n += 1

            for r in range(NCORES):
                @block.sync
                def _(q, r=r):
                    q.wait_ge(DV.h, DV.n)
                    q.dma_start(out=d_r[:, 0:8],
                                in_=cc_outD[r * 128:(r + 1) * 128, :]
                                ).then_inc(LD.h, 16)
                    LD.n += 16
                    for p in range(8):
                        q.dma_start(out=m_r[:, p * C:(p + 1) * C],
                                    in_=cc_outM[(r * 8 + p) * 128:
                                                (r * 8 + p + 1) * 128, :]
                                    ).then_inc(LD.h, 16)
                        LD.n += 16

                @block.vector
                def _(v, r=r):
                    v.wait_ge(LD.h, LD.n)
                    v.tensor_scalar(dadj[:], d_r[:], rmk[:, r:r + 1],
                                    rmkb[:, r:r + 1], ALU.mult, ALU.add)
                    v.tensor_scalar_mul(m_r[:], m_r[:], rmk[:, r:r + 1])
                    v.tensor_copy(zeros64[:], zeros64[:])
                    v.tensor_copy(zeros64[:], zeros64[:])
                    for p in range(8):
                        v.tensor_scalar_mul(s_start[:, p * C:(p + 1) * C],
                                            s_start[:, p * C:(p + 1) * C],
                                            dadj[:, p:p + 1])
                        v.tensor_tensor(s_start[:, p * C:(p + 1) * C],
                                        m_r[:, p * C:(p + 1) * C],
                                        s_start[:, p * C:(p + 1) * C], ALU.add)
                    v.tensor_copy(junk1[:], s_start[:, 0:1]).then_inc(DV.h, 1)
                    DV.n += 1

            s1t_sp = [LD.n]
            for g in range(ng):
                c_ld_now = [0]
                @block.sync
                def _(q, g=g):
                    q.wait_ge(PE.h, PE.n)
                    q.wait_ge(LD.h, LD.n)
                    for m in range(NTG):
                        n0 = g * NCHG + 2 * m
                        q.dma_start(out=vA[m][:],
                                    in_=sc_v[n0 * C:(n0 + 2) * C, :]
                                    ).then_inc(LD.h, 16)
                        q.dma_start(out=vB[m][0:64, :],
                                    in_=sc_v[(n0 + 1) * C:(n0 + 2) * C, :]
                                    ).then_inc(LD.h, 16)
                        q.dma_start(out=vB[m][64:128, :],
                                    in_=sc_v[n0 * C:(n0 + 1) * C, :]
                                    ).then_inc(LD.h, 16)
                        LD.n += 48
                c_ld_now[0] = LD.n

                for n in range(NCHG):
                    n_glob = g * NCHG + n
                    m = n // 2
                    bb = (n % 2) * 64
                    t0 = n_glob * C

                    for p in range(8):
                        ip = (n * 8 + p) % 2
                        PO0, PO1 = ((P_P0, P_P1) if ip == 0 else (P_N, P_TRF))
                        PO0n, PO1n = (("P_P0", "P_P1") if ip == 0
                                      else ("P_N", "P_TRF"))
                        STB = stb[ip]

                        @block.vector
                        def _(v, p=p, n_glob=n_glob, STB=STB, ip=ip):
                            bwait(v, "STB%d" % ip)
                            v.scalar_tensor_tensor(
                                STB[:], s_start[:, p * C:(p + 1) * C],
                                gam[:, p * nch + n_glob:
                                    p * nch + n_glob + 1],
                                sloc2[:, (p * nch + n_glob) * C:
                                      (p * nch + n_glob + 1) * C],
                                ALU.mult, ALU.add).then_inc(DV.h, 1)
                            DV.n += 1

                        @block.tensor
                        def _(t, p=p, t0=t0, g=g):
                            t.wait_ge(LD.h, c_ld_now[0])
                            t.wait_ge(GD.h, GD.n)
                            bwait(t, "P_M", "P_BC")
                            t.matmul(P_M[0:64, 0:64],
                                     ktf[0:64, p * TC + t0:p * TC + t0 + C],
                                     qtf[0:64, p * TC + t0:p * TC + t0 + C],
                                     start=True, stop=True, tile_position=(0, 0))
                            t.matmul(P_BC[64:128, 0:64],
                                     ktf[64:128, p * TC + t0:p * TC + t0 + C],
                                     qtf[64:128, p * TC + t0:p * TC + t0 + C],
                                     start=True, stop=True, tile_position=(64, 64)
                                     ).then_inc(PE.h, 1)
                            PE.n += 1

                        @block.vector
                        def _(v, p=p):
                            v.wait_ge(PE.h, PE.n)
                            v.tensor_tensor(at_sb[0:64, :], P_M[0:64, 0:64],
                                            maskA[0:64, :], ALU.mult)
                            v.tensor_tensor(at_sb[64:128, :], P_BC[64:128, 0:64],
                                            maskA[64:128, :], ALU.mult
                                            ).then_inc(DV.h, 1)
                            DV.n += 1
                        bfree("P_M", DV)
                        bfree("P_BC", DV)

                        @block.tensor
                        def _(t, p=p, n=n, m=m, bb=bb, t0=t0, PO0=PO0,
                              PO1=PO1, PO0n=PO0n, PO1n=PO1n, STB=STB):
                            t.wait_ge(DV.h, DV.n)
                            bwait(t, PO0n, PO1n)
                            t.matmul(PO0[0:64, 0:64], STB[0:64, :],
                                     qtf[0:64, p * TC + t0:p * TC + t0 + C],
                                     start=True, stop=False, tile_position=(0, 0))
                            t.matmul(PO1[64:128, 0:64],
                                     STB[64:128, :],
                                     qtf[64:128, p * TC + t0:p * TC + t0 + C],
                                     start=True, stop=False, tile_position=(64, 64))
                            va = vA[m] if n % 2 == 0 else vB[m]
                            vb_ = vB[m] if n % 2 == 0 else vA[m]
                            t.matmul(PO0[0:64, 0:64], va[0:64, p * 128:p * 128 + 64],
                                     at_sb[0:64, :], start=False, stop=True,
                                     tile_position=(0, 0))
                            t.matmul(PO1[64:128, 0:64],
                                     vb_[64:128, p * 128 + 64:(p + 1) * 128],
                                     at_sb[64:128, :], start=False, stop=True,
                                     tile_position=(64, 64)).then_inc(PE.h, 1)
                            PE.n += 1
                        bfree("STB%d" % ip, PE)

                        @block.vector
                        def _(v, p=p, n=n, PO0=PO0, PO1=PO1):
                            v.wait_ge(PE.h, PE.n)
                            v.tensor_copy(
                                o_sb[0:64, p * GS + n * C:p * GS + (n + 1) * C],
                                PO0[0:64, 0:64])
                            v.tensor_copy(
                                o_sb[64:128, p * GS + n * C:p * GS + (n + 1) * C],
                                PO1[64:128, 0:64]).then_inc(DV.h, 1)
                            DV.n += 1
                        bfree(PO0n, DV)
                        bfree(PO1n, DV)

                for tl in range(NTG):
                    tt = g * NTG + tl

                    @block.sync
                    def _(q, tt=tt):
                        q.wait_ge(DV.h, DV.n)
                        q.dma_start(out=xn32t[:],
                                    in_=sc_xn32[tt * 128:(tt + 1) * 128, :]
                                    ).then_inc(LD.h, 16)
                        LD.n += 16
                    xn_ld = LD.n

                    @block.tensor
                    def _(t, tl=tl):
                        t.wait_ge(DV.h, DV.n)
                        t.wait_ge(AC.h, AC.n)
                        last = None
                        for p in range(8):
                            lhs = o_sb[:, p * GS + tl * 128:p * GS + (tl + 1) * 128]
                            t.matmul(P_A[:], lhs, wob[:, p * D:p * D + 512],
                                     start=(p == 0), stop=(p == 7))
                            last = t.matmul(P_B[:], lhs,
                                            wob[:, p * D + 512:(p + 1) * D],
                                            start=(p == 0), stop=(p == 7))
                        last.then_inc(PE.h, 1)
                        PE.n += 1

                    @block.vector
                    def _(v, tt=tt, xn_ld=xn_ld):
                        v.wait_ge(PE.h, PE.n)
                        v.wait_ge(LD.h, LD.n)
                        v.tensor_tensor(s1t[:, 0:512], P_A[:], xn32t[:, 0:512],
                                        ALU.add)
                        v.tensor_tensor(s1t[:, 512:1024], P_B[:],
                                        xn32t[:, 512:1024], ALU.add
                                        ).then_inc(DV.h, 1)
                        DV.n += 1

                    @block.sync
                    def _(q, tt=tt):
                        q.wait_ge(DV.h, DV.n)
                        q.dma_start(out=sc_s1[tt * 128:(tt + 1) * 128, :],
                                    in_=s1t[:]).then_inc(LD.h, 16)
                        LD.n += 16
                    s1t_sp[0] = LD.n
            c_spills_done = LD.n

            barrier()

            close_scope("C")
            s1_sb = sbt("s1_sb", [128, nt_th * D], F32, scope="D")
            h2fs = [sbt(f"h2f{i}", [128, THS], BF16, scope="D", tdest=True)
                 for i in range(8)]
            w1d = sbt("w1d", [128, 8 * 1408], BF16, scope="D")
            w2d = sbt("w2d", [128, 8 * 1408], BF16, scope="D")
            w3d = sbt("w3d", [128, 11 * D], BF16, scope="D")
            zbuf = sbt("zbuf", [128, 11 * GS], BF16, scope="D")
            zsil = [sbt(f"zsil{i}", [128, GS], F32, scope="D") for i in range(2)]
            rstd2c = sbt("rstd2c", [128, 1], F32, scope="D")
            h2bt = sbt("h2bt", [128, D], BF16, scope="D")
            gxt = sbt("gxt", [128, D], F32, scope="D")
            xrt = sbt("xrt", [128, D], F32, scope="D")
            rt = sbt("rt", [128, D], F32, scope="D")

            maybe_stop("c", [("s1", sc_s1)])
            h2sp = [LD.n]
            for th in range(n_th):
                toff = th * THS

                for tl in range(nt_th):
                    tt = toff // 128 + tl

                    @block.sync
                    def _(q, th=th, tl=tl, tt=tt):
                        if th == 0 and tl == 0:
                            q.wait_ge(LD.h, LD.n)
                        q.wait_ge(DV.h, DV.n)
                        q.dma_start(out=s1_sb[:, tl * D:(tl + 1) * D],
                                    in_=sc_s1[tt * 128:(tt + 1) * 128, :]
                                    ).then_inc(LD.h, 16)
                        LD.n += 16
                    s1_ld = LD.n

                    @block.scalar
                    def _(s, tl=tl, s1_ld=s1_ld):
                        s.wait_ge(LD.h, LD.n)
                        s.activation(gxt[:], s1_sb[:, tl * D:(tl + 1) * D],
                                     AF.Square, accum_out=stat[:, 7:8])
                        s.activation(stat[:, 6:7], stat[:, 7:8], AF.Ln,
                                     bias=constv[:, 1:2], scale=constv[:, 3:4])
                        s.activation(rstd2c[:], stat[:, 6:7], AF.Exp,
                                     scale=constv[:, 2:3]).then_inc(AC.h, 1)
                        AC.n += 1

                    @block.vector
                    def _(v, tl=tl):
                        v.wait_ge(AC.h, AC.n)
                        v.wait_ge(LD.h, LD.n)
                        v.scalar_tensor_tensor(h2bt[:], s1_sb[:, tl * D:(tl + 1) * D],
                                               rstd2c[:], gfbc[:],
                                               ALU.mult, ALU.mult).then_inc(DV.h, 1)
                        DV.n += 1

                    @block.sync
                    def _(q, tt=tt):
                        q.wait_ge(DV.h, DV.n)
                        q.dma_start(out=sc_h2[tt * 128:(tt + 1) * 128, :],
                                    in_=h2bt[:]).then_inc(LD.h, 16)
                        LD.n += 16
                    h2sp[0] = LD.n
                h2_spilled = LD.n

                @block.sync
                def _(q, th=th, toff=toff):
                    q.wait_ge(LD.h, LD.n)
                    q.wait_ge(PE.h, PE.n)
                    for fc in range(8):
                        q.dma_start(out=h2fs[fc][:],
                                    in_=sc_h2[toff:toff + THS,
                                              fc * 128:(fc + 1) * 128],
                                    transpose=True).then_inc(LD.h, 16)
                        LD.n += 16
                h2f_ld = LD.n

                for dh in range(2):
                    chunks = DFF_HALF_CHUNKS[dh]
                    w_half = DFF_HALF_W[dh]
                    off = DFF_HALF_OFF[dh]

                    @block.gpsimd
                    def _(gq, dh=dh, off=off, w_half=w_half):
                        gq.wait_ge(PE.h, PE.n)
                        for fc in range(8):
                            gq.dma_start(out=w1d[:, fc * 1408:fc * 1408 + w_half],
                                         in_=w1[fc * 128:(fc + 1) * 128,
                                                off:off + w_half]).then_inc(GD.h, 16)
                            gq.dma_start(out=w2d[:, fc * 1408:fc * 1408 + w_half],
                                         in_=w2[fc * 128:(fc + 1) * 128,
                                                off:off + w_half]).then_inc(GD.h, 16)
                            GD.n += 32
                        poff = 0
                        for ci, cw in enumerate(chunks):
                            gq.dma_start(out=w3d[0:cw, ci * D:(ci + 1) * D],
                                         in_=w3[off + poff:off + poff + cw, :]
                                         ).then_inc(GD.h, 16)
                            GD.n += 16
                            poff += cw
                    d_gd_now = [GD.n]
                    d_ld_now = [LD.n]

                    for g in range(ng_th):
                        poff = 0
                        for ci, cw in enumerate(chunks):
                            par = ci % 2
                            Z0, Z1 = (P_P0, P_P1) if par == 0 else (P_M, P_BC)
                            Z0n, Z1n = (("P_P0", "P_P1") if par == 0
                                        else ("P_M", "P_BC"))
                            ZS = zsil[par]

                            @block.tensor
                            def _(t, g=g, ci=ci, cw=cw, poff=poff,
                                  Z0=Z0, Z1=Z1, Z0n=Z0n, Z1n=Z1n):
                                t.wait_ge(GD.h, d_gd_now[0])
                                t.wait_ge(LD.h, d_ld_now[0])
                                bwait(t, Z0n, Z1n)
                                last = None
                                for fc in range(8):
                                    rhs = h2fs[fc][:, g * GS:(g + 1) * GS]
                                    t.matmul(Z0[0:cw, :],
                                             w1d[:, fc * 1408 + poff:
                                                  fc * 1408 + poff + cw],
                                             rhs, start=(fc == 0), stop=(fc == 7))
                                    last = t.matmul(
                                        Z1[0:cw, :],
                                        w2d[:, fc * 1408 + poff:
                                             fc * 1408 + poff + cw],
                                        rhs, start=(fc == 0), stop=(fc == 7))
                                last.then_inc(PE.h, 1)
                                PE.n += 1

                            @block.scalar
                            def _(s, ci=ci, cw=cw, Z0=Z0, ZS=ZS, par=par):
                                s.wait_ge(PE.h, PE.n)
                                bwait(s, "ZS%d" % par)
                                s.activation(ZS[0:cw, :], Z0[0:cw, :], AF.Silu
                                             ).then_inc(AC.h, 1)
                                AC.n += 1
                            bfree(Z0n, AC)

                            @block.vector
                            def _(v, ci=ci, cw=cw, Z1=Z1, ZS=ZS):
                                v.wait_ge(AC.h, AC.n)
                                v.tensor_tensor(zbuf[0:cw, ci * GS:(ci + 1) * GS],
                                                Z1[0:cw, :], ZS[0:cw, :],
                                                ALU.mult).then_inc(DV.h, 1)
                                DV.n += 1
                            bfree(Z1n, DV)
                            bfree("ZS%d" % par, DV)
                            poff += cw

                        z_done_dv = DV.n
                        for tl in range(NTG):
                            tlg = g * NTG + tl
                            par = tl % 2
                            W0, W1_ = (P_A, P_B) if par == 0 else (P_N, P_TRF)
                            W0n, W1n = (("P_A", "P_B") if par == 0
                                        else ("P_N", "P_TRF"))

                            @block.tensor
                            def _(t, g=g, tl=tl, chunks=chunks,
                                  W0=W0, W1_=W1_, W0n=W0n, W1n=W1n,
                                  z_done_dv=z_done_dv):
                                t.wait_ge(DV.h, z_done_dv)
                                bwait(t, W0n, W1n)
                                nchk = len(chunks)
                                last = None
                                for ci, cw in enumerate(chunks):
                                    lhs = zbuf[0:cw, ci * GS + tl * 128:
                                               ci * GS + (tl + 1) * 128]
                                    t.matmul(W0[:, 0:512], lhs,
                                             w3d[0:cw, ci * D:ci * D + 512],
                                             start=(ci == 0), stop=(ci == nchk - 1))
                                    last = t.matmul(
                                        W1_[:, 0:512], lhs,
                                        w3d[0:cw, ci * D + 512:(ci + 1) * D],
                                        start=(ci == 0), stop=(ci == nchk - 1))
                                last.then_inc(PE.h, 1)
                                PE.n += 1

                            @block.vector
                            def _(v, tlg=tlg, W0=W0, W1_=W1_):
                                v.wait_ge(PE.h, PE.n)
                                v.tensor_tensor(s1_sb[:, tlg * D:tlg * D + 512],
                                                W0[:, 0:512],
                                                s1_sb[:, tlg * D:tlg * D + 512],
                                                ALU.add)
                                v.tensor_tensor(s1_sb[:, tlg * D + 512:(tlg + 1) * D],
                                                W1_[:, 0:512],
                                                s1_sb[:, tlg * D + 512:(tlg + 1) * D],
                                                ALU.add).then_inc(DV.h, 1)
                                DV.n += 1
                            bfree(W0n, DV)
                            bfree(W1n, DV)

                for tl in range(nt_th):
                    tt = toff // 128 + tl

                    @block.sync
                    def _(q, tt=tt, th=th, tl=tl):
                        if th == 0 and tl == 0:
                            q.wait_ge(LD.h, LD.n)
                        q.wait_ge(DV.h, DV.n)
                        q.dma_start(out=gxt[:], in_=sc_g[tt * 128:(tt + 1) * 128, :]
                                    ).then_inc(LD.h, 16)
                        q.dma_start(out=xrt[:], in_=x_in[tt * 128:(tt + 1) * 128, :]
                                    ).then_inc(LD.h, 16)
                        LD.n += 32
                    e_ld = LD.n

                    @block.vector
                    def _(v, tl=tl, e_ld=e_ld):
                        v.wait_ge(LD.h, LD.n)
                        v.wait_ge(OUT.h, OUT.n)
                        v.tensor_mul(rt[:], s1_sb[:, tl * D:(tl + 1) * D], gxt[:])
                        v.tensor_add(rt[:], rt[:], xrt[:]).then_inc(DV.h, 1)
                        DV.n += 1

                    @block.gpsimd
                    def _(gq, tt=tt):
                        gq.wait_ge(DV.h, DV.n)
                        gq.dma_start(out=out_t[tt * 128:(tt + 1) * 128, :],
                                     in_=rt[:]).then_inc(OUT.h, 16)
                        OUT.n += 16

            @block.gpsimd
            def _(gq):
                gq.wait_ge(OUT.h, OUT.n)

            barrier()
            close_scope("D")
        except _StopBuild:
            pass


    return nc


_MASK_CACHE = {}


def _consts(core_id):
    if core_id in _MASK_CACHE:
        return _MASK_CACHE[core_id]
    ones_b = np.ones((1, 128), ml_dtypes.bfloat16)
    ones_f = np.ones((1, 128), np.float32)
    ones2_f = np.zeros((128, 2), np.float32)
    ones2_f[0:64, 0] = 1.0
    ones2_f[64:128, 1] = 1.0
    bcast2_f = np.zeros((2, 128), np.float32)
    bcast2_f[0, 0:64] = 1.0
    bcast2_f[1, 64:128] = 1.0
    p = np.arange(128)[:, None] % 64
    q = np.arange(C)[None, :]
    mask_tri = (q >= p).astype(np.float32)
    id_b = np.eye(128, dtype=ml_dtypes.bfloat16)
    id_f = np.eye(128, dtype=np.float32)
    rmask = np.zeros((128, NCORES), np.float32)
    rmask[:, :core_id] = 1.0
    rmaskbar = 1.0 - rmask
    d = dict(ones_b=ones_b, ones_f=ones_f, ones2_f=ones2_f,
             bcast2_f=bcast2_f, mask_tri=mask_tri, id_b=id_b, id_f=id_f,
             rmask=rmask, rmaskbar=rmaskbar)
    _MASK_CACHE[core_id] = d
    return d


def kernel(**inputs):
    x = np.ascontiguousarray(np.asarray(inputs["L0_attn_x"])[0])
    T = x.shape[0]
    TC = T // NCORES
    w = {k: np.ascontiguousarray(np.asarray(v)) for k, v in inputs.items()}
    b1 = w["L0_ffn_b1"]
    b1r = np.ascontiguousarray(
        b1.reshape(NE, HME // 128, 128).transpose(0, 2, 1))

    base = dict(
        Wq=w["L0_attn_Wq"], Wk=w["L0_attn_Wk"], Wv=w["L0_attn_Wv"],
        Wg=w["L0_attn_Wg"], Wo=w["L0_attn_Wo"],
        w1=w["L0_attn_w1"], w2=w["L0_attn_w2"], w3=w["L0_attn_w3"],
        g_attn=w["L0_attn_g_attn"].reshape(1, D),
        g_ffn=w["L0_attn_g_ffn"].reshape(1, D),
        Wr=w["L0_ffn_Wr"], br=w["L0_ffn_br"].reshape(1, NE),
        W1=w["L0_ffn_W1"], b1r=b1r, W2=w["L0_ffn_W2"], b2=w["L0_ffn_b2"],
    )

    nc = build_nc(TC=TC, stop_after=globals().get("_STOP_AFTER"))
    in_maps = []
    for c in range(NCORES):
        m = dict(base)
        m["x"] = np.ascontiguousarray(x[c * TC:(c + 1) * TC])
        m.update(_consts(c))
        in_maps.append(m)

    res = run_bass_kernel_spmd(nc, in_maps, core_ids=list(range(NCORES)))
    globals()["_LAST"] = res.results
    out = np.concatenate([np.asarray(r["out"]) for r in res.results], axis=0)
    return out[None]


# version 14
# speedup vs baseline: 1.1106x; 1.0142x over previous
import numpy as np
import ml_dtypes
from contextlib import ExitStack

import concourse.bass as bass
import concourse.mybir as mybir
from concourse.bass_utils import run_bass_kernel_spmd

F32 = mybir.dt.float32
BF16 = mybir.dt.bfloat16
AF = mybir.ActivationFunctionType
ALU = mybir.AluOpType
AX = mybir.AxisListType

NCORES = 8
D = 1024
C = 64
DFF = 2736
NE = 8
HME = 2048
GS = 512
EPS_LN = 1e-5
EPS_RMS = 1e-6
EPS_N = 1e-12
DFF_HALF_CHUNKS = [[128] * 11, [128] * 10 + [48]]
DFF_HALF_W = [sum(h) for h in DFF_HALF_CHUNKS]
DFF_HALF_OFF = [0, 1408]


class Sem:
    def __init__(self, stack, nc, name):
        self.h = stack.enter_context(nc.semaphore(name))
        self.n = 0


class _StopBuild(Exception):
    pass


SERIAL_B = False


def build_nc(TC=2048, stop_after=None):
    nt = TC // 128
    ng = TC // GS
    nch = TC // C
    NCHG = GS // C
    NTG = GS // 128
    n_th = 2 if TC >= 2048 else 1
    n_bh = 2 if TC >= 2048 else 1
    NG_BH_C = (TC // GS) // n_bh
    THS = TC // n_th
    nt_th = THS // 128
    ng_th = THS // GS

    nc = bass.Bass("TRN2", target_bir_lowering=False, debug=False)

    def din(name, shape, dt=F32):
        return nc.dram_tensor(name, shape, dt, kind="ExternalInput").ap()

    x_in = din("x", [TC, D])
    Wq = din("Wq", [D, D]); Wk = din("Wk", [D, D])
    Wv = din("Wv", [D, D]); Wg = din("Wg", [D, D])
    Wo = din("Wo", [D, D])
    w1 = din("w1", [D, DFF]); w2 = din("w2", [D, DFF]); w3 = din("w3", [DFF, D])
    g_attn = din("g_attn", [1, D]); g_ffn = din("g_ffn", [1, D])
    Wr = din("Wr", [D, NE]); br = din("br", [1, NE])
    W1 = din("W1", [NE, D, HME]); b1r = din("b1r", [NE, 128, HME // 128])
    W2 = din("W2", [NE, HME, D]); b2 = din("b2", [NE, D])
    ones_b = din("ones_b", [1, 128], BF16)
    ones_f = din("ones_f", [1, 128], F32)
    ones2_f = din("ones2_f", [128, 2], F32)
    bcast2_f = din("bcast2_f", [2, 128], F32)
    mask_tri = din("mask_tri", [128, C], F32)
    id_b = din("id_b", [128, 128], BF16)
    id_f = din("id_f", [128, 128], F32)
    rmask = din("rmask", [128, NCORES], F32)
    rmaskbar = din("rmaskbar", [128, NCORES], F32)

    out_t = nc.dram_tensor("out", [TC, D], F32, kind="ExternalOutput").ap()

    def dsc(name, shape, dt=F32):
        return nc.dram_tensor(name, shape, dt, kind="Internal").ap()

    sc_xn32 = dsc("sc_xn32", [TC, D])
    sc_xnb = dsc("sc_xnb", [TC, D], BF16)
    sc_xnbg = dsc("sc_xnbg", [TC, D], BF16)
    sc_rms = dsc("sc_rms", [1, TC])
    sc_qt = dsc("sc_qt", [8, 128, TC], BF16)
    sc_kt = dsc("sc_kt", [8, 128, TC], BF16)
    sc_v = dsc("sc_v", [nch * C, D], BF16)
    sc_sloc = dsc("sc_sloc", [128, 8 * nch * C], BF16)
    sc_s1 = dsc("sc_s1", [TC, D])
    sc_h2 = dsc("sc_h2", [TC, D], BF16)
    sc_g = dsc("sc_g", [TC, D])
    cc_inM = dsc("cc_inM", [8 * 128, C])
    cc_outM = dsc("cc_outM", [NCORES * 8 * 128, C])
    cc_inD = dsc("cc_inD", [128, 8])
    sc_gates = dsc("sc_gates", [1, NE * TC], BF16)
    cc_outD = dsc("cc_outD", [NCORES * 128, 8])

    with ExitStack() as st:
        scopes = {}

        def sbt(name, shape, dt=F32, scope=None, tdest=False):
            tgt = st if scope is None else scopes.setdefault(scope, ExitStack())
            return tgt.enter_context(
                nc.sbuf_tensor(name, shape, dt, is_transpose_dest=tdest))

        def close_scope(name):
            if name in scopes:
                scopes.pop(name).close()

        def pst(name, shape, dt=F32):
            return st.enter_context(nc.psum_tensor(name, shape, dt))

        onesb = sbt("onesb", [1, 128], BF16)
        onesf = sbt("onesf", [1, 128], F32)
        ones2f = sbt("ones2f", [128, 2], F32)
        bc2f = sbt("bc2f", [2, 128], F32)
        maskA = sbt("maskA", [128, C], F32)
        idb = sbt("idb", [128, 128], BF16)
        idf = sbt("idf", [128, 128], F32)
        rmk = sbt("rmk", [128, NCORES], F32)
        rmkb = sbt("rmkb", [128, NCORES], F32)
        garow = sbt("garow", [1, D], F32)
        gfrow = sbt("gfrow", [1, D], F32)
        gfbc = sbt("gfbc", [128, D], F32)
        brf = sbt("brf", [1, NE], F32)
        b2s = sbt("b2s", [NE, D], F32)
        rmsr = sbt("rmsr", [128, nt], F32)
        zeros64 = sbt("zeros64", [128, C], F32)
        constv = sbt("constv", [128, 4], F32)
        junk1 = sbt("junk1", [128, 1], F32)
        gam = sbt("gam", [128, 8 * nch], F32)

        stat = sbt("stat", [128, 8])
        rstd_c = sbt("rstd_c", [128, 1])





        P_A = pst("P_A", [128, 512])
        P_B = pst("P_B", [128, 512])
        P_P0 = pst("P_P0", [128, 512])
        P_P1 = pst("P_P1", [128, 512])
        P_N = pst("P_N", [128, 512])
        P_BC = pst("P_BC", [128, 512])
        P_TRB = pst("P_TRB", [128, 1024], BF16)
        P_TRF = P_TRB.ap().bitcast(F32)
        P_M = pst("P_M", [128, 512])

        LD = Sem(st, nc, "LD")
        GD = Sem(st, nc, "GD")
        GD1 = Sem(st, nc, "GD1")
        GD2 = Sem(st, nc, "GD2")
        PE = Sem(st, nc, "PE")
        DV = Sem(st, nc, "DV")
        AC = Sem(st, nc, "AC")
        CCS = Sem(st, nc, "CCS")
        OUT = Sem(st, nc, "OUT")
        BAR = Sem(st, nc, "BAR")

        bankfree = {}

        def bfree(name, sem):
            bankfree[name] = (sem.h, sem.n)

        def bwait(eng, *names):
            for nm in names:
                if nm in bankfree:
                    h, n = bankfree[nm]
                    eng.wait_ge(h, n)

        block = st.enter_context(nc.Block())
        nbar = [0]

        def maybe_stop(tag, dumps=()):
            if stop_after == tag:
                @block.gpsimd
                def _(gq):
                    gq.wait_ge(CCS.h, CCS.n)
                    gq.wait_ge(LD.h, LD.n)
                    gq.wait_ge(GD.h, GD.n)
                    for nm, src in dumps:
                        dt_ = src.dtype if hasattr(src, "dtype") else F32
                        dd = nc.dram_tensor("dbg_" + nm, list(src.shape), dt_,
                                            kind="ExternalOutput").ap()
                        gq.dma_start(out=dd, in_=src).then_inc(OUT.h, 16)
                        OUT.n += 16
                    gq.wait_ge(OUT.h, OUT.n)
                barrier()
                for k in list(scopes.keys())[::-1]:
                    scopes.pop(k).close()
                raise _StopBuild()

        def barrier():
            nbar[0] += 1
            tgt = 5 * nbar[0]
            for deco in (block.sync, block.gpsimd, block.tensor,
                         block.vector, block.scalar):
                @deco
                def _(q):
                    q.sem_inc(BAR.h, 1)
                    q.wait_ge(BAR.h, tgt)

        try:
            @block.sync
            def _(q):
                for dst, src in ((onesb, ones_b), (onesf, ones_f),
                                 (ones2f, ones2_f), (bc2f, bcast2_f),
                                 (maskA, mask_tri), (idb, id_b), (idf, id_f),
                                 (rmk, rmask), (rmkb, rmaskbar),
                                 (garow, g_attn), (gfrow, g_ffn), (brf, br),
                                 (b2s, b2)):
                    q.dma_start(out=dst[:], in_=src).then_inc(LD.h, 16)
                    LD.n += 16

            @block.vector
            def _(v):
                v.wait_ge(LD.h, LD.n)
                v.memset(zeros64[:], 0.0)
                v.memset(constv[:, 0:1], EPS_LN)
                v.memset(constv[:, 1:2], EPS_RMS)
                v.memset(constv[:, 2:3], -0.5)
                v.memset(constv[:, 3:4], 1.0 / D).then_inc(DV.h, 1)
                DV.n += 1

            barrier()

            gabc = sbt("gabc", [128, D], F32, scope="A1")
            xt = [sbt(f"xt{i}", [128, D], scope="A1") for i in range(2)]
            xnt = sbt("xnt", [128, D], scope="A1")
            xnbt = sbt("xnbt", [128, D], BF16, scope="A1")
            xnbgt = sbt("xnbgt", [128, D], BF16, scope="A1")
            sq = sbt("sq", [128, D], scope="A1")

            maybe_stop("p0")
            @block.tensor
            def _(t):
                t.wait_ge(LD.h, LD.n)
                t.matmul(P_A[:], onesf[:], garow[0:1, 0:512], start=True, stop=True)
                t.matmul(P_B[:], onesf[:], garow[0:1, 512:1024], start=True, stop=True)
                t.matmul(P_P0[:], onesf[:], gfrow[0:1, 0:512], start=True, stop=True)
                t.matmul(P_P1[:], onesf[:], gfrow[0:1, 512:1024],
                         start=True, stop=True).then_inc(PE.h, 1)
                PE.n += 1

            @block.vector
            def _(v):
                v.wait_ge(PE.h, PE.n)
                v.tensor_copy(gabc[:, 0:512], P_A[:])
                v.tensor_copy(gabc[:, 512:1024], P_B[:])
                v.tensor_copy(gfbc[:, 0:512], P_P0[:])
                v.tensor_copy(gfbc[:, 512:1024], P_P1[:]).then_inc(DV.h, 1)
                DV.n += 1

            x_ld = {}

            @block.sync
            def _(q):
                for t in range(min(2, nt)):
                    q.dma_start(out=xt[t % 2][:],
                                in_=x_in[t * 128:(t + 1) * 128, :]).then_inc(LD.h, 16)
                    LD.n += 16
                    x_ld[t] = LD.n

            dv_done = {}
            ac_done = {}
            spill_done = {}
            for t in range(nt):
                xb = xt[t % 2]

                @block.vector
                def _(v, t=t, xb=xb):
                    v.wait_ge(LD.h, LD.n)
                    v.reduce_sum(stat[:, 0:1], xb[:], axis=AX.X)
                    v.tensor_copy(zeros64[:], zeros64[:])
                    v.tensor_scalar_mul(stat[:, 1:2], stat[:, 0:1], 1.0 / D)
                    v.tensor_copy(zeros64[:], zeros64[:])
                    v.tensor_mul(stat[:, 2:3], stat[:, 1:2], stat[:, 1:2]
                                 ).then_inc(DV.h, 1)
                    DV.n += 1

                @block.scalar
                def _(s, t=t, xb=xb):
                    s.wait_ge(LD.h, LD.n)
                    s.activation(sq[:], xb[:], AF.Square,
                                 accum_out=stat[:, 3:4]).then_inc(AC.h, 1)
                    AC.n += 1

                @block.vector
                def _(v, t=t):
                    v.wait_ge(AC.h, AC.n)
                    v.tensor_scalar_mul(stat[:, 4:5], stat[:, 3:4], 1.0 / D)
                    v.tensor_copy(zeros64[:], zeros64[:])
                    v.tensor_sub(stat[:, 5:6], stat[:, 4:5], stat[:, 2:3]
                                 ).then_inc(DV.h, 1)
                    DV.n += 1

                @block.scalar
                def _(s, t=t):
                    s.wait_ge(DV.h, DV.n)
                    s.activation(stat[:, 6:7], stat[:, 5:6], AF.Ln, bias=constv[:, 0:1])
                    s.activation(rstd_c[:], stat[:, 6:7], AF.Exp,
                                 scale=constv[:, 2:3]).then_inc(AC.h, 1)
                    AC.n += 1

                @block.vector
                def _(v, t=t, xb=xb):
                    v.wait_ge(AC.h, AC.n)
                    if t >= 1:
                        v.wait_ge(LD.h, LD.n)
                    v.tensor_scalar(xnt[:], xb[:], stat[:, 1:2], rstd_c[:],
                                    ALU.subtract, ALU.mult)
                    v.tensor_tensor(xnbgt[:], xnt[:], gabc[:], ALU.mult
                                    ).then_inc(DV.h, 1)
                    DV.n += 1
                    dv_done[t] = DV.n

                @block.scalar
                def _(s, t=t):
                    s.wait_ge(DV.h, dv_done[t])
                    if t >= 1:
                        s.wait_ge(LD.h, LD.n)
                    s.copy(xnbt[:], xnt[:])
                    s.activation(sq[:], xnt[:], AF.Square, accum_out=stat[:, 7:8])
                    s.activation(stat[:, 6:7], stat[:, 7:8], AF.Ln,
                                 bias=constv[:, 1:2], scale=constv[:, 3:4])
                    s.activation(rmsr[:, t:t + 1], stat[:, 6:7], AF.Exp,
                                 scale=constv[:, 2:3]).then_inc(AC.h, 1)
                    AC.n += 1
                    ac_done[t] = AC.n

                @block.sync
                def _(q, t=t):
                    q.wait_ge(DV.h, dv_done[t])
                    q.dma_start(out=sc_xn32[t * 128:(t + 1) * 128, :],
                                in_=xnt[:]).then_inc(LD.h, 16)
                    q.dma_start(out=sc_xnbg[t * 128:(t + 1) * 128, :],
                                in_=xnbgt[:]).then_inc(LD.h, 16)
                    LD.n += 32
                    q.wait_ge(AC.h, ac_done[t])
                    q.dma_start(out=sc_xnb[t * 128:(t + 1) * 128, :],
                                in_=xnbt[:]).then_inc(LD.h, 16)
                    q.dma_start(out=sc_rms[0:1, t * 128:(t + 1) * 128]
                                .rearrange("a p -> p a"),
                                in_=rmsr[:, t:t + 1]).then_inc(LD.h, 16)
                    LD.n += 32
                    if t + 2 < nt:
                        q.dma_start(out=xt[t % 2][:],
                                    in_=x_in[(t + 2) * 128:(t + 3) * 128, :]
                                    ).then_inc(LD.h, 16)
                        LD.n += 16
                        x_ld[t + 2] = LD.n

                spill_done[t] = LD.n

            a1_spills_done = LD.n
            barrier()

            close_scope("A1")
            xn_fs = [sbt(f"xn_f{i}", [128, TC], BF16, scope="AB", tdest=True)
                 for i in range(8)]
            l_run = sbt("l_run", [128, 8 * C], F32, scope="A2")
            gam_run = sbt("gam_run", [128, 8], F32, scope="A2")
            s_local = sbt("s_local", [128, 8 * nch * C], BF16, scope="A2")
            wqb = sbt("wqb", [128, 8 * D], BF16, scope="A2")
            wkb = sbt("wkb", [128, 8 * D], BF16, scope="A2")
            wvb = sbt("wvb", [128, 8 * D], BF16, scope="A2")
            wgb = sbt("wgb", [128, 8 * D], BF16, scope="A2")
            rstd_row = sbt("rstd_row", [1, TC], F32, scope="A2")
            rstdb = sbt("rstdb", [128, GS], F32, scope="A2")
            a_oc = sbt("a_oc", [128, GS], F32, scope="A2")
            ra_oc = sbt("ra_oc", [128, GS], F32, scope="A2")
            az = sbt("az", [128, GS], F32, scope="A2")
            b64 = [sbt(f"b64_{i}", [128, C], scope="A2") for i in range(NCHG)]
            rb64 = [sbt(f"rb64_{i}", [128, C], scope="A2") for i in range(NCHG)]
            sqf = sbt("sqf", [128, GS], F32, scope="A2")
            nn2 = sbt("nn2", [2, GS], F32, scope="A2")
            invn2 = sbt("invn2", [2, GS], F32, scope="A2")
            invb = sbt("invb", [128, GS], F32, scope="A2")
            qn_oc = sbt("qn_oc", [128, GS], F32, scope="A2")
            qtb_oc = sbt("qtb_oc", [128, GS], BF16, scope="A2")
            kt32 = sbt("kt32", [128, C], F32, scope="A2")
            khb = sbt("khb", [128, C], BF16, scope="A2")
            khtok = sbt("khtok", [128, 128], BF16, scope="A2")
            vtile = [sbt(f"vtile{i}", [128, D], BF16, scope="A2") for i in range(NTG)]

            maybe_stop("a1", [("xn32", sc_xn32), ("xnb", sc_xnb), ("xnbg", sc_xnbg), ("rms", sc_rms)])
            @block.sync
            def _(q):
                q.wait_ge(LD.h, LD.n)
                for fc in range(8):
                    q.dma_start(out=xn_fs[fc][:],
                                in_=sc_xnbg[:, fc * 128:(fc + 1) * 128],
                                transpose=True).then_inc(LD.h, 16)
                    LD.n += 16
                q.dma_start(out=rstd_row[:], in_=sc_rms).then_inc(LD.h, 16)
                LD.n += 16

            @block.gpsimd
            def _(g):
                for wdst, wsrc in ((wqb, Wq), (wkb, Wk), (wvb, Wv), (wgb, Wg)):
                    for fc in range(8):
                        g.dma_start(out=wdst[:, fc * D:(fc + 1) * D],
                                    in_=wsrc[fc * 128:(fc + 1) * 128, :]
                                    ).then_inc(GD.h, 16)
                        GD.n += 16

            a2_ld = LD.n
            a2_gd = GD.n

            @block.vector
            def _(v):
                v.memset(l_run[:], 0.0)
                v.memset(gam_run[:], 1.0).then_inc(DV.h, 1)
                DV.n += 1
            maybe_stop("a2m")

            @block.sync
            def _(q):
                q.wait_ge(LD.h, LD.n)
            maybe_stop("a2t")

            @block.gpsimd
            def _(gq):
                gq.wait_ge(GD.h, GD.n)
            maybe_stop("a2w")

            def a2_proj(g, oc, PP, wsb, PPn):
                @block.tensor
                def _(t):
                    bwait(t, PPn)
                    last = None
                    for fc in range(8):
                        last = t.matmul(
                            PP[:],
                            wsb[:, fc * D + oc * 128: fc * D + (oc + 1) * 128],
                            xn_fs[fc][:, g * GS:(g + 1) * GS],
                            start=(fc == 0), stop=(fc == 7))
                    last.then_inc(PE.h, 1)
                    PE.n += 1

            def a2_l2norm(PP, PPn):
                @block.scalar
                def _(s):
                    s.wait_ge(PE.h, PE.n)
                    s.activation(sqf[:], PP[:], AF.Square).then_inc(AC.h, 1)
                    AC.n += 1

                @block.tensor
                def _(t):
                    t.wait_ge(AC.h, AC.n)
                    t.matmul(P_N[0:2, :], ones2f[:], sqf[:], start=True, stop=True
                             ).then_inc(PE.h, 1)
                    PE.n += 1

                @block.scalar
                def _(s):
                    s.wait_ge(PE.h, PE.n)
                    s.activation(nn2[:], P_N[0:2, :], AF.Sqrt).then_inc(AC.h, 1)
                    AC.n += 1

                @block.vector
                def _(v):
                    v.wait_ge(AC.h, AC.n)
                    v.tensor_scalar_max(nn2[:], nn2[:], EPS_N)
                    v.reciprocal(invn2[:], nn2[:]).then_inc(DV.h, 1)
                    DV.n += 1

                @block.tensor
                def _(t):
                    t.wait_ge(DV.h, DV.n)
                    t.matmul(P_BC[:], bc2f[:], invn2[:], start=True, stop=True
                             ).then_inc(PE.h, 1)
                    PE.n += 1

                @block.vector
                def _(v):
                    v.wait_ge(PE.h, PE.n)
                    v.tensor_copy(invb[:], P_BC[:])
                    v.tensor_tensor(qn_oc[:], PP[:], invb[:], ALU.mult
                                    ).then_inc(DV.h, 1)
                    DV.n += 1
                bfree(PPn, DV)

            vspill = {}
            kspill_ld = [LD.n]
            a2_ld_now = [LD.n]
            a2_gd_now = [GD.n]
            for g in range(ng):
                @block.tensor
                def _(t, g=g):
                    t.wait_ge(LD.h, a2_ld_now[0])
                    t.wait_ge(GD.h, a2_gd_now[0])
                    t.wait_ge(DV.h, DV.n)
                    t.matmul(P_BC[:], onesf[:], rstd_row[0:1, g * GS:(g + 1) * GS],
                             start=True, stop=True).then_inc(PE.h, 1)
                    PE.n += 1

                @block.vector
                def _(v, g=g):
                    v.wait_ge(PE.h, PE.n)
                    v.tensor_copy(rstdb[:], P_BC[:]).then_inc(DV.h, 1)
                    DV.n += 1
                if g == 0:
                    maybe_stop("a2r")

                for tl in range(NTG):
                    tt = g * NTG + tl

                    @block.tensor
                    def _(t, tt=tt):
                        bwait(t, "P_A", "P_B")
                        last = None
                        for fc in range(8):
                            lhs = xn_fs[fc][:, tt * 128:(tt + 1) * 128]
                            t.matmul(P_A[:], lhs, wvb[:, fc * D:fc * D + 512],
                                     start=(fc == 0), stop=(fc == 7))
                            last = t.matmul(P_B[:], lhs,
                                            wvb[:, fc * D + 512:(fc + 1) * D],
                                            start=(fc == 0), stop=(fc == 7))
                        last.then_inc(PE.h, 1)
                        PE.n += 1

                    if g == 0 and tl == 0:
                        maybe_stop("a2vm")

                    @block.vector
                    def _(v, tl=tl, tt=tt, g=g):
                        v.wait_ge(PE.h, PE.n)
                        if g > 0:
                            v.wait_ge(LD.h, LD.n)
                        v.tensor_scalar_mul(vtile[tl][:, 0:512], P_A[:],
                                            rmsr[:, tt:tt + 1])
                        v.tensor_scalar_mul(vtile[tl][:, 512:1024], P_B[:],
                                            rmsr[:, tt:tt + 1]).then_inc(DV.h, 1)
                        DV.n += 1
                    bfree("P_A", DV)
                    bfree("P_B", DV)

                    if g == 0 and tl == 0:
                        maybe_stop("a2ve")

                    @block.sync
                    def _(q, tl=tl, tt=tt):
                        q.wait_ge(DV.h, DV.n)
                        for half in range(2):
                            q.dma_start(out=sc_v[(tt * 2 + half) * C:
                                                 (tt * 2 + half + 1) * C, :],
                                        in_=vtile[tl][half * 64:(half + 1) * 64, :]
                                        ).then_inc(LD.h, 16)
                            LD.n += 16
                    vspill[tl] = LD.n
                if g == 0:
                    maybe_stop("a2v")

                for oc in range(8):
                    PP = P_P0 if oc % 2 == 0 else P_P1
                    PPn = 'P_P0' if oc % 2 == 0 else 'P_P1'

                    a2_proj(g, oc, PP, wgb, PPn)

                    @block.vector
                    def _(v, oc=oc, PP=PP):
                        v.wait_ge(PE.h, PE.n)
                        v.tensor_tensor(az[:], PP[:], rstdb[:], ALU.mult
                                        ).then_inc(DV.h, 1)
                        DV.n += 1
                    bfree(PPn, DV)

                    @block.scalar
                    def _(s, oc=oc):
                        s.wait_ge(DV.h, DV.n)
                        s.activation(a_oc[:], az[:], AF.Sigmoid).then_inc(AC.h, 1)
                        AC.n += 1

                    @block.vector
                    def _(v, oc=oc):
                        v.wait_ge(AC.h, AC.n)
                        v.reciprocal(ra_oc[:], a_oc[:])
                        last = None
                        for c in range(NCHG):
                            v.tensor_tensor_scan(b64[c][:], a_oc[:, c * C:(c + 1) * C],
                                                 zeros64[:], 1.0, ALU.mult, ALU.add)
                            last = v.tensor_tensor_scan(
                                rb64[c][:], ra_oc[:, c * C:(c + 1) * C],
                                zeros64[:], 1.0, ALU.mult, ALU.add)
                        last.then_inc(DV.h, 1)
                        DV.n += 1

                    if g == 0 and oc == 0:
                        maybe_stop("a2zg")
                    a2_proj(g, oc, PP, wqb, PPn)
                    a2_l2norm(PP, PPn)

                    @block.vector
                    def _(v, g=g, oc=oc):
                        v.wait_ge(LD.h, LD.n)
                        last = None
                        for c in range(NCHG):
                            last = v.tensor_tensor(qtb_oc[:, c * C:(c + 1) * C],
                                                   qn_oc[:, c * C:(c + 1) * C],
                                                   b64[c][:], ALU.mult)
                        last.then_inc(DV.h, 1)
                        DV.n += 1

                    @block.sync
                    def _(q, g=g, oc=oc):
                        q.wait_ge(DV.h, DV.n)
                        q.dma_start(out=sc_qt[oc, :, g * GS:(g + 1) * GS],
                                    in_=qtb_oc[:]).then_inc(LD.h, 16)
                        LD.n += 16
                    q_spill_ld = LD.n

                    if g == 0 and oc == 0:
                        maybe_stop("a2q")
                    a2_proj(g, oc, PP, wkb, PPn)
                    a2_l2norm(PP, PPn)

                    for c in range(NCHG):
                        n_glob = g * NCHG + c
                        bb = (c % 2) * 64

                        @block.vector
                        def _(v, g=g, oc=oc, c=c):
                            if c == 0:
                                v.wait_ge(LD.h, LD.n)
                            v.tensor_tensor(kt32[:], qn_oc[:, c * C:(c + 1) * C],
                                            rb64[c][:], ALU.mult)
                            v.tensor_copy(qtb_oc[:, c * C:(c + 1) * C], kt32[:])
                            v.tensor_scalar_mul(khb[:], kt32[:], b64[c][:, C - 1:C]
                                                ).then_inc(DV.h, 1)
                            DV.n += 1

                        @block.tensor
                        def _(t, c=c):
                            t.wait_ge(DV.h, DV.n)
                            t.transpose(P_TRB[0:64, 0:128], khb[:], idb[:]).then_inc(PE.h, 1)
                            PE.n += 1

                        @block.vector
                        def _(v, c=c, bb=bb):
                            v.wait_ge(PE.h, PE.n)
                            v.tensor_copy(khtok[bb:bb + 64, :], P_TRB[0:64, 0:128]
                                          ).then_inc(DV.h, 1)
                            DV.n += 1

                        @block.tensor
                        def _(t, c=c, bb=bb, oc=oc):
                            t.wait_ge(DV.h, DV.n)
                            vt = vtile[c // 2]
                            t.matmul(P_M[0:64, 0:64], khtok[bb:bb + 64, 0:64],
                                     vt[bb:bb + 64, oc * 128:oc * 128 + 64],
                                     start=True, stop=True, tile_position=(bb, 0))
                            t.matmul(P_BC[64:128, 0:64], khtok[bb:bb + 64, 64:128],
                                     vt[bb:bb + 64, oc * 128 + 64:(oc + 1) * 128],
                                     start=True, stop=True, tile_position=(bb, 64)
                                     ).then_inc(PE.h, 1)
                            PE.n += 1

                        @block.vector
                        def _(v, oc=oc, c=c, n_glob=n_glob):
                            v.wait_ge(PE.h, PE.n)
                            v.tensor_copy(
                                s_local[:, (oc * nch + n_glob) * C:
                                        (oc * nch + n_glob + 1) * C],
                                l_run[:, oc * C:(oc + 1) * C])
                            v.tensor_scalar_mul(l_run[:, oc * C:(oc + 1) * C],
                                                l_run[:, oc * C:(oc + 1) * C],
                                                b64[c][:, C - 1:C])
                            v.tensor_copy(gam[:, oc * nch + n_glob:
                                              oc * nch + n_glob + 1],
                                          gam_run[:, oc:oc + 1])
                            v.tensor_tensor(l_run[0:64, oc * C:(oc + 1) * C],
                                            P_M[0:64, 0:64],
                                            l_run[0:64, oc * C:(oc + 1) * C],
                                            ALU.add)
                            v.tensor_tensor(l_run[64:128, oc * C:(oc + 1) * C],
                                            P_BC[64:128, 0:64],
                                            l_run[64:128, oc * C:(oc + 1) * C],
                                            ALU.add)
                            v.tensor_scalar_mul(gam_run[:, oc:oc + 1],
                                                gam_run[:, oc:oc + 1],
                                                b64[c][:, C - 1:C]).then_inc(DV.h, 1)
                            DV.n += 1

                    @block.sync
                    def _(q, g=g, oc=oc):
                        q.wait_ge(DV.h, DV.n)
                        q.dma_start(out=sc_kt[oc, :, g * GS:(g + 1) * GS],
                                    in_=qtb_oc[:]).then_inc(LD.h, 16)
                        LD.n += 16
                    kspill_ld[0] = LD.n
                    if g == 0 and oc == 0:
                        maybe_stop("a2k")

            @block.sync
            def _(q):
                q.wait_ge(DV.h, DV.n)
                q.dma_start(out=sc_sloc, in_=s_local[:]).then_inc(LD.h, 16)
                LD.n += 16
            a2_spills_done = LD.n

            barrier()

            maybe_stop("a2", [("qt", sc_qt), ("kt", sc_kt), ("v", sc_v), ("sloc", sc_sloc), ("lrun", l_run[:]), ("gam", gam[:]), ("gamrun", gam_run[:])])
            @block.gpsimd
            def _(gq):
                for p in range(8):
                    gq.dma_start(out=cc_inM[p * 128:(p + 1) * 128, :],
                                 in_=l_run[:, p * C:(p + 1) * C]).then_inc(GD.h, 16)
                    GD.n += 16
                gq.dma_start(out=cc_inD, in_=gam_run[:]).then_inc(GD.h, 16)
                GD.n += 16
                gq.wait_ge(GD.h, GD.n)
                gq.collective_compute(
                    "AllGather", ALU.bypass,
                    replica_groups=[list(range(NCORES))],
                    ins=[cc_inM.opt()], outs=[cc_outM.opt()],
                ).then_inc(CCS.h, 1)
                gq.collective_compute(
                    "AllGather", ALU.bypass,
                    replica_groups=[list(range(NCORES))],
                    ins=[cc_inD.opt()], outs=[cc_outD.opt()],
                ).then_inc(CCS.h, 1)
                CCS.n += 2

            close_scope("A2")
            gates_rows = sbt("gates_rows", [NE, TC], F32, scope="B")
            gates_rows_b = sbt("gates_rows_b", [NE, TC], BF16, scope="B")
            gflbuf = sbt("gflbuf", [1, GS], BF16, scope="B")
            wrb = sbt("wrb", [128, 8 * NE], BF16, scope="B")
            lg = sbt("lg", [128, 2 * NE], scope="B")
            srt = sbt("srt", [128, 2 * NE], scope="B")
            sel = sbt("sel", [128, 2 * NE], scope="B")
            ega = sbt("ega", [128, 2 * NE], scope="B")
            den2 = sbt("den2", [128, 2], scope="B")
            di2 = sbt("di2", [128, 2], scope="B")
            gat2 = sbt("gat2", [128, 2 * NE], scope="B")
            w1b = sbt("w1b", [128, 8 * HME], BF16, scope="B")
            w2b = sbt("w2b", [128, 16 * D], BF16, scope="B")
            b1s = sbt("b1s", [128, NE * (HME // 128)], F32, scope="B")
            gbc = sbt("gbc", [128, GS], BF16, scope="B")
            htmp = [sbt(f"htmp{i}", [128, GS], F32, scope="B") for i in range(2)]
            hhat = sbt("hhat", [128, NG_BH_C * 16 * GS], BF16, scope="B")
            macc = sbt("macc", [128, (nt // n_bh) * D], F32, scope="B")
            silt = [sbt(f"silt{i}", [128, D], F32, scope="B") for i in range(2)]

            maybe_stop("x", [("ccM", cc_outM), ("ccD", cc_outD)])
            @block.sync
            def _(q):
                for fc in range(8):
                    q.dma_start(out=xn_fs[fc][:],
                                in_=sc_xnb[:, fc * 128:(fc + 1) * 128],
                                transpose=True).then_inc(LD.h, 16)
                    LD.n += 16

            @block.gpsimd
            def _(gq):
                for fc in range(8):
                    gq.dma_start(out=wrb[:, fc * NE:(fc + 1) * NE],
                                 in_=Wr[fc * 128:(fc + 1) * 128, :]).then_inc(GD.h, 16)
                    GD.n += 16
                for e in range(NE):
                    gq.dma_start(out=b1s[:, e * 16:(e + 1) * 16],
                                 in_=b1r[e]).then_inc(GD.h, 16)
                    GD.n += 16

            b_ld = LD.n
            b_gd = GD.n

            for tp in range(0, nt, 2):
                @block.tensor
                def _(t, tp=tp):
                    t.wait_ge(LD.h, LD.n)
                    t.wait_ge(GD.h, GD.n)
                    t.wait_ge(DV.h, DV.n)
                    last = None
                    for j in range(2):
                        tt = tp + j
                        PP = P_P0 if j == 0 else P_P1
                        for fc in range(8):
                            t.matmul(PP[:, 0:NE],
                                     xn_fs[fc][:, tt * 128:(tt + 1) * 128],
                                     wrb[:, fc * NE:(fc + 1) * NE],
                                     start=(fc == 0), stop=False)
                        last = t.matmul(PP[:, 0:NE], onesf[:], brf[:],
                                        start=False, stop=True)
                    last.then_inc(PE.h, 1)
                    PE.n += 1

                @block.vector
                def _(v, tp=tp):
                    v.wait_ge(PE.h, PE.n)
                    v.tensor_copy(lg[:, 0:NE], P_P0[:, 0:NE])
                    v.tensor_copy(lg[:, NE:2 * NE], P_P1[:, 0:NE])
                    v.tensor_copy(zeros64[:], zeros64[:])
                    v.tensor_copy(zeros64[:], zeros64[:])
                    v.max(srt[:, 0:NE], lg[:, 0:NE])
                    v.max(srt[:, NE:2 * NE], lg[:, NE:2 * NE])
                    v.tensor_copy(zeros64[:], zeros64[:])
                    v.tensor_copy(zeros64[:], zeros64[:])
                    v.tensor_scalar(sel[:, 0:NE], lg[:, 0:NE], srt[:, 1:2], 0.0,
                                    ALU.is_ge, ALU.add)
                    v.tensor_scalar(sel[:, NE:2 * NE], lg[:, NE:2 * NE],
                                    srt[:, NE + 1:NE + 2], 0.0, ALU.is_ge, ALU.add)
                    v.tensor_scalar(ega[:, 0:NE], lg[:, 0:NE], srt[:, 0:1], 0.0,
                                    ALU.subtract, ALU.add)
                    v.tensor_scalar(ega[:, NE:2 * NE], lg[:, NE:2 * NE],
                                    srt[:, NE:NE + 1], 0.0, ALU.subtract, ALU.add
                                    ).then_inc(DV.h, 1)
                    DV.n += 1

                @block.scalar
                def _(s, tp=tp):
                    s.wait_ge(DV.h, DV.n)
                    s.activation(ega[:, 0:2 * NE], ega[:, 0:2 * NE], AF.Exp
                                 ).then_inc(AC.h, 1)
                    AC.n += 1

                @block.vector
                def _(v, tp=tp):
                    v.wait_ge(AC.h, AC.n)
                    v.tensor_mul(ega[:, 0:2 * NE], ega[:, 0:2 * NE],
                                 sel[:, 0:2 * NE])
                    v.tensor_copy(zeros64[:], zeros64[:])
                    v.reduce_sum(den2[:, 0:1], ega[:, 0:NE], axis=AX.X)
                    v.reduce_sum(den2[:, 1:2], ega[:, NE:2 * NE], axis=AX.X)
                    v.tensor_copy(zeros64[:], zeros64[:])
                    v.tensor_copy(zeros64[:], zeros64[:])
                    v.reciprocal(di2[:], den2[:])
                    v.tensor_copy(zeros64[:], zeros64[:])
                    v.tensor_copy(zeros64[:], zeros64[:])
                    v.tensor_copy(junk1[:], di2[:, 0:1])
                    v.tensor_scalar_mul(gat2[:, 0:NE], ega[:, 0:NE], di2[:, 0:1])
                    v.tensor_scalar_mul(gat2[:, NE:2 * NE], ega[:, NE:2 * NE],
                                        di2[:, 1:2]).then_inc(DV.h, 1)
                    DV.n += 1

                @block.tensor
                def _(t, tp=tp):
                    t.wait_ge(DV.h, DV.n)
                    t.transpose(P_BC[0:NE, 0:128], gat2[:, 0:NE], idf[:])
                    t.transpose(P_A[0:NE, 0:128], gat2[:, NE:2 * NE], idf[:]
                                ).then_inc(PE.h, 1)
                    PE.n += 1

                @block.vector
                def _(v, tp=tp):
                    v.wait_ge(PE.h, PE.n)
                    v.tensor_copy(gates_rows[:, tp * 128:(tp + 1) * 128],
                                  P_BC[0:NE, 0:128])
                    v.tensor_copy(gates_rows[:, (tp + 1) * 128:(tp + 2) * 128],
                                  P_A[0:NE, 0:128])
                    v.tensor_copy(gates_rows_b[:, tp * 128:(tp + 1) * 128],
                                  P_BC[0:NE, 0:128])
                    v.tensor_copy(gates_rows_b[:, (tp + 1) * 128:(tp + 2) * 128],
                                  P_A[0:NE, 0:128]).then_inc(DV.h, 1)
                    DV.n += 1

            @block.sync
            def _(q):
                q.wait_ge(DV.h, DV.n)
                q.dma_start(out=sc_gates.rearrange("a (e t) -> (a e) t", e=NE),
                            in_=gates_rows_b[:]).then_inc(LD.h, 16)
                LD.n += 16
            gfl_ld = LD.n

            NT_BH = nt // n_bh
            NG_BH = ng // n_bh
            silt_sp = [LD.n]
            pe_h_done = [PE.n]
            pe_w2_done = [PE.n]

            @block.gpsimd
            def _(gq):
                for fc in range(8):
                    gq.dma_start(out=w1b[:, fc * HME:(fc + 1) * HME],
                                 in_=W1[0, fc * 128:(fc + 1) * 128, :]
                                 ).then_inc(GD1.h, 16)
                    GD1.n += 16

            eseq = [(bh, e) for bh in range(n_bh) for e in range(NE)]
            for idx, (bh, e) in enumerate(eseq):
                @block.gpsimd
                def _(gq, bh=bh, e=e):
                    gq.wait_ge(PE.h, PE.n if SERIAL_B else pe_w2_done[0])
                    for hc in range(16):
                        gq.dma_start(out=w2b[:, hc * D:(hc + 1) * D],
                                     in_=W2[e, hc * 128:(hc + 1) * 128, :]
                                     ).then_inc(GD2.h, 16)
                        GD2.n += 16

                for gl in range(NG_BH):
                    g = bh * NG_BH + gl

                    @block.sync
                    def _(q, e=e, g=g):
                        q.wait_ge(PE.h, PE.n)
                        q.wait_ge(LD.h, LD.n)
                        q.dma_start(
                            out=gflbuf[:],
                            in_=sc_gates[0:1, e * TC + g * GS:
                                         e * TC + (g + 1) * GS]
                            ).then_inc(LD.h, 16)
                        LD.n += 16

                    @block.tensor
                    def _(t, e=e, g=g):
                        t.wait_ge(GD1.h, GD1.n)
                        t.wait_ge(LD.h, LD.n)
                        bwait(t, "P_BC")
                        t.matmul(P_BC[:], onesb[:], gflbuf[:],
                                 start=True, stop=True).then_inc(PE.h, 1)
                        PE.n += 1

                    @block.vector
                    def _(v):
                        v.wait_ge(PE.h, PE.n)
                        v.tensor_copy(gbc[:], P_BC[:]).then_inc(DV.h, 1)
                        DV.n += 1
                    bfree("P_BC", DV)

                    for hc in range(16):
                        PPn = "P_A" if hc % 2 == 0 else "P_B"
                        PP = P_A if hc % 2 == 0 else P_B
                        HT = htmp[hc % 2]

                        @block.tensor
                        def _(t, e=e, g=g, hc=hc, PP=PP, PPn=PPn):
                            bwait(t, PPn)
                            last = None
                            for fc in range(8):
                                last = t.matmul(
                                    PP[:],
                                    w1b[:, fc * HME + hc * 128:
                                         fc * HME + (hc + 1) * 128],
                                    xn_fs[fc][:, g * GS:(g + 1) * GS],
                                    start=(fc == 0), stop=(fc == 7))
                            last.then_inc(PE.h, 1)
                            PE.n += 1

                        @block.vector
                        def _(v, e=e, gl=gl, hc=hc, PP=PP, HT=HT):
                            v.wait_ge(PE.h, PE.n)
                            bwait(v, "HT%d" % (hc % 2))
                            v.tensor_scalar_add(HT[:], PP[:],
                                                b1s[:, e * 16 + hc:
                                                    e * 16 + hc + 1])
                            v.scalar_tensor_tensor(
                                hhat[:, (gl * 16 + hc) * GS:
                                     (gl * 16 + hc + 1) * GS],
                                HT[:], 0.0, gbc[:],
                                ALU.max, ALU.mult).then_inc(DV.h, 1)
                            DV.n += 1
                        bfree(PPn, DV)
                        bfree("HT%d" % (hc % 2), DV)
                pe_h_done[0] = PE.n
                h_done_dv = DV.n

                if idx + 1 < len(eseq):
                    bhn, en = eseq[idx + 1]

                    @block.gpsimd
                    def _(gq, en=en):
                        gq.wait_ge(PE.h, PE.n if SERIAL_B else pe_h_done[0])
                        for fc in range(8):
                            gq.dma_start(out=w1b[:, fc * HME:(fc + 1) * HME],
                                         in_=W1[en, fc * 128:(fc + 1) * 128, :]
                                         ).then_inc(GD1.h, 16)
                            GD1.n += 16

                for gl in range(NG_BH):
                    g = bh * NG_BH + gl
                    for tl in range(NTG):
                        tt = g * NTG + tl
                        ttl = tt - bh * NT_BH
                        par = (gl * NTG + tl) % 2
                        Q0, Q1 = (P_P0, P_P1) if par == 0 else (P_M, P_BC)
                        Q0n, Q1n = ("P_P0", "P_P1") if par == 0 else ("P_M", "P_BC")

                        @block.tensor
                        def _(t, bh=bh, e=e, g=g, gl=gl, tl=tl, tt=tt,
                              Q0=Q0, Q1=Q1, Q0n=Q0n, Q1n=Q1n,
                              h_done_dv=h_done_dv):
                            t.wait_ge(GD2.h, GD2.n)
                            t.wait_ge(DV.h, h_done_dv)
                            bwait(t, Q0n, Q1n)
                            last_is_b2 = (e == 0)
                            last = None
                            for hc in range(16):
                                lhs = hhat[:, (gl * 16 + hc) * GS + tl * 128:
                                           (gl * 16 + hc) * GS + (tl + 1) * 128]
                                stp = (hc == 15) and not last_is_b2
                                t.matmul(Q0[:], lhs,
                                         w2b[:, hc * D:hc * D + 512],
                                         start=(hc == 0), stop=stp)
                                last = t.matmul(Q1[:], lhs,
                                                w2b[:, hc * D + 512:(hc + 1) * D],
                                                start=(hc == 0), stop=stp)
                            if last_is_b2:
                                t.matmul(Q0[:],
                                         gates_rows[0:NE, tt * 128:(tt + 1) * 128],
                                         b2s[:, 0:512], start=False, stop=True)
                                last = t.matmul(
                                    Q1[:],
                                    gates_rows[0:NE, tt * 128:(tt + 1) * 128],
                                    b2s[:, 512:1024], start=False, stop=True)
                            last.then_inc(PE.h, 1)
                            PE.n += 1

                        @block.vector
                        def _(v, e=e, ttl=ttl, Q0=Q0, Q1=Q1):
                            v.wait_ge(PE.h, PE.n)
                            if e == 0:
                                v.tensor_copy(macc[:, ttl * D:ttl * D + 512],
                                              Q0[:])
                                v.tensor_copy(macc[:, ttl * D + 512:(ttl + 1) * D],
                                              Q1[:]).then_inc(DV.h, 1)
                            else:
                                v.tensor_tensor(macc[:, ttl * D:ttl * D + 512],
                                                Q0[:],
                                                macc[:, ttl * D:ttl * D + 512],
                                                ALU.add)
                                v.tensor_tensor(
                                    macc[:, ttl * D + 512:(ttl + 1) * D],
                                    Q1[:],
                                    macc[:, ttl * D + 512:(ttl + 1) * D],
                                    ALU.add).then_inc(DV.h, 1)
                            DV.n += 1
                        bfree(Q0n, DV)
                        bfree(Q1n, DV)
                pe_w2_done[0] = PE.n


                if e == NE - 1:
                    silt_ld = {}
                    for ttl in range(NT_BH):
                        tt = bh * NT_BH + ttl
                        ld_need = silt_ld.get(ttl - 2, LD.n)

                        @block.scalar
                        def _(s2, ttl=ttl, tt=tt, ld_need=ld_need):
                            s2.wait_ge(DV.h, DV.n)
                            s2.wait_ge(LD.h, ld_need)
                            s2.activation(silt[ttl % 2][:],
                                          macc[:, ttl * D:(ttl + 1) * D],
                                          AF.Silu).then_inc(AC.h, 1)
                            AC.n += 1

                        @block.sync
                        def _(q, tt=tt, ttl=ttl):
                            q.wait_ge(AC.h, AC.n)
                            q.dma_start(out=sc_g[tt * 128:(tt + 1) * 128, :],
                                        in_=silt[ttl % 2][:]).then_inc(LD.h, 16)
                            LD.n += 16
                        silt_ld[ttl] = LD.n
                    silt_sp[0] = LD.n
            b_spills_done = LD.n

            barrier()

            close_scope("B")
            close_scope("AB")
            s_start = sbt("s_start", [128, 8 * C], F32, scope="C")
            d_r = sbt("d_r", [128, NE], F32, scope="C")
            m_r = sbt("m_r", [128, 8 * C], F32, scope="C")
            dadj = sbt("dadj", [128, NE], F32, scope="C")
            qtf = sbt("qtf", [128, 8 * TC], BF16, scope="C")
            sloc2 = sbt("sloc2", [128, 8 * nch * C], BF16, scope="C")
            ktf = sbt("ktf", [128, 8 * TC], BF16, scope="C")
            wob = sbt("wob", [128, 8 * D], BF16, scope="C")
            vA = [sbt(f"vA{i}", [128, D], BF16, scope="C") for i in range(NTG)]
            vB = [sbt(f"vB{i}", [128, D], BF16, scope="C") for i in range(NTG)]
            stmp = sbt("stmp", [128, C], F32, scope="C")
            sl32 = sbt("sl32", [128, C], F32, scope="C")
            stb = [sbt(f"stb{i}", [128, C], BF16, scope="C") for i in range(2)]
            at_sb = sbt("at_sb", [128, C], BF16, scope="C")
            o_sb = sbt("o_sb", [128, 8 * GS], BF16, scope="C")
            xn32t = sbt("xn32t", [128, D], F32, scope="C")
            s1t = sbt("s1t", [128, D], F32, scope="C")

            maybe_stop("b", [("g", sc_g), ("gates", sc_gates)])
            @block.sync
            def _(q):
                q.wait_ge(LD.h, LD.n)
                q.wait_ge(CCS.h, CCS.n)
                for oc in range(8):
                    q.dma_start(out=qtf[:, oc * TC:(oc + 1) * TC],
                                in_=sc_qt[oc]).then_inc(LD.h, 16)
                    q.dma_start(out=ktf[:, oc * TC:(oc + 1) * TC],
                                in_=sc_kt[oc]).then_inc(LD.h, 16)
                    LD.n += 32
                q.dma_start(out=sloc2[:], in_=sc_sloc).then_inc(LD.h, 16)
                LD.n += 16

            @block.gpsimd
            def _(gq):
                for p in range(8):
                    gq.dma_start(out=wob[:, p * D:(p + 1) * D],
                                 in_=Wo[p * 128:(p + 1) * 128, :]).then_inc(GD.h, 16)
                    GD.n += 16
            c_gd = GD.n

            @block.vector
            def _(v):
                v.wait_ge(LD.h, LD.n)
                v.memset(s_start[:], 0.0).then_inc(DV.h, 1)
                DV.n += 1

            for r in range(NCORES):
                @block.sync
                def _(q, r=r):
                    q.wait_ge(DV.h, DV.n)
                    q.dma_start(out=d_r[:, 0:8],
                                in_=cc_outD[r * 128:(r + 1) * 128, :]
                                ).then_inc(LD.h, 16)
                    LD.n += 16
                    for p in range(8):
                        q.dma_start(out=m_r[:, p * C:(p + 1) * C],
                                    in_=cc_outM[(r * 8 + p) * 128:
                                                (r * 8 + p + 1) * 128, :]
                                    ).then_inc(LD.h, 16)
                        LD.n += 16

                @block.vector
                def _(v, r=r):
                    v.wait_ge(LD.h, LD.n)
                    v.tensor_scalar(dadj[:], d_r[:], rmk[:, r:r + 1],
                                    rmkb[:, r:r + 1], ALU.mult, ALU.add)
                    v.tensor_scalar_mul(m_r[:], m_r[:], rmk[:, r:r + 1])
                    v.tensor_copy(zeros64[:], zeros64[:])
                    v.tensor_copy(zeros64[:], zeros64[:])
                    for p in range(8):
                        v.tensor_scalar_mul(s_start[:, p * C:(p + 1) * C],
                                            s_start[:, p * C:(p + 1) * C],
                                            dadj[:, p:p + 1])
                        v.tensor_tensor(s_start[:, p * C:(p + 1) * C],
                                        m_r[:, p * C:(p + 1) * C],
                                        s_start[:, p * C:(p + 1) * C], ALU.add)
                    v.tensor_copy(junk1[:], s_start[:, 0:1]).then_inc(DV.h, 1)
                    DV.n += 1

            s1t_sp = [LD.n]
            pend_evac = [None]
            for g in range(ng):
                c_ld_now = [0]
                @block.sync
                def _(q, g=g):
                    q.wait_ge(PE.h, PE.n)
                    q.wait_ge(LD.h, LD.n)
                    for m in range(NTG):
                        n0 = g * NCHG + 2 * m
                        q.dma_start(out=vA[m][:],
                                    in_=sc_v[n0 * C:(n0 + 2) * C, :]
                                    ).then_inc(LD.h, 16)
                        q.dma_start(out=vB[m][0:64, :],
                                    in_=sc_v[(n0 + 1) * C:(n0 + 2) * C, :]
                                    ).then_inc(LD.h, 16)
                        q.dma_start(out=vB[m][64:128, :],
                                    in_=sc_v[n0 * C:(n0 + 1) * C, :]
                                    ).then_inc(LD.h, 16)
                        LD.n += 48
                c_ld_now[0] = LD.n

                for n in range(NCHG):
                    n_glob = g * NCHG + n
                    m = n // 2
                    bb = (n % 2) * 64
                    t0 = n_glob * C

                    for p in range(8):
                        ip = (n * 8 + p) % 2
                        PO0, PO1 = ((P_P0, P_P1) if ip == 0 else (P_N, P_TRF))
                        PO0n, PO1n = (("P_P0", "P_P1") if ip == 0
                                      else ("P_N", "P_TRF"))
                        STB = stb[ip]

                        @block.vector
                        def _(v, p=p, n_glob=n_glob, STB=STB, ip=ip):
                            bwait(v, "STB%d" % ip)
                            v.scalar_tensor_tensor(
                                STB[:], s_start[:, p * C:(p + 1) * C],
                                gam[:, p * nch + n_glob:
                                    p * nch + n_glob + 1],
                                sloc2[:, (p * nch + n_glob) * C:
                                      (p * nch + n_glob + 1) * C],
                                ALU.mult, ALU.add).then_inc(DV.h, 1)
                            DV.n += 1

                        @block.tensor
                        def _(t, p=p, t0=t0, g=g):
                            t.wait_ge(LD.h, c_ld_now[0])
                            t.wait_ge(GD.h, GD.n)
                            bwait(t, "P_M", "P_BC")
                            t.matmul(P_M[0:64, 0:64],
                                     ktf[0:64, p * TC + t0:p * TC + t0 + C],
                                     qtf[0:64, p * TC + t0:p * TC + t0 + C],
                                     start=True, stop=True, tile_position=(0, 0))
                            t.matmul(P_BC[64:128, 0:64],
                                     ktf[64:128, p * TC + t0:p * TC + t0 + C],
                                     qtf[64:128, p * TC + t0:p * TC + t0 + C],
                                     start=True, stop=True, tile_position=(64, 64)
                                     ).then_inc(PE.h, 1)
                            PE.n += 1

                        @block.vector
                        def _(v, p=p):
                            v.wait_ge(PE.h, PE.n)
                            v.tensor_tensor(at_sb[0:64, :], P_M[0:64, 0:64],
                                            maskA[0:64, :], ALU.mult)
                            v.tensor_tensor(at_sb[64:128, :], P_BC[64:128, 0:64],
                                            maskA[64:128, :], ALU.mult
                                            ).then_inc(DV.h, 1)
                            DV.n += 1
                        bfree("P_M", DV)
                        bfree("P_BC", DV)

                        dv_mask = DV.n
                        if pend_evac[0] is not None:
                            pend_evac[0]()
                            pend_evac[0] = None

                        @block.tensor
                        def _(t, p=p, n=n, m=m, bb=bb, t0=t0, PO0=PO0,
                              PO1=PO1, PO0n=PO0n, PO1n=PO1n, STB=STB,
                              dv_mask=dv_mask):
                            t.wait_ge(DV.h, dv_mask)
                            bwait(t, PO0n, PO1n)
                            t.matmul(PO0[0:64, 0:64], STB[0:64, :],
                                     qtf[0:64, p * TC + t0:p * TC + t0 + C],
                                     start=True, stop=False, tile_position=(0, 0))
                            t.matmul(PO1[64:128, 0:64],
                                     STB[64:128, :],
                                     qtf[64:128, p * TC + t0:p * TC + t0 + C],
                                     start=True, stop=False, tile_position=(64, 64))
                            va = vA[m] if n % 2 == 0 else vB[m]
                            vb_ = vB[m] if n % 2 == 0 else vA[m]
                            t.matmul(PO0[0:64, 0:64], va[0:64, p * 128:p * 128 + 64],
                                     at_sb[0:64, :], start=False, stop=True,
                                     tile_position=(0, 0))
                            t.matmul(PO1[64:128, 0:64],
                                     vb_[64:128, p * 128 + 64:(p + 1) * 128],
                                     at_sb[64:128, :], start=False, stop=True,
                                     tile_position=(64, 64)).then_inc(PE.h, 1)
                            PE.n += 1
                        bfree("STB%d" % ip, PE)

                        def mk_evac(p=p, n=n, PO0=PO0, PO1=PO1,
                                    PO0n=PO0n, PO1n=PO1n):
                            def emit():
                                @block.vector
                                def _(v, p=p, n=n, PO0=PO0, PO1=PO1):
                                    v.wait_ge(PE.h, PE.n)
                                    v.tensor_copy(
                                        o_sb[0:64,
                                             p * GS + n * C:p * GS + (n + 1) * C],
                                        PO0[0:64, 0:64])
                                    v.tensor_copy(
                                        o_sb[64:128,
                                             p * GS + n * C:p * GS + (n + 1) * C],
                                        PO1[64:128, 0:64]).then_inc(DV.h, 1)
                                    DV.n += 1
                                bfree(PO0n, DV)
                                bfree(PO1n, DV)
                            return emit
                        pend_evac[0] = mk_evac()

                if pend_evac[0] is not None:
                    pend_evac[0]()
                    pend_evac[0] = None

                for tl in range(NTG):
                    tt = g * NTG + tl

                    @block.sync
                    def _(q, tt=tt):
                        q.wait_ge(DV.h, DV.n)
                        q.dma_start(out=xn32t[:],
                                    in_=sc_xn32[tt * 128:(tt + 1) * 128, :]
                                    ).then_inc(LD.h, 16)
                        LD.n += 16
                    xn_ld = LD.n

                    @block.tensor
                    def _(t, tl=tl):
                        t.wait_ge(DV.h, DV.n)
                        t.wait_ge(AC.h, AC.n)
                        last = None
                        for p in range(8):
                            lhs = o_sb[:, p * GS + tl * 128:p * GS + (tl + 1) * 128]
                            t.matmul(P_A[:], lhs, wob[:, p * D:p * D + 512],
                                     start=(p == 0), stop=(p == 7))
                            last = t.matmul(P_B[:], lhs,
                                            wob[:, p * D + 512:(p + 1) * D],
                                            start=(p == 0), stop=(p == 7))
                        last.then_inc(PE.h, 1)
                        PE.n += 1

                    @block.vector
                    def _(v, tt=tt, xn_ld=xn_ld):
                        v.wait_ge(PE.h, PE.n)
                        v.wait_ge(LD.h, LD.n)
                        v.tensor_tensor(s1t[:, 0:512], P_A[:], xn32t[:, 0:512],
                                        ALU.add)
                        v.tensor_tensor(s1t[:, 512:1024], P_B[:],
                                        xn32t[:, 512:1024], ALU.add
                                        ).then_inc(DV.h, 1)
                        DV.n += 1

                    @block.sync
                    def _(q, tt=tt):
                        q.wait_ge(DV.h, DV.n)
                        q.dma_start(out=sc_s1[tt * 128:(tt + 1) * 128, :],
                                    in_=s1t[:]).then_inc(LD.h, 16)
                        LD.n += 16
                    s1t_sp[0] = LD.n
            c_spills_done = LD.n

            barrier()

            close_scope("C")
            s1_sb = sbt("s1_sb", [128, nt_th * D], F32, scope="D")
            h2fs = [sbt(f"h2f{i}", [128, THS], BF16, scope="D", tdest=True)
                 for i in range(8)]
            w1d = sbt("w1d", [128, 8 * 1408], BF16, scope="D")
            w2d = sbt("w2d", [128, 8 * 1408], BF16, scope="D")
            w3d = sbt("w3d", [128, 11 * D], BF16, scope="D")
            zbuf = sbt("zbuf", [128, 11 * GS], BF16, scope="D")
            zsil = [sbt(f"zsil{i}", [128, GS], F32, scope="D") for i in range(2)]
            rstd2c = sbt("rstd2c", [128, 1], F32, scope="D")
            h2bt = sbt("h2bt", [128, D], BF16, scope="D")
            gxt = sbt("gxt", [128, D], F32, scope="D")
            xrt = sbt("xrt", [128, D], F32, scope="D")
            rt = sbt("rt", [128, D], F32, scope="D")

            maybe_stop("c", [("s1", sc_s1)])
            h2sp = [LD.n]
            for th in range(n_th):
                toff = th * THS

                for tl in range(nt_th):
                    tt = toff // 128 + tl

                    @block.sync
                    def _(q, th=th, tl=tl, tt=tt):
                        if th == 0 and tl == 0:
                            q.wait_ge(LD.h, LD.n)
                        q.wait_ge(DV.h, DV.n)
                        q.dma_start(out=s1_sb[:, tl * D:(tl + 1) * D],
                                    in_=sc_s1[tt * 128:(tt + 1) * 128, :]
                                    ).then_inc(LD.h, 16)
                        LD.n += 16
                    s1_ld = LD.n

                    @block.scalar
                    def _(s, tl=tl, s1_ld=s1_ld):
                        s.wait_ge(LD.h, LD.n)
                        s.activation(gxt[:], s1_sb[:, tl * D:(tl + 1) * D],
                                     AF.Square, accum_out=stat[:, 7:8])
                        s.activation(stat[:, 6:7], stat[:, 7:8], AF.Ln,
                                     bias=constv[:, 1:2], scale=constv[:, 3:4])
                        s.activation(rstd2c[:], stat[:, 6:7], AF.Exp,
                                     scale=constv[:, 2:3]).then_inc(AC.h, 1)
                        AC.n += 1

                    @block.vector
                    def _(v, tl=tl):
                        v.wait_ge(AC.h, AC.n)
                        v.wait_ge(LD.h, LD.n)
                        v.scalar_tensor_tensor(h2bt[:], s1_sb[:, tl * D:(tl + 1) * D],
                                               rstd2c[:], gfbc[:],
                                               ALU.mult, ALU.mult).then_inc(DV.h, 1)
                        DV.n += 1

                    @block.sync
                    def _(q, tt=tt):
                        q.wait_ge(DV.h, DV.n)
                        q.dma_start(out=sc_h2[tt * 128:(tt + 1) * 128, :],
                                    in_=h2bt[:]).then_inc(LD.h, 16)
                        LD.n += 16
                    h2sp[0] = LD.n
                h2_spilled = LD.n

                @block.sync
                def _(q, th=th, toff=toff):
                    q.wait_ge(LD.h, LD.n)
                    q.wait_ge(PE.h, PE.n)
                    for fc in range(8):
                        q.dma_start(out=h2fs[fc][:],
                                    in_=sc_h2[toff:toff + THS,
                                              fc * 128:(fc + 1) * 128],
                                    transpose=True).then_inc(LD.h, 16)
                        LD.n += 16
                h2f_ld = LD.n

                for dh in range(2):
                    chunks = DFF_HALF_CHUNKS[dh]
                    w_half = DFF_HALF_W[dh]
                    off = DFF_HALF_OFF[dh]

                    @block.gpsimd
                    def _(gq, dh=dh, off=off, w_half=w_half):
                        gq.wait_ge(PE.h, PE.n)
                        for fc in range(8):
                            gq.dma_start(out=w1d[:, fc * 1408:fc * 1408 + w_half],
                                         in_=w1[fc * 128:(fc + 1) * 128,
                                                off:off + w_half]).then_inc(GD.h, 16)
                            gq.dma_start(out=w2d[:, fc * 1408:fc * 1408 + w_half],
                                         in_=w2[fc * 128:(fc + 1) * 128,
                                                off:off + w_half]).then_inc(GD.h, 16)
                            GD.n += 32
                        poff = 0
                        for ci, cw in enumerate(chunks):
                            gq.dma_start(out=w3d[0:cw, ci * D:(ci + 1) * D],
                                         in_=w3[off + poff:off + poff + cw, :]
                                         ).then_inc(GD.h, 16)
                            GD.n += 16
                            poff += cw
                    d_gd_now = [GD.n]
                    d_ld_now = [LD.n]

                    for g in range(ng_th):
                        poff = 0
                        for ci, cw in enumerate(chunks):
                            par = ci % 2
                            Z0, Z1 = (P_P0, P_P1) if par == 0 else (P_M, P_BC)
                            Z0n, Z1n = (("P_P0", "P_P1") if par == 0
                                        else ("P_M", "P_BC"))
                            ZS = zsil[par]

                            @block.tensor
                            def _(t, g=g, ci=ci, cw=cw, poff=poff,
                                  Z0=Z0, Z1=Z1, Z0n=Z0n, Z1n=Z1n):
                                t.wait_ge(GD.h, d_gd_now[0])
                                t.wait_ge(LD.h, d_ld_now[0])
                                bwait(t, Z0n, Z1n)
                                last = None
                                for fc in range(8):
                                    rhs = h2fs[fc][:, g * GS:(g + 1) * GS]
                                    t.matmul(Z0[0:cw, :],
                                             w1d[:, fc * 1408 + poff:
                                                  fc * 1408 + poff + cw],
                                             rhs, start=(fc == 0), stop=(fc == 7))
                                    last = t.matmul(
                                        Z1[0:cw, :],
                                        w2d[:, fc * 1408 + poff:
                                             fc * 1408 + poff + cw],
                                        rhs, start=(fc == 0), stop=(fc == 7))
                                last.then_inc(PE.h, 1)
                                PE.n += 1

                            @block.scalar
                            def _(s, ci=ci, cw=cw, Z0=Z0, ZS=ZS, par=par):
                                s.wait_ge(PE.h, PE.n)
                                bwait(s, "ZS%d" % par)
                                s.activation(ZS[0:cw, :], Z0[0:cw, :], AF.Silu
                                             ).then_inc(AC.h, 1)
                                AC.n += 1
                            bfree(Z0n, AC)

                            @block.vector
                            def _(v, ci=ci, cw=cw, Z1=Z1, ZS=ZS):
                                v.wait_ge(AC.h, AC.n)
                                v.tensor_tensor(zbuf[0:cw, ci * GS:(ci + 1) * GS],
                                                Z1[0:cw, :], ZS[0:cw, :],
                                                ALU.mult).then_inc(DV.h, 1)
                                DV.n += 1
                            bfree(Z1n, DV)
                            bfree("ZS%d" % par, DV)
                            poff += cw

                        z_done_dv = DV.n
                        for tl in range(NTG):
                            tlg = g * NTG + tl
                            par = tl % 2
                            W0, W1_ = (P_A, P_B) if par == 0 else (P_N, P_TRF)
                            W0n, W1n = (("P_A", "P_B") if par == 0
                                        else ("P_N", "P_TRF"))

                            @block.tensor
                            def _(t, g=g, tl=tl, chunks=chunks,
                                  W0=W0, W1_=W1_, W0n=W0n, W1n=W1n,
                                  z_done_dv=z_done_dv):
                                t.wait_ge(DV.h, z_done_dv)
                                bwait(t, W0n, W1n)
                                nchk = len(chunks)
                                last = None
                                for ci, cw in enumerate(chunks):
                                    lhs = zbuf[0:cw, ci * GS + tl * 128:
                                               ci * GS + (tl + 1) * 128]
                                    t.matmul(W0[:, 0:512], lhs,
                                             w3d[0:cw, ci * D:ci * D + 512],
                                             start=(ci == 0), stop=(ci == nchk - 1))
                                    last = t.matmul(
                                        W1_[:, 0:512], lhs,
                                        w3d[0:cw, ci * D + 512:(ci + 1) * D],
                                        start=(ci == 0), stop=(ci == nchk - 1))
                                last.then_inc(PE.h, 1)
                                PE.n += 1

                            @block.vector
                            def _(v, tlg=tlg, W0=W0, W1_=W1_):
                                v.wait_ge(PE.h, PE.n)
                                v.tensor_tensor(s1_sb[:, tlg * D:tlg * D + 512],
                                                W0[:, 0:512],
                                                s1_sb[:, tlg * D:tlg * D + 512],
                                                ALU.add)
                                v.tensor_tensor(s1_sb[:, tlg * D + 512:(tlg + 1) * D],
                                                W1_[:, 0:512],
                                                s1_sb[:, tlg * D + 512:(tlg + 1) * D],
                                                ALU.add).then_inc(DV.h, 1)
                                DV.n += 1
                            bfree(W0n, DV)
                            bfree(W1n, DV)

                for tl in range(nt_th):
                    tt = toff // 128 + tl

                    @block.sync
                    def _(q, tt=tt, th=th, tl=tl):
                        if th == 0 and tl == 0:
                            q.wait_ge(LD.h, LD.n)
                        q.wait_ge(DV.h, DV.n)
                        q.dma_start(out=gxt[:], in_=sc_g[tt * 128:(tt + 1) * 128, :]
                                    ).then_inc(LD.h, 16)
                        q.dma_start(out=xrt[:], in_=x_in[tt * 128:(tt + 1) * 128, :]
                                    ).then_inc(LD.h, 16)
                        LD.n += 32
                    e_ld = LD.n

                    @block.vector
                    def _(v, tl=tl, e_ld=e_ld):
                        v.wait_ge(LD.h, LD.n)
                        v.wait_ge(OUT.h, OUT.n)
                        v.tensor_mul(rt[:], s1_sb[:, tl * D:(tl + 1) * D], gxt[:])
                        v.tensor_add(rt[:], rt[:], xrt[:]).then_inc(DV.h, 1)
                        DV.n += 1

                    @block.gpsimd
                    def _(gq, tt=tt):
                        gq.wait_ge(DV.h, DV.n)
                        gq.dma_start(out=out_t[tt * 128:(tt + 1) * 128, :],
                                     in_=rt[:]).then_inc(OUT.h, 16)
                        OUT.n += 16

            @block.gpsimd
            def _(gq):
                gq.wait_ge(OUT.h, OUT.n)

            barrier()
            close_scope("D")
        except _StopBuild:
            pass


    return nc


_MASK_CACHE = {}


def _consts(core_id):
    if core_id in _MASK_CACHE:
        return _MASK_CACHE[core_id]
    ones_b = np.ones((1, 128), ml_dtypes.bfloat16)
    ones_f = np.ones((1, 128), np.float32)
    ones2_f = np.zeros((128, 2), np.float32)
    ones2_f[0:64, 0] = 1.0
    ones2_f[64:128, 1] = 1.0
    bcast2_f = np.zeros((2, 128), np.float32)
    bcast2_f[0, 0:64] = 1.0
    bcast2_f[1, 64:128] = 1.0
    p = np.arange(128)[:, None] % 64
    q = np.arange(C)[None, :]
    mask_tri = (q >= p).astype(np.float32)
    id_b = np.eye(128, dtype=ml_dtypes.bfloat16)
    id_f = np.eye(128, dtype=np.float32)
    rmask = np.zeros((128, NCORES), np.float32)
    rmask[:, :core_id] = 1.0
    rmaskbar = 1.0 - rmask
    d = dict(ones_b=ones_b, ones_f=ones_f, ones2_f=ones2_f,
             bcast2_f=bcast2_f, mask_tri=mask_tri, id_b=id_b, id_f=id_f,
             rmask=rmask, rmaskbar=rmaskbar)
    _MASK_CACHE[core_id] = d
    return d


def kernel(**inputs):
    x = np.ascontiguousarray(np.asarray(inputs["L0_attn_x"])[0])
    T = x.shape[0]
    TC = T // NCORES
    w = {k: np.ascontiguousarray(np.asarray(v)) for k, v in inputs.items()}
    b1 = w["L0_ffn_b1"]
    b1r = np.ascontiguousarray(
        b1.reshape(NE, HME // 128, 128).transpose(0, 2, 1))

    base = dict(
        Wq=w["L0_attn_Wq"], Wk=w["L0_attn_Wk"], Wv=w["L0_attn_Wv"],
        Wg=w["L0_attn_Wg"], Wo=w["L0_attn_Wo"],
        w1=w["L0_attn_w1"], w2=w["L0_attn_w2"], w3=w["L0_attn_w3"],
        g_attn=w["L0_attn_g_attn"].reshape(1, D),
        g_ffn=w["L0_attn_g_ffn"].reshape(1, D),
        Wr=w["L0_ffn_Wr"], br=w["L0_ffn_br"].reshape(1, NE),
        W1=w["L0_ffn_W1"], b1r=b1r, W2=w["L0_ffn_W2"], b2=w["L0_ffn_b2"],
    )

    nc = build_nc(TC=TC, stop_after=globals().get("_STOP_AFTER"))
    in_maps = []
    for c in range(NCORES):
        m = dict(base)
        m["x"] = np.ascontiguousarray(x[c * TC:(c + 1) * TC])
        m.update(_consts(c))
        in_maps.append(m)

    res = run_bass_kernel_spmd(nc, in_maps, core_ids=list(range(NCORES)))
    globals()["_LAST"] = res.results
    out = np.concatenate([np.asarray(r["out"]) for r in res.results], axis=0)
    return out[None]


# version 15 (final)
# speedup vs baseline: 1.1200x; 1.0085x over previous
import numpy as np
import ml_dtypes
from contextlib import ExitStack

import concourse.bass as bass
import concourse.mybir as mybir
from concourse.bass_utils import run_bass_kernel_spmd

F32 = mybir.dt.float32
BF16 = mybir.dt.bfloat16
AF = mybir.ActivationFunctionType
ALU = mybir.AluOpType
AX = mybir.AxisListType

NCORES = 8
D = 1024
C = 64
DFF = 2736
NE = 8
HME = 2048
GS = 512
EPS_LN = 1e-5
EPS_RMS = 1e-6
EPS_N = 1e-12
DFF_HALF_CHUNKS = [[128] * 11, [128] * 10 + [48]]
DFF_HALF_W = [sum(h) for h in DFF_HALF_CHUNKS]
DFF_HALF_OFF = [0, 1408]


class Sem:
    def __init__(self, stack, nc, name):
        self.h = stack.enter_context(nc.semaphore(name))
        self.n = 0


class _StopBuild(Exception):
    pass


SERIAL_B = False


def build_nc(TC=2048, stop_after=None):
    nt = TC // 128
    ng = TC // GS
    nch = TC // C
    NCHG = GS // C
    NTG = GS // 128
    n_th = 2 if TC >= 2048 else 1
    n_bh = 2 if TC >= 2048 else 1
    NG_BH_C = (TC // GS) // n_bh
    THS = TC // n_th
    nt_th = THS // 128
    ng_th = THS // GS

    nc = bass.Bass("TRN2", target_bir_lowering=False, debug=False)

    def din(name, shape, dt=F32):
        return nc.dram_tensor(name, shape, dt, kind="ExternalInput").ap()

    x_in = din("x", [TC, D])
    Wq = din("Wq", [D, D]); Wk = din("Wk", [D, D])
    Wv = din("Wv", [D, D]); Wg = din("Wg", [D, D])
    Wo = din("Wo", [D, D])
    w1 = din("w1", [D, DFF]); w2 = din("w2", [D, DFF]); w3 = din("w3", [DFF, D])
    g_attn = din("g_attn", [1, D]); g_ffn = din("g_ffn", [1, D])
    Wr = din("Wr", [D, NE]); br = din("br", [1, NE])
    W1 = din("W1", [NE, D, HME]); b1r = din("b1r", [NE, 128, HME // 128])
    W2 = din("W2", [NE, HME, D]); b2 = din("b2", [NE, D])
    ones_b = din("ones_b", [1, 128], BF16)
    ones_f = din("ones_f", [1, 128], F32)
    ones2_f = din("ones2_f", [128, 2], F32)
    bcast2_f = din("bcast2_f", [2, 128], F32)
    mask_tri = din("mask_tri", [128, C], F32)
    id_b = din("id_b", [128, 128], BF16)
    id_f = din("id_f", [128, 128], F32)
    rmask = din("rmask", [128, NCORES], F32)
    rmaskbar = din("rmaskbar", [128, NCORES], F32)

    out_t = nc.dram_tensor("out", [TC, D], F32, kind="ExternalOutput").ap()

    def dsc(name, shape, dt=F32):
        return nc.dram_tensor(name, shape, dt, kind="Internal").ap()

    sc_xn32 = dsc("sc_xn32", [TC, D])
    sc_xnb = dsc("sc_xnb", [TC, D], BF16)
    sc_xnbg = dsc("sc_xnbg", [TC, D], BF16)
    sc_rms = dsc("sc_rms", [1, TC])
    sc_qt = dsc("sc_qt", [8, 128, TC], BF16)
    sc_kt = dsc("sc_kt", [8, 128, TC], BF16)
    sc_v = dsc("sc_v", [nch * C, D], BF16)
    sc_sloc = dsc("sc_sloc", [128, 8 * nch * C], BF16)
    sc_s1 = dsc("sc_s1", [TC, D])
    sc_h2 = dsc("sc_h2", [TC, D], BF16)
    sc_g = dsc("sc_g", [TC, D])
    cc_inM = dsc("cc_inM", [8 * 128, C])
    cc_outM = dsc("cc_outM", [NCORES * 8 * 128, C])
    cc_inD = dsc("cc_inD", [128, 8])
    sc_gates = dsc("sc_gates", [1, NE * TC], BF16)
    cc_outD = dsc("cc_outD", [NCORES * 128, 8])

    with ExitStack() as st:
        scopes = {}

        def sbt(name, shape, dt=F32, scope=None, tdest=False):
            tgt = st if scope is None else scopes.setdefault(scope, ExitStack())
            return tgt.enter_context(
                nc.sbuf_tensor(name, shape, dt, is_transpose_dest=tdest))

        def close_scope(name):
            if name in scopes:
                scopes.pop(name).close()

        def pst(name, shape, dt=F32):
            return st.enter_context(nc.psum_tensor(name, shape, dt))

        onesb = sbt("onesb", [1, 128], BF16)
        onesf = sbt("onesf", [1, 128], F32)
        ones2f = sbt("ones2f", [128, 2], F32)
        bc2f = sbt("bc2f", [2, 128], F32)
        maskA = sbt("maskA", [128, C], F32)
        idb = sbt("idb", [128, 128], BF16)
        idf = sbt("idf", [128, 128], F32)
        rmk = sbt("rmk", [128, NCORES], F32)
        rmkb = sbt("rmkb", [128, NCORES], F32)
        garow = sbt("garow", [1, D], F32)
        gfrow = sbt("gfrow", [1, D], F32)
        gfbc = sbt("gfbc", [128, D], F32)
        brf = sbt("brf", [1, NE], F32)
        b2s = sbt("b2s", [NE, D], F32)
        rmsr = sbt("rmsr", [128, nt], F32)
        zeros64 = sbt("zeros64", [128, C], F32)
        constv = sbt("constv", [128, 4], F32)
        junk1 = sbt("junk1", [128, 1], F32)
        gam = sbt("gam", [128, 8 * nch], F32)

        stat = sbt("stat", [128, 8])
        rstd_c = sbt("rstd_c", [128, 1])





        P_A = pst("P_A", [128, 512])
        P_B = pst("P_B", [128, 512])
        P_P0 = pst("P_P0", [128, 512])
        P_P1 = pst("P_P1", [128, 512])
        P_N = pst("P_N", [128, 512])
        P_BC = pst("P_BC", [128, 512])
        P_TRB = pst("P_TRB", [128, 1024], BF16)
        P_TRF = P_TRB.ap().bitcast(F32)
        P_M = pst("P_M", [128, 512])

        LD = Sem(st, nc, "LD")
        GD = Sem(st, nc, "GD")
        GD1 = Sem(st, nc, "GD1")
        GD2 = Sem(st, nc, "GD2")
        PE = Sem(st, nc, "PE")
        DV = Sem(st, nc, "DV")
        AC = Sem(st, nc, "AC")
        CCS = Sem(st, nc, "CCS")
        OUT = Sem(st, nc, "OUT")
        BAR = Sem(st, nc, "BAR")

        bankfree = {}

        def bfree(name, sem):
            bankfree[name] = (sem.h, sem.n)

        def bwait(eng, *names):
            for nm in names:
                if nm in bankfree:
                    h, n = bankfree[nm]
                    eng.wait_ge(h, n)

        block = st.enter_context(nc.Block())
        nbar = [0]

        def maybe_stop(tag, dumps=()):
            if stop_after == tag:
                @block.gpsimd
                def _(gq):
                    gq.wait_ge(CCS.h, CCS.n)
                    gq.wait_ge(LD.h, LD.n)
                    gq.wait_ge(GD.h, GD.n)
                    for nm, src in dumps:
                        dt_ = src.dtype if hasattr(src, "dtype") else F32
                        dd = nc.dram_tensor("dbg_" + nm, list(src.shape), dt_,
                                            kind="ExternalOutput").ap()
                        gq.dma_start(out=dd, in_=src).then_inc(OUT.h, 16)
                        OUT.n += 16
                    gq.wait_ge(OUT.h, OUT.n)
                barrier()
                for k in list(scopes.keys())[::-1]:
                    scopes.pop(k).close()
                raise _StopBuild()

        def barrier():
            nbar[0] += 1
            tgt = 5 * nbar[0]
            for deco in (block.sync, block.gpsimd, block.tensor,
                         block.vector, block.scalar):
                @deco
                def _(q):
                    q.sem_inc(BAR.h, 1)
                    q.wait_ge(BAR.h, tgt)

        try:
            @block.sync
            def _(q):
                for dst, src in ((onesb, ones_b), (onesf, ones_f),
                                 (ones2f, ones2_f), (bc2f, bcast2_f),
                                 (maskA, mask_tri), (idb, id_b), (idf, id_f),
                                 (rmk, rmask), (rmkb, rmaskbar),
                                 (garow, g_attn), (gfrow, g_ffn), (brf, br),
                                 (b2s, b2)):
                    q.dma_start(out=dst[:], in_=src).then_inc(LD.h, 16)
                    LD.n += 16

            @block.vector
            def _(v):
                v.wait_ge(LD.h, LD.n)
                v.memset(zeros64[:], 0.0)
                v.memset(constv[:, 0:1], EPS_LN)
                v.memset(constv[:, 1:2], EPS_RMS)
                v.memset(constv[:, 2:3], -0.5)
                v.memset(constv[:, 3:4], 1.0 / D).then_inc(DV.h, 1)
                DV.n += 1

            barrier()

            gabc = sbt("gabc", [128, D], F32, scope="A1")
            xt = [sbt(f"xt{i}", [128, D], scope="A1") for i in range(2)]
            xnt = sbt("xnt", [128, D], scope="A1")
            xnbt = sbt("xnbt", [128, D], BF16, scope="A1")
            xnbgt = sbt("xnbgt", [128, D], BF16, scope="A1")
            sq = sbt("sq", [128, D], scope="A1")

            maybe_stop("p0")
            @block.tensor
            def _(t):
                t.wait_ge(LD.h, LD.n)
                t.matmul(P_A[:], onesf[:], garow[0:1, 0:512], start=True, stop=True)
                t.matmul(P_B[:], onesf[:], garow[0:1, 512:1024], start=True, stop=True)
                t.matmul(P_P0[:], onesf[:], gfrow[0:1, 0:512], start=True, stop=True)
                t.matmul(P_P1[:], onesf[:], gfrow[0:1, 512:1024],
                         start=True, stop=True).then_inc(PE.h, 1)
                PE.n += 1

            @block.vector
            def _(v):
                v.wait_ge(PE.h, PE.n)
                v.tensor_copy(gabc[:, 0:512], P_A[:])
                v.tensor_copy(gabc[:, 512:1024], P_B[:])
                v.tensor_copy(gfbc[:, 0:512], P_P0[:])
                v.tensor_copy(gfbc[:, 512:1024], P_P1[:]).then_inc(DV.h, 1)
                DV.n += 1

            x_ld = {}

            @block.sync
            def _(q):
                for t in range(min(2, nt)):
                    q.dma_start(out=xt[t % 2][:],
                                in_=x_in[t * 128:(t + 1) * 128, :]).then_inc(LD.h, 16)
                    LD.n += 16
                    x_ld[t] = LD.n

            dv_done = {}
            ac_done = {}
            spill_done = {}
            for t in range(nt):
                xb = xt[t % 2]

                @block.vector
                def _(v, t=t, xb=xb):
                    v.wait_ge(LD.h, LD.n)
                    v.reduce_sum(stat[:, 0:1], xb[:], axis=AX.X)
                    v.tensor_copy(zeros64[:], zeros64[:])
                    v.tensor_scalar_mul(stat[:, 1:2], stat[:, 0:1], 1.0 / D)
                    v.tensor_copy(zeros64[:], zeros64[:])
                    v.tensor_mul(stat[:, 2:3], stat[:, 1:2], stat[:, 1:2]
                                 ).then_inc(DV.h, 1)
                    DV.n += 1

                @block.scalar
                def _(s, t=t, xb=xb):
                    s.wait_ge(LD.h, LD.n)
                    s.activation(sq[:], xb[:], AF.Square,
                                 accum_out=stat[:, 3:4]).then_inc(AC.h, 1)
                    AC.n += 1

                @block.vector
                def _(v, t=t):
                    v.wait_ge(AC.h, AC.n)
                    v.tensor_scalar_mul(stat[:, 4:5], stat[:, 3:4], 1.0 / D)
                    v.tensor_copy(zeros64[:], zeros64[:])
                    v.tensor_sub(stat[:, 5:6], stat[:, 4:5], stat[:, 2:3]
                                 ).then_inc(DV.h, 1)
                    DV.n += 1

                @block.scalar
                def _(s, t=t):
                    s.wait_ge(DV.h, DV.n)
                    s.activation(stat[:, 6:7], stat[:, 5:6], AF.Ln, bias=constv[:, 0:1])
                    s.activation(rstd_c[:], stat[:, 6:7], AF.Exp,
                                 scale=constv[:, 2:3]).then_inc(AC.h, 1)
                    AC.n += 1

                @block.vector
                def _(v, t=t, xb=xb):
                    v.wait_ge(AC.h, AC.n)
                    if t >= 1:
                        v.wait_ge(LD.h, LD.n)
                    v.tensor_scalar(xnt[:], xb[:], stat[:, 1:2], rstd_c[:],
                                    ALU.subtract, ALU.mult)
                    v.tensor_tensor(xnbgt[:], xnt[:], gabc[:], ALU.mult
                                    ).then_inc(DV.h, 1)
                    DV.n += 1
                    dv_done[t] = DV.n

                @block.scalar
                def _(s, t=t):
                    s.wait_ge(DV.h, dv_done[t])
                    if t >= 1:
                        s.wait_ge(LD.h, LD.n)
                    s.copy(xnbt[:], xnt[:])
                    s.activation(sq[:], xnt[:], AF.Square, accum_out=stat[:, 7:8])
                    s.activation(stat[:, 6:7], stat[:, 7:8], AF.Ln,
                                 bias=constv[:, 1:2], scale=constv[:, 3:4])
                    s.activation(rmsr[:, t:t + 1], stat[:, 6:7], AF.Exp,
                                 scale=constv[:, 2:3]).then_inc(AC.h, 1)
                    AC.n += 1
                    ac_done[t] = AC.n

                @block.sync
                def _(q, t=t):
                    q.wait_ge(DV.h, dv_done[t])
                    q.dma_start(out=sc_xn32[t * 128:(t + 1) * 128, :],
                                in_=xnt[:]).then_inc(LD.h, 16)
                    q.dma_start(out=sc_xnbg[t * 128:(t + 1) * 128, :],
                                in_=xnbgt[:]).then_inc(LD.h, 16)
                    LD.n += 32
                    q.wait_ge(AC.h, ac_done[t])
                    q.dma_start(out=sc_xnb[t * 128:(t + 1) * 128, :],
                                in_=xnbt[:]).then_inc(LD.h, 16)
                    q.dma_start(out=sc_rms[0:1, t * 128:(t + 1) * 128]
                                .rearrange("a p -> p a"),
                                in_=rmsr[:, t:t + 1]).then_inc(LD.h, 16)
                    LD.n += 32
                    if t + 2 < nt:
                        q.dma_start(out=xt[t % 2][:],
                                    in_=x_in[(t + 2) * 128:(t + 3) * 128, :]
                                    ).then_inc(LD.h, 16)
                        LD.n += 16
                        x_ld[t + 2] = LD.n

                spill_done[t] = LD.n

            a1_spills_done = LD.n
            barrier()

            close_scope("A1")
            xn_fs = [sbt(f"xn_f{i}", [128, TC], BF16, scope="AB", tdest=True)
                 for i in range(8)]
            l_run = sbt("l_run", [128, 8 * C], F32, scope="A2")
            gam_run = sbt("gam_run", [128, 8], F32, scope="A2")
            s_local = sbt("s_local", [128, 8 * nch * C], BF16, scope="A2")
            wqb = sbt("wqb", [128, 8 * D], BF16, scope="A2")
            wkb = sbt("wkb", [128, 8 * D], BF16, scope="A2")
            wvb = sbt("wvb", [128, 8 * D], BF16, scope="A2")
            wgb = sbt("wgb", [128, 8 * D], BF16, scope="A2")
            rstd_row = sbt("rstd_row", [1, TC], F32, scope="A2")
            rstdb = sbt("rstdb", [128, GS], F32, scope="A2")
            a_oc = sbt("a_oc", [128, GS], F32, scope="A2")
            ra_oc = sbt("ra_oc", [128, GS], F32, scope="A2")
            az = sbt("az", [128, GS], F32, scope="A2")
            b64 = [sbt(f"b64_{i}", [128, C], scope="A2") for i in range(NCHG)]
            rb64 = [sbt(f"rb64_{i}", [128, C], scope="A2") for i in range(NCHG)]
            sqf = sbt("sqf", [128, GS], F32, scope="A2")
            nn2 = sbt("nn2", [2, GS], F32, scope="A2")
            invn2 = sbt("invn2", [2, GS], F32, scope="A2")
            invb = sbt("invb", [128, GS], F32, scope="A2")
            qn_oc = sbt("qn_oc", [128, GS], F32, scope="A2")
            qtb_oc = sbt("qtb_oc", [128, GS], BF16, scope="A2")
            kt32 = sbt("kt32", [128, C], F32, scope="A2")
            khb = sbt("khb", [128, C], BF16, scope="A2")
            khtok = sbt("khtok", [128, 128], BF16, scope="A2")
            vtile = [sbt(f"vtile{i}", [128, D], BF16, scope="A2") for i in range(NTG)]

            maybe_stop("a1", [("xn32", sc_xn32), ("xnb", sc_xnb), ("xnbg", sc_xnbg), ("rms", sc_rms)])
            @block.sync
            def _(q):
                q.wait_ge(LD.h, LD.n)
                for fc in range(8):
                    q.dma_start(out=xn_fs[fc][:],
                                in_=sc_xnbg[:, fc * 128:(fc + 1) * 128],
                                transpose=True).then_inc(LD.h, 16)
                    LD.n += 16
                q.dma_start(out=rstd_row[:], in_=sc_rms).then_inc(LD.h, 16)
                LD.n += 16

            @block.gpsimd
            def _(g):
                for wdst, wsrc in ((wqb, Wq), (wkb, Wk), (wvb, Wv), (wgb, Wg)):
                    for fc in range(8):
                        g.dma_start(out=wdst[:, fc * D:(fc + 1) * D],
                                    in_=wsrc[fc * 128:(fc + 1) * 128, :]
                                    ).then_inc(GD.h, 16)
                        GD.n += 16

            a2_ld = LD.n
            a2_gd = GD.n

            @block.vector
            def _(v):
                v.memset(l_run[:], 0.0)
                v.memset(gam_run[:], 1.0).then_inc(DV.h, 1)
                DV.n += 1
            maybe_stop("a2m")

            @block.sync
            def _(q):
                q.wait_ge(LD.h, LD.n)
            maybe_stop("a2t")

            @block.gpsimd
            def _(gq):
                gq.wait_ge(GD.h, GD.n)
            maybe_stop("a2w")

            def a2_proj(g, oc, PP, wsb, PPn):
                @block.tensor
                def _(t):
                    bwait(t, PPn)
                    last = None
                    for fc in range(8):
                        last = t.matmul(
                            PP[:],
                            wsb[:, fc * D + oc * 128: fc * D + (oc + 1) * 128],
                            xn_fs[fc][:, g * GS:(g + 1) * GS],
                            start=(fc == 0), stop=(fc == 7))
                    last.then_inc(PE.h, 1)
                    PE.n += 1

            def a2_l2norm(PP, PPn):
                @block.scalar
                def _(s):
                    s.wait_ge(PE.h, PE.n)
                    s.activation(sqf[:], PP[:], AF.Square).then_inc(AC.h, 1)
                    AC.n += 1

                @block.tensor
                def _(t):
                    t.wait_ge(AC.h, AC.n)
                    t.matmul(P_N[0:2, :], ones2f[:], sqf[:], start=True, stop=True
                             ).then_inc(PE.h, 1)
                    PE.n += 1

                @block.scalar
                def _(s):
                    s.wait_ge(PE.h, PE.n)
                    s.activation(nn2[:], P_N[0:2, :], AF.Sqrt).then_inc(AC.h, 1)
                    AC.n += 1

                @block.vector
                def _(v):
                    v.wait_ge(AC.h, AC.n)
                    v.tensor_scalar_max(nn2[:], nn2[:], EPS_N)
                    v.reciprocal(invn2[:], nn2[:]).then_inc(DV.h, 1)
                    DV.n += 1

                @block.tensor
                def _(t):
                    t.wait_ge(DV.h, DV.n)
                    t.matmul(P_BC[:], bc2f[:], invn2[:], start=True, stop=True
                             ).then_inc(PE.h, 1)
                    PE.n += 1

                @block.vector
                def _(v):
                    v.wait_ge(PE.h, PE.n)
                    v.tensor_copy(invb[:], P_BC[:])
                    v.tensor_tensor(qn_oc[:], PP[:], invb[:], ALU.mult
                                    ).then_inc(DV.h, 1)
                    DV.n += 1
                bfree(PPn, DV)

            vspill = {}
            kspill_ld = [LD.n]
            a2_ld_now = [LD.n]
            a2_gd_now = [GD.n]
            for g in range(ng):
                @block.tensor
                def _(t, g=g):
                    t.wait_ge(LD.h, a2_ld_now[0])
                    t.wait_ge(GD.h, a2_gd_now[0])
                    t.wait_ge(DV.h, DV.n)
                    t.matmul(P_BC[:], onesf[:], rstd_row[0:1, g * GS:(g + 1) * GS],
                             start=True, stop=True).then_inc(PE.h, 1)
                    PE.n += 1

                @block.vector
                def _(v, g=g):
                    v.wait_ge(PE.h, PE.n)
                    v.tensor_copy(rstdb[:], P_BC[:]).then_inc(DV.h, 1)
                    DV.n += 1
                if g == 0:
                    maybe_stop("a2r")

                for tl in range(NTG):
                    tt = g * NTG + tl

                    @block.tensor
                    def _(t, tt=tt):
                        bwait(t, "P_A", "P_B")
                        last = None
                        for fc in range(8):
                            lhs = xn_fs[fc][:, tt * 128:(tt + 1) * 128]
                            t.matmul(P_A[:], lhs, wvb[:, fc * D:fc * D + 512],
                                     start=(fc == 0), stop=(fc == 7))
                            last = t.matmul(P_B[:], lhs,
                                            wvb[:, fc * D + 512:(fc + 1) * D],
                                            start=(fc == 0), stop=(fc == 7))
                        last.then_inc(PE.h, 1)
                        PE.n += 1

                    if g == 0 and tl == 0:
                        maybe_stop("a2vm")

                    @block.vector
                    def _(v, tl=tl, tt=tt, g=g):
                        v.wait_ge(PE.h, PE.n)
                        if g > 0:
                            v.wait_ge(LD.h, LD.n)
                        v.tensor_scalar_mul(vtile[tl][:, 0:512], P_A[:],
                                            rmsr[:, tt:tt + 1])
                        v.tensor_scalar_mul(vtile[tl][:, 512:1024], P_B[:],
                                            rmsr[:, tt:tt + 1]).then_inc(DV.h, 1)
                        DV.n += 1
                    bfree("P_A", DV)
                    bfree("P_B", DV)

                    if g == 0 and tl == 0:
                        maybe_stop("a2ve")

                    @block.sync
                    def _(q, tl=tl, tt=tt):
                        q.wait_ge(DV.h, DV.n)
                        for half in range(2):
                            q.dma_start(out=sc_v[(tt * 2 + half) * C:
                                                 (tt * 2 + half + 1) * C, :],
                                        in_=vtile[tl][half * 64:(half + 1) * 64, :]
                                        ).then_inc(LD.h, 16)
                            LD.n += 16
                    vspill[tl] = LD.n
                if g == 0:
                    maybe_stop("a2v")

                for oc in range(8):
                    PP = P_P0 if oc % 2 == 0 else P_P1
                    PPn = 'P_P0' if oc % 2 == 0 else 'P_P1'

                    a2_proj(g, oc, PP, wgb, PPn)

                    @block.vector
                    def _(v, oc=oc, PP=PP):
                        v.wait_ge(PE.h, PE.n)
                        v.tensor_tensor(az[:], PP[:], rstdb[:], ALU.mult
                                        ).then_inc(DV.h, 1)
                        DV.n += 1
                    bfree(PPn, DV)

                    @block.scalar
                    def _(s, oc=oc):
                        s.wait_ge(DV.h, DV.n)
                        s.activation(a_oc[:], az[:], AF.Sigmoid).then_inc(AC.h, 1)
                        AC.n += 1

                    @block.vector
                    def _(v, oc=oc):
                        v.wait_ge(AC.h, AC.n)
                        v.reciprocal(ra_oc[:], a_oc[:])
                        last = None
                        for c in range(NCHG):
                            v.tensor_tensor_scan(b64[c][:], a_oc[:, c * C:(c + 1) * C],
                                                 zeros64[:], 1.0, ALU.mult, ALU.add)
                            last = v.tensor_tensor_scan(
                                rb64[c][:], ra_oc[:, c * C:(c + 1) * C],
                                zeros64[:], 1.0, ALU.mult, ALU.add)
                        last.then_inc(DV.h, 1)
                        DV.n += 1

                    if g == 0 and oc == 0:
                        maybe_stop("a2zg")
                    a2_proj(g, oc, PP, wqb, PPn)
                    a2_l2norm(PP, PPn)

                    @block.vector
                    def _(v, g=g, oc=oc):
                        v.wait_ge(LD.h, LD.n)
                        last = None
                        for c in range(NCHG):
                            last = v.tensor_tensor(qtb_oc[:, c * C:(c + 1) * C],
                                                   qn_oc[:, c * C:(c + 1) * C],
                                                   b64[c][:], ALU.mult)
                        last.then_inc(DV.h, 1)
                        DV.n += 1

                    @block.sync
                    def _(q, g=g, oc=oc):
                        q.wait_ge(DV.h, DV.n)
                        q.dma_start(out=sc_qt[oc, :, g * GS:(g + 1) * GS],
                                    in_=qtb_oc[:]).then_inc(LD.h, 16)
                        LD.n += 16
                    q_spill_ld = LD.n

                    if g == 0 and oc == 0:
                        maybe_stop("a2q")
                    a2_proj(g, oc, PP, wkb, PPn)
                    a2_l2norm(PP, PPn)

                    for c in range(NCHG):
                        n_glob = g * NCHG + c
                        bb = (c % 2) * 64

                        @block.vector
                        def _(v, g=g, oc=oc, c=c):
                            if c == 0:
                                v.wait_ge(LD.h, LD.n)
                            v.tensor_tensor(kt32[:], qn_oc[:, c * C:(c + 1) * C],
                                            rb64[c][:], ALU.mult)
                            v.tensor_copy(qtb_oc[:, c * C:(c + 1) * C], kt32[:])
                            v.tensor_scalar_mul(khb[:], kt32[:], b64[c][:, C - 1:C]
                                                ).then_inc(DV.h, 1)
                            DV.n += 1

                        @block.tensor
                        def _(t, c=c):
                            t.wait_ge(DV.h, DV.n)
                            t.transpose(P_TRB[0:64, 0:128], khb[:], idb[:]).then_inc(PE.h, 1)
                            PE.n += 1

                        @block.vector
                        def _(v, c=c, bb=bb):
                            v.wait_ge(PE.h, PE.n)
                            v.tensor_copy(khtok[bb:bb + 64, :], P_TRB[0:64, 0:128]
                                          ).then_inc(DV.h, 1)
                            DV.n += 1

                        @block.tensor
                        def _(t, c=c, bb=bb, oc=oc):
                            t.wait_ge(DV.h, DV.n)
                            vt = vtile[c // 2]
                            t.matmul(P_M[0:64, 0:64], khtok[bb:bb + 64, 0:64],
                                     vt[bb:bb + 64, oc * 128:oc * 128 + 64],
                                     start=True, stop=True, tile_position=(bb, 0))
                            t.matmul(P_BC[64:128, 0:64], khtok[bb:bb + 64, 64:128],
                                     vt[bb:bb + 64, oc * 128 + 64:(oc + 1) * 128],
                                     start=True, stop=True, tile_position=(bb, 64)
                                     ).then_inc(PE.h, 1)
                            PE.n += 1

                        @block.vector
                        def _(v, oc=oc, c=c, n_glob=n_glob):
                            v.wait_ge(PE.h, PE.n)
                            v.tensor_copy(
                                s_local[:, (oc * nch + n_glob) * C:
                                        (oc * nch + n_glob + 1) * C],
                                l_run[:, oc * C:(oc + 1) * C])
                            v.tensor_scalar_mul(l_run[:, oc * C:(oc + 1) * C],
                                                l_run[:, oc * C:(oc + 1) * C],
                                                b64[c][:, C - 1:C])
                            v.tensor_copy(gam[:, oc * nch + n_glob:
                                              oc * nch + n_glob + 1],
                                          gam_run[:, oc:oc + 1])
                            v.tensor_tensor(l_run[0:64, oc * C:(oc + 1) * C],
                                            P_M[0:64, 0:64],
                                            l_run[0:64, oc * C:(oc + 1) * C],
                                            ALU.add)
                            v.tensor_tensor(l_run[64:128, oc * C:(oc + 1) * C],
                                            P_BC[64:128, 0:64],
                                            l_run[64:128, oc * C:(oc + 1) * C],
                                            ALU.add)
                            v.tensor_scalar_mul(gam_run[:, oc:oc + 1],
                                                gam_run[:, oc:oc + 1],
                                                b64[c][:, C - 1:C]).then_inc(DV.h, 1)
                            DV.n += 1

                    @block.sync
                    def _(q, g=g, oc=oc):
                        q.wait_ge(DV.h, DV.n)
                        q.dma_start(out=sc_kt[oc, :, g * GS:(g + 1) * GS],
                                    in_=qtb_oc[:]).then_inc(LD.h, 16)
                        LD.n += 16
                    kspill_ld[0] = LD.n
                    if g == 0 and oc == 0:
                        maybe_stop("a2k")

            @block.sync
            def _(q):
                q.wait_ge(DV.h, DV.n)
                q.dma_start(out=sc_sloc, in_=s_local[:]).then_inc(LD.h, 16)
                LD.n += 16
            a2_spills_done = LD.n

            barrier()

            maybe_stop("a2", [("qt", sc_qt), ("kt", sc_kt), ("v", sc_v), ("sloc", sc_sloc), ("lrun", l_run[:]), ("gam", gam[:]), ("gamrun", gam_run[:])])
            @block.gpsimd
            def _(gq):
                for p in range(8):
                    gq.dma_start(out=cc_inM[p * 128:(p + 1) * 128, :],
                                 in_=l_run[:, p * C:(p + 1) * C]).then_inc(GD.h, 16)
                    GD.n += 16
                gq.dma_start(out=cc_inD, in_=gam_run[:]).then_inc(GD.h, 16)
                GD.n += 16
                gq.wait_ge(GD.h, GD.n)
                gq.collective_compute(
                    "AllGather", ALU.bypass,
                    replica_groups=[list(range(NCORES))],
                    ins=[cc_inM.opt()], outs=[cc_outM.opt()],
                ).then_inc(CCS.h, 1)
                gq.collective_compute(
                    "AllGather", ALU.bypass,
                    replica_groups=[list(range(NCORES))],
                    ins=[cc_inD.opt()], outs=[cc_outD.opt()],
                ).then_inc(CCS.h, 1)
                CCS.n += 2

            close_scope("A2")
            gates_rows = sbt("gates_rows", [NE, TC], F32, scope="B")
            gates_rows_b = sbt("gates_rows_b", [NE, TC], BF16, scope="B")
            gflbuf = sbt("gflbuf", [1, GS], BF16, scope="B")
            wrb = sbt("wrb", [128, 8 * NE], BF16, scope="B")
            lg = sbt("lg", [128, 2 * NE], scope="B")
            srt = sbt("srt", [128, 2 * NE], scope="B")
            sel = sbt("sel", [128, 2 * NE], scope="B")
            ega = sbt("ega", [128, 2 * NE], scope="B")
            den2 = sbt("den2", [128, 2], scope="B")
            di2 = sbt("di2", [128, 2], scope="B")
            gat2 = sbt("gat2", [128, 2 * NE], scope="B")
            w1b = sbt("w1b", [128, 8 * HME], BF16, scope="B")
            w2b = sbt("w2b", [128, 16 * D], BF16, scope="B")
            b1s = sbt("b1s", [128, NE * (HME // 128)], F32, scope="B")
            gbc = sbt("gbc", [128, GS], BF16, scope="B")
            htmp = [sbt(f"htmp{i}", [128, GS], F32, scope="B") for i in range(2)]
            hhat = sbt("hhat", [128, NG_BH_C * 16 * GS], BF16, scope="B")
            macc = sbt("macc", [128, (nt // n_bh) * D], F32, scope="B")
            silt = [sbt(f"silt{i}", [128, D], F32, scope="B") for i in range(2)]

            maybe_stop("x", [("ccM", cc_outM), ("ccD", cc_outD)])
            @block.sync
            def _(q):
                for fc in range(8):
                    q.dma_start(out=xn_fs[fc][:],
                                in_=sc_xnb[:, fc * 128:(fc + 1) * 128],
                                transpose=True).then_inc(LD.h, 16)
                    LD.n += 16

            @block.gpsimd
            def _(gq):
                for fc in range(8):
                    gq.dma_start(out=wrb[:, fc * NE:(fc + 1) * NE],
                                 in_=Wr[fc * 128:(fc + 1) * 128, :]).then_inc(GD.h, 16)
                    GD.n += 16
                for e in range(NE):
                    gq.dma_start(out=b1s[:, e * 16:(e + 1) * 16],
                                 in_=b1r[e]).then_inc(GD.h, 16)
                    GD.n += 16

            b_ld = LD.n
            b_gd = GD.n

            for tp in range(0, nt, 2):
                @block.tensor
                def _(t, tp=tp):
                    t.wait_ge(LD.h, LD.n)
                    t.wait_ge(GD.h, GD.n)
                    t.wait_ge(DV.h, DV.n)
                    last = None
                    for j in range(2):
                        tt = tp + j
                        PP = P_P0 if j == 0 else P_P1
                        for fc in range(8):
                            t.matmul(PP[:, 0:NE],
                                     xn_fs[fc][:, tt * 128:(tt + 1) * 128],
                                     wrb[:, fc * NE:(fc + 1) * NE],
                                     start=(fc == 0), stop=False)
                        last = t.matmul(PP[:, 0:NE], onesf[:], brf[:],
                                        start=False, stop=True)
                    last.then_inc(PE.h, 1)
                    PE.n += 1

                @block.vector
                def _(v, tp=tp):
                    v.wait_ge(PE.h, PE.n)
                    v.tensor_copy(lg[:, 0:NE], P_P0[:, 0:NE])
                    v.tensor_copy(lg[:, NE:2 * NE], P_P1[:, 0:NE])
                    v.tensor_copy(zeros64[:], zeros64[:])
                    v.tensor_copy(zeros64[:], zeros64[:])
                    v.max(srt[:, 0:NE], lg[:, 0:NE])
                    v.max(srt[:, NE:2 * NE], lg[:, NE:2 * NE])
                    v.tensor_copy(zeros64[:], zeros64[:])
                    v.tensor_copy(zeros64[:], zeros64[:])
                    v.tensor_scalar(sel[:, 0:NE], lg[:, 0:NE], srt[:, 1:2], 0.0,
                                    ALU.is_ge, ALU.add)
                    v.tensor_scalar(sel[:, NE:2 * NE], lg[:, NE:2 * NE],
                                    srt[:, NE + 1:NE + 2], 0.0, ALU.is_ge, ALU.add)
                    v.tensor_scalar(ega[:, 0:NE], lg[:, 0:NE], srt[:, 0:1], 0.0,
                                    ALU.subtract, ALU.add)
                    v.tensor_scalar(ega[:, NE:2 * NE], lg[:, NE:2 * NE],
                                    srt[:, NE:NE + 1], 0.0, ALU.subtract, ALU.add
                                    ).then_inc(DV.h, 1)
                    DV.n += 1

                @block.scalar
                def _(s, tp=tp):
                    s.wait_ge(DV.h, DV.n)
                    s.activation(ega[:, 0:2 * NE], ega[:, 0:2 * NE], AF.Exp
                                 ).then_inc(AC.h, 1)
                    AC.n += 1

                @block.vector
                def _(v, tp=tp):
                    v.wait_ge(AC.h, AC.n)
                    v.tensor_mul(ega[:, 0:2 * NE], ega[:, 0:2 * NE],
                                 sel[:, 0:2 * NE])
                    v.tensor_copy(zeros64[:], zeros64[:])
                    v.reduce_sum(den2[:, 0:1], ega[:, 0:NE], axis=AX.X)
                    v.reduce_sum(den2[:, 1:2], ega[:, NE:2 * NE], axis=AX.X)
                    v.tensor_copy(zeros64[:], zeros64[:])
                    v.tensor_copy(zeros64[:], zeros64[:])
                    v.reciprocal(di2[:], den2[:])
                    v.tensor_copy(zeros64[:], zeros64[:])
                    v.tensor_copy(zeros64[:], zeros64[:])
                    v.tensor_copy(junk1[:], di2[:, 0:1])
                    v.tensor_scalar_mul(gat2[:, 0:NE], ega[:, 0:NE], di2[:, 0:1])
                    v.tensor_scalar_mul(gat2[:, NE:2 * NE], ega[:, NE:2 * NE],
                                        di2[:, 1:2]).then_inc(DV.h, 1)
                    DV.n += 1

                @block.tensor
                def _(t, tp=tp):
                    t.wait_ge(DV.h, DV.n)
                    t.transpose(P_BC[0:NE, 0:128], gat2[:, 0:NE], idf[:])
                    t.transpose(P_A[0:NE, 0:128], gat2[:, NE:2 * NE], idf[:]
                                ).then_inc(PE.h, 1)
                    PE.n += 1

                @block.vector
                def _(v, tp=tp):
                    v.wait_ge(PE.h, PE.n)
                    v.tensor_copy(gates_rows[:, tp * 128:(tp + 1) * 128],
                                  P_BC[0:NE, 0:128])
                    v.tensor_copy(gates_rows[:, (tp + 1) * 128:(tp + 2) * 128],
                                  P_A[0:NE, 0:128])
                    v.tensor_copy(gates_rows_b[:, tp * 128:(tp + 1) * 128],
                                  P_BC[0:NE, 0:128])
                    v.tensor_copy(gates_rows_b[:, (tp + 1) * 128:(tp + 2) * 128],
                                  P_A[0:NE, 0:128]).then_inc(DV.h, 1)
                    DV.n += 1

            @block.sync
            def _(q):
                q.wait_ge(DV.h, DV.n)
                q.dma_start(out=sc_gates.rearrange("a (e t) -> (a e) t", e=NE),
                            in_=gates_rows_b[:]).then_inc(LD.h, 16)
                LD.n += 16
            gfl_ld = LD.n

            NT_BH = nt // n_bh
            NG_BH = ng // n_bh
            silt_sp = [LD.n]
            pe_h_done = [PE.n]
            pe_w2_done = [PE.n]

            @block.gpsimd
            def _(gq):
                for fc in range(8):
                    gq.dma_start(out=w1b[:, fc * HME:(fc + 1) * HME],
                                 in_=W1[0, fc * 128:(fc + 1) * 128, :]
                                 ).then_inc(GD1.h, 16)
                    GD1.n += 16

            eseq = [(bh, e) for bh in range(n_bh) for e in range(NE)]
            for idx, (bh, e) in enumerate(eseq):
                @block.gpsimd
                def _(gq, bh=bh, e=e):
                    gq.wait_ge(PE.h, PE.n if SERIAL_B else pe_w2_done[0])
                    for hc in range(16):
                        gq.dma_start(out=w2b[:, hc * D:(hc + 1) * D],
                                     in_=W2[e, hc * 128:(hc + 1) * 128, :]
                                     ).then_inc(GD2.h, 16)
                        GD2.n += 16

                for gl in range(NG_BH):
                    g = bh * NG_BH + gl

                    @block.sync
                    def _(q, e=e, g=g):
                        q.wait_ge(PE.h, PE.n)
                        q.wait_ge(LD.h, LD.n)
                        q.dma_start(
                            out=gflbuf[:],
                            in_=sc_gates[0:1, e * TC + g * GS:
                                         e * TC + (g + 1) * GS]
                            ).then_inc(LD.h, 16)
                        LD.n += 16

                    @block.tensor
                    def _(t, e=e, g=g):
                        t.wait_ge(GD1.h, GD1.n)
                        t.wait_ge(LD.h, LD.n)
                        bwait(t, "P_BC")
                        t.matmul(P_BC[:], onesb[:], gflbuf[:],
                                 start=True, stop=True).then_inc(PE.h, 1)
                        PE.n += 1

                    @block.vector
                    def _(v):
                        v.wait_ge(PE.h, PE.n)
                        v.tensor_copy(gbc[:], P_BC[:]).then_inc(DV.h, 1)
                        DV.n += 1
                    bfree("P_BC", DV)

                    for hc in range(16):
                        PPn = "P_A" if hc % 2 == 0 else "P_B"
                        PP = P_A if hc % 2 == 0 else P_B
                        HT = htmp[hc % 2]

                        @block.tensor
                        def _(t, e=e, g=g, hc=hc, PP=PP, PPn=PPn):
                            bwait(t, PPn)
                            last = None
                            for fc in range(8):
                                last = t.matmul(
                                    PP[:],
                                    w1b[:, fc * HME + hc * 128:
                                         fc * HME + (hc + 1) * 128],
                                    xn_fs[fc][:, g * GS:(g + 1) * GS],
                                    start=(fc == 0), stop=(fc == 7))
                            last.then_inc(PE.h, 1)
                            PE.n += 1

                        @block.vector
                        def _(v, e=e, gl=gl, hc=hc, PP=PP, HT=HT):
                            v.wait_ge(PE.h, PE.n)
                            bwait(v, "HT%d" % (hc % 2))
                            v.tensor_scalar_add(HT[:], PP[:],
                                                b1s[:, e * 16 + hc:
                                                    e * 16 + hc + 1])
                            v.scalar_tensor_tensor(
                                hhat[:, (gl * 16 + hc) * GS:
                                     (gl * 16 + hc + 1) * GS],
                                HT[:], 0.0, gbc[:],
                                ALU.max, ALU.mult).then_inc(DV.h, 1)
                            DV.n += 1
                        bfree(PPn, DV)
                        bfree("HT%d" % (hc % 2), DV)
                pe_h_done[0] = PE.n
                h_done_dv = DV.n

                if idx + 1 < len(eseq):
                    bhn, en = eseq[idx + 1]

                    @block.gpsimd
                    def _(gq, en=en):
                        gq.wait_ge(PE.h, PE.n if SERIAL_B else pe_h_done[0])
                        for fc in range(8):
                            gq.dma_start(out=w1b[:, fc * HME:(fc + 1) * HME],
                                         in_=W1[en, fc * 128:(fc + 1) * 128, :]
                                         ).then_inc(GD1.h, 16)
                            GD1.n += 16

                for gl in range(NG_BH):
                    g = bh * NG_BH + gl
                    for tl in range(NTG):
                        tt = g * NTG + tl
                        ttl = tt - bh * NT_BH
                        par = (gl * NTG + tl) % 2
                        Q0, Q1 = (P_P0, P_P1) if par == 0 else (P_M, P_BC)
                        Q0n, Q1n = ("P_P0", "P_P1") if par == 0 else ("P_M", "P_BC")

                        @block.tensor
                        def _(t, bh=bh, e=e, g=g, gl=gl, tl=tl, tt=tt,
                              Q0=Q0, Q1=Q1, Q0n=Q0n, Q1n=Q1n,
                              h_done_dv=h_done_dv):
                            t.wait_ge(GD2.h, GD2.n)
                            t.wait_ge(DV.h, h_done_dv)
                            bwait(t, Q0n, Q1n)
                            last_is_b2 = (e == 0)
                            last = None
                            for hc in range(16):
                                lhs = hhat[:, (gl * 16 + hc) * GS + tl * 128:
                                           (gl * 16 + hc) * GS + (tl + 1) * 128]
                                stp = (hc == 15) and not last_is_b2
                                t.matmul(Q0[:], lhs,
                                         w2b[:, hc * D:hc * D + 512],
                                         start=(hc == 0), stop=stp)
                                last = t.matmul(Q1[:], lhs,
                                                w2b[:, hc * D + 512:(hc + 1) * D],
                                                start=(hc == 0), stop=stp)
                            if last_is_b2:
                                t.matmul(Q0[:],
                                         gates_rows[0:NE, tt * 128:(tt + 1) * 128],
                                         b2s[:, 0:512], start=False, stop=True)
                                last = t.matmul(
                                    Q1[:],
                                    gates_rows[0:NE, tt * 128:(tt + 1) * 128],
                                    b2s[:, 512:1024], start=False, stop=True)
                            last.then_inc(PE.h, 1)
                            PE.n += 1

                        @block.vector
                        def _(v, e=e, ttl=ttl, Q0=Q0, Q1=Q1):
                            v.wait_ge(PE.h, PE.n)
                            if e == 0:
                                v.tensor_copy(macc[:, ttl * D:ttl * D + 512],
                                              Q0[:])
                                v.tensor_copy(macc[:, ttl * D + 512:(ttl + 1) * D],
                                              Q1[:]).then_inc(DV.h, 1)
                            else:
                                v.tensor_tensor(macc[:, ttl * D:ttl * D + 512],
                                                Q0[:],
                                                macc[:, ttl * D:ttl * D + 512],
                                                ALU.add)
                                v.tensor_tensor(
                                    macc[:, ttl * D + 512:(ttl + 1) * D],
                                    Q1[:],
                                    macc[:, ttl * D + 512:(ttl + 1) * D],
                                    ALU.add).then_inc(DV.h, 1)
                            DV.n += 1
                        bfree(Q0n, DV)
                        bfree(Q1n, DV)
                pe_w2_done[0] = PE.n


                if e == NE - 1:
                    silt_ld = {}
                    for ttl in range(NT_BH):
                        tt = bh * NT_BH + ttl
                        ld_need = silt_ld.get(ttl - 2, LD.n)

                        @block.scalar
                        def _(s2, ttl=ttl, tt=tt, ld_need=ld_need):
                            s2.wait_ge(DV.h, DV.n)
                            s2.wait_ge(LD.h, ld_need)
                            s2.activation(silt[ttl % 2][:],
                                          macc[:, ttl * D:(ttl + 1) * D],
                                          AF.Silu).then_inc(AC.h, 1)
                            AC.n += 1

                        @block.sync
                        def _(q, tt=tt, ttl=ttl):
                            q.wait_ge(AC.h, AC.n)
                            q.dma_start(out=sc_g[tt * 128:(tt + 1) * 128, :],
                                        in_=silt[ttl % 2][:]).then_inc(LD.h, 16)
                            LD.n += 16
                        silt_ld[ttl] = LD.n
                    silt_sp[0] = LD.n
            b_spills_done = LD.n

            barrier()

            close_scope("B")
            close_scope("AB")
            s_start = sbt("s_start", [128, 8 * C], F32, scope="C")
            d_r = sbt("d_r", [128, NE], F32, scope="C")
            m_r = sbt("m_r", [128, 8 * C], F32, scope="C")
            dadj = sbt("dadj", [128, NE], F32, scope="C")
            qtf = sbt("qtf", [128, 8 * TC], BF16, scope="C")
            sloc2 = sbt("sloc2", [128, 8 * nch * C], BF16, scope="C")
            ktf = sbt("ktf", [128, 8 * TC], BF16, scope="C")
            wob = sbt("wob", [128, 8 * D], BF16, scope="C")
            vA = [sbt(f"vA{i}", [128, D], BF16, scope="C") for i in range(NTG)]
            vB = [sbt(f"vB{i}", [128, D], BF16, scope="C") for i in range(NTG)]
            stmp = sbt("stmp", [128, C], F32, scope="C")
            sl32 = sbt("sl32", [128, C], F32, scope="C")
            stb = [sbt(f"stb{i}", [128, C], BF16, scope="C") for i in range(2)]
            at_sb = sbt("at_sb", [128, C], BF16, scope="C")
            o_sb = sbt("o_sb", [128, 8 * GS], BF16, scope="C")
            xn32t = sbt("xn32t", [128, D], F32, scope="C")
            s1t = sbt("s1t", [128, D], F32, scope="C")

            maybe_stop("b", [("g", sc_g), ("gates", sc_gates)])
            c_ld_pre = LD.n

            @block.sync
            def _(q):
                q.wait_ge(LD.h, c_ld_pre)
                q.wait_ge(CCS.h, CCS.n)
                for oc in range(8):
                    q.dma_start(out=qtf[:, oc * TC:(oc + 1) * TC],
                                in_=sc_qt[oc]).then_inc(LD.h, 16)
                    LD.n += 16
                q.dma_start(out=sloc2[:], in_=sc_sloc).then_inc(LD.h, 16)
                LD.n += 16

            @block.scalar
            def _(s):
                s.wait_ge(LD.h, c_ld_pre)
                for oc in range(8):
                    s.dma_start(out=ktf[:, oc * TC:(oc + 1) * TC],
                                in_=sc_kt[oc]).then_inc(LD.h, 16)
                    LD.n += 16

            @block.gpsimd
            def _(gq):
                for p in range(8):
                    gq.dma_start(out=wob[:, p * D:(p + 1) * D],
                                 in_=Wo[p * 128:(p + 1) * 128, :]).then_inc(GD.h, 16)
                    GD.n += 16
            c_gd = GD.n

            @block.vector
            def _(v):
                v.wait_ge(LD.h, LD.n)
                v.memset(s_start[:], 0.0).then_inc(DV.h, 1)
                DV.n += 1

            for r in range(NCORES):
                @block.sync
                def _(q, r=r):
                    q.wait_ge(DV.h, DV.n)
                    q.dma_start(out=d_r[:, 0:8],
                                in_=cc_outD[r * 128:(r + 1) * 128, :]
                                ).then_inc(LD.h, 16)
                    LD.n += 16
                    for p in range(8):
                        q.dma_start(out=m_r[:, p * C:(p + 1) * C],
                                    in_=cc_outM[(r * 8 + p) * 128:
                                                (r * 8 + p + 1) * 128, :]
                                    ).then_inc(LD.h, 16)
                        LD.n += 16

                @block.vector
                def _(v, r=r):
                    v.wait_ge(LD.h, LD.n)
                    v.tensor_scalar(dadj[:], d_r[:], rmk[:, r:r + 1],
                                    rmkb[:, r:r + 1], ALU.mult, ALU.add)
                    v.tensor_scalar_mul(m_r[:], m_r[:], rmk[:, r:r + 1])
                    v.tensor_copy(zeros64[:], zeros64[:])
                    v.tensor_copy(zeros64[:], zeros64[:])
                    for p in range(8):
                        v.tensor_scalar_mul(s_start[:, p * C:(p + 1) * C],
                                            s_start[:, p * C:(p + 1) * C],
                                            dadj[:, p:p + 1])
                        v.tensor_tensor(s_start[:, p * C:(p + 1) * C],
                                        m_r[:, p * C:(p + 1) * C],
                                        s_start[:, p * C:(p + 1) * C], ALU.add)
                    v.tensor_copy(junk1[:], s_start[:, 0:1]).then_inc(DV.h, 1)
                    DV.n += 1

            s1t_sp = [LD.n]
            pend_evac = [None]
            for g in range(ng):
                c_ld_now = [0]
                @block.sync
                def _(q, g=g):
                    q.wait_ge(PE.h, PE.n)
                    q.wait_ge(LD.h, LD.n)
                    for m in range(NTG):
                        n0 = g * NCHG + 2 * m
                        q.dma_start(out=vA[m][:],
                                    in_=sc_v[n0 * C:(n0 + 2) * C, :]
                                    ).then_inc(LD.h, 16)
                        q.dma_start(out=vB[m][0:64, :],
                                    in_=sc_v[(n0 + 1) * C:(n0 + 2) * C, :]
                                    ).then_inc(LD.h, 16)
                        q.dma_start(out=vB[m][64:128, :],
                                    in_=sc_v[n0 * C:(n0 + 1) * C, :]
                                    ).then_inc(LD.h, 16)
                        LD.n += 48
                c_ld_now[0] = LD.n

                for n in range(NCHG):
                    n_glob = g * NCHG + n
                    m = n // 2
                    bb = (n % 2) * 64
                    t0 = n_glob * C

                    for p in range(8):
                        ip = (n * 8 + p) % 2
                        PO0, PO1 = ((P_P0, P_P1) if ip == 0 else (P_N, P_TRF))
                        PO0n, PO1n = (("P_P0", "P_P1") if ip == 0
                                      else ("P_N", "P_TRF"))
                        STB = stb[ip]

                        @block.vector
                        def _(v, p=p, n_glob=n_glob, STB=STB, ip=ip):
                            bwait(v, "STB%d" % ip)
                            v.scalar_tensor_tensor(
                                STB[:], s_start[:, p * C:(p + 1) * C],
                                gam[:, p * nch + n_glob:
                                    p * nch + n_glob + 1],
                                sloc2[:, (p * nch + n_glob) * C:
                                      (p * nch + n_glob + 1) * C],
                                ALU.mult, ALU.add).then_inc(DV.h, 1)
                            DV.n += 1

                        @block.tensor
                        def _(t, p=p, t0=t0, g=g):
                            t.wait_ge(LD.h, c_ld_now[0])
                            t.wait_ge(GD.h, GD.n)
                            bwait(t, "P_M", "P_BC")
                            t.matmul(P_M[0:64, 0:64],
                                     ktf[0:64, p * TC + t0:p * TC + t0 + C],
                                     qtf[0:64, p * TC + t0:p * TC + t0 + C],
                                     start=True, stop=True, tile_position=(0, 0))
                            t.matmul(P_BC[64:128, 0:64],
                                     ktf[64:128, p * TC + t0:p * TC + t0 + C],
                                     qtf[64:128, p * TC + t0:p * TC + t0 + C],
                                     start=True, stop=True, tile_position=(64, 64)
                                     ).then_inc(PE.h, 1)
                            PE.n += 1

                        @block.vector
                        def _(v, p=p):
                            v.wait_ge(PE.h, PE.n)
                            v.tensor_tensor(at_sb[0:64, :], P_M[0:64, 0:64],
                                            maskA[0:64, :], ALU.mult)
                            v.tensor_tensor(at_sb[64:128, :], P_BC[64:128, 0:64],
                                            maskA[64:128, :], ALU.mult
                                            ).then_inc(DV.h, 1)
                            DV.n += 1
                        bfree("P_M", DV)
                        bfree("P_BC", DV)

                        dv_mask = DV.n
                        if pend_evac[0] is not None:
                            pend_evac[0]()
                            pend_evac[0] = None

                        @block.tensor
                        def _(t, p=p, n=n, m=m, bb=bb, t0=t0, PO0=PO0,
                              PO1=PO1, PO0n=PO0n, PO1n=PO1n, STB=STB,
                              dv_mask=dv_mask):
                            t.wait_ge(DV.h, dv_mask)
                            bwait(t, PO0n, PO1n)
                            t.matmul(PO0[0:64, 0:64], STB[0:64, :],
                                     qtf[0:64, p * TC + t0:p * TC + t0 + C],
                                     start=True, stop=False, tile_position=(0, 0))
                            t.matmul(PO1[64:128, 0:64],
                                     STB[64:128, :],
                                     qtf[64:128, p * TC + t0:p * TC + t0 + C],
                                     start=True, stop=False, tile_position=(64, 64))
                            va = vA[m] if n % 2 == 0 else vB[m]
                            vb_ = vB[m] if n % 2 == 0 else vA[m]
                            t.matmul(PO0[0:64, 0:64], va[0:64, p * 128:p * 128 + 64],
                                     at_sb[0:64, :], start=False, stop=True,
                                     tile_position=(0, 0))
                            t.matmul(PO1[64:128, 0:64],
                                     vb_[64:128, p * 128 + 64:(p + 1) * 128],
                                     at_sb[64:128, :], start=False, stop=True,
                                     tile_position=(64, 64)).then_inc(PE.h, 1)
                            PE.n += 1
                        bfree("STB%d" % ip, PE)

                        def mk_evac(p=p, n=n, PO0=PO0, PO1=PO1,
                                    PO0n=PO0n, PO1n=PO1n):
                            def emit():
                                @block.vector
                                def _(v, p=p, n=n, PO0=PO0, PO1=PO1):
                                    v.wait_ge(PE.h, PE.n)
                                    v.tensor_copy(
                                        o_sb[0:64,
                                             p * GS + n * C:p * GS + (n + 1) * C],
                                        PO0[0:64, 0:64])
                                    v.tensor_copy(
                                        o_sb[64:128,
                                             p * GS + n * C:p * GS + (n + 1) * C],
                                        PO1[64:128, 0:64]).then_inc(DV.h, 1)
                                    DV.n += 1
                                bfree(PO0n, DV)
                                bfree(PO1n, DV)
                            return emit
                        pend_evac[0] = mk_evac()

                if pend_evac[0] is not None:
                    pend_evac[0]()
                    pend_evac[0] = None

                for tl in range(NTG):
                    tt = g * NTG + tl

                    @block.sync
                    def _(q, tt=tt):
                        q.wait_ge(DV.h, DV.n)
                        q.dma_start(out=xn32t[:],
                                    in_=sc_xn32[tt * 128:(tt + 1) * 128, :]
                                    ).then_inc(LD.h, 16)
                        LD.n += 16
                    xn_ld = LD.n

                    @block.tensor
                    def _(t, tl=tl):
                        t.wait_ge(DV.h, DV.n)
                        t.wait_ge(AC.h, AC.n)
                        last = None
                        for p in range(8):
                            lhs = o_sb[:, p * GS + tl * 128:p * GS + (tl + 1) * 128]
                            t.matmul(P_A[:], lhs, wob[:, p * D:p * D + 512],
                                     start=(p == 0), stop=(p == 7))
                            last = t.matmul(P_B[:], lhs,
                                            wob[:, p * D + 512:(p + 1) * D],
                                            start=(p == 0), stop=(p == 7))
                        last.then_inc(PE.h, 1)
                        PE.n += 1

                    @block.vector
                    def _(v, tt=tt, xn_ld=xn_ld):
                        v.wait_ge(PE.h, PE.n)
                        v.wait_ge(LD.h, LD.n)
                        v.tensor_tensor(s1t[:, 0:512], P_A[:], xn32t[:, 0:512],
                                        ALU.add)
                        v.tensor_tensor(s1t[:, 512:1024], P_B[:],
                                        xn32t[:, 512:1024], ALU.add
                                        ).then_inc(DV.h, 1)
                        DV.n += 1

                    @block.sync
                    def _(q, tt=tt):
                        q.wait_ge(DV.h, DV.n)
                        q.dma_start(out=sc_s1[tt * 128:(tt + 1) * 128, :],
                                    in_=s1t[:]).then_inc(LD.h, 16)
                        LD.n += 16
                    s1t_sp[0] = LD.n
            c_spills_done = LD.n

            barrier()

            close_scope("C")
            s1_sb = sbt("s1_sb", [128, nt_th * D], F32, scope="D")
            h2fs = [sbt(f"h2f{i}", [128, THS], BF16, scope="D", tdest=True)
                 for i in range(8)]
            w1d = sbt("w1d", [128, 8 * 1408], BF16, scope="D")
            w2d = sbt("w2d", [128, 8 * 1408], BF16, scope="D")
            w3d = sbt("w3d", [128, 11 * D], BF16, scope="D")
            zbuf = sbt("zbuf", [128, 11 * GS], BF16, scope="D")
            zsil = [sbt(f"zsil{i}", [128, GS], F32, scope="D") for i in range(2)]
            rstd2c = sbt("rstd2c", [128, 1], F32, scope="D")
            h2bt = sbt("h2bt", [128, D], BF16, scope="D")
            gxt = sbt("gxt", [128, D], F32, scope="D")
            xrt = sbt("xrt", [128, D], F32, scope="D")
            rt = sbt("rt", [128, D], F32, scope="D")

            maybe_stop("c", [("s1", sc_s1)])
            h2sp = [LD.n]
            for th in range(n_th):
                toff = th * THS

                for tl in range(nt_th):
                    tt = toff // 128 + tl

                    @block.sync
                    def _(q, th=th, tl=tl, tt=tt):
                        if th == 0 and tl == 0:
                            q.wait_ge(LD.h, LD.n)
                        q.wait_ge(DV.h, DV.n)
                        q.dma_start(out=s1_sb[:, tl * D:(tl + 1) * D],
                                    in_=sc_s1[tt * 128:(tt + 1) * 128, :]
                                    ).then_inc(LD.h, 16)
                        LD.n += 16
                    s1_ld = LD.n

                    @block.scalar
                    def _(s, tl=tl, s1_ld=s1_ld):
                        s.wait_ge(LD.h, LD.n)
                        s.activation(gxt[:], s1_sb[:, tl * D:(tl + 1) * D],
                                     AF.Square, accum_out=stat[:, 7:8])
                        s.activation(stat[:, 6:7], stat[:, 7:8], AF.Ln,
                                     bias=constv[:, 1:2], scale=constv[:, 3:4])
                        s.activation(rstd2c[:], stat[:, 6:7], AF.Exp,
                                     scale=constv[:, 2:3]).then_inc(AC.h, 1)
                        AC.n += 1

                    @block.vector
                    def _(v, tl=tl):
                        v.wait_ge(AC.h, AC.n)
                        v.wait_ge(LD.h, LD.n)
                        v.scalar_tensor_tensor(h2bt[:], s1_sb[:, tl * D:(tl + 1) * D],
                                               rstd2c[:], gfbc[:],
                                               ALU.mult, ALU.mult).then_inc(DV.h, 1)
                        DV.n += 1

                    @block.sync
                    def _(q, tt=tt):
                        q.wait_ge(DV.h, DV.n)
                        q.dma_start(out=sc_h2[tt * 128:(tt + 1) * 128, :],
                                    in_=h2bt[:]).then_inc(LD.h, 16)
                        LD.n += 16
                    h2sp[0] = LD.n
                h2_spilled = LD.n

                @block.sync
                def _(q, th=th, toff=toff):
                    q.wait_ge(LD.h, LD.n)
                    q.wait_ge(PE.h, PE.n)
                    for fc in range(8):
                        q.dma_start(out=h2fs[fc][:],
                                    in_=sc_h2[toff:toff + THS,
                                              fc * 128:(fc + 1) * 128],
                                    transpose=True).then_inc(LD.h, 16)
                        LD.n += 16
                h2f_ld = LD.n

                for dh in range(2):
                    chunks = DFF_HALF_CHUNKS[dh]
                    w_half = DFF_HALF_W[dh]
                    off = DFF_HALF_OFF[dh]

                    @block.gpsimd
                    def _(gq, dh=dh, off=off, w_half=w_half):
                        gq.wait_ge(PE.h, PE.n)
                        for fc in range(8):
                            gq.dma_start(out=w1d[:, fc * 1408:fc * 1408 + w_half],
                                         in_=w1[fc * 128:(fc + 1) * 128,
                                                off:off + w_half]).then_inc(GD.h, 16)
                            gq.dma_start(out=w2d[:, fc * 1408:fc * 1408 + w_half],
                                         in_=w2[fc * 128:(fc + 1) * 128,
                                                off:off + w_half]).then_inc(GD.h, 16)
                            GD.n += 32
                        poff = 0
                        for ci, cw in enumerate(chunks):
                            gq.dma_start(out=w3d[0:cw, ci * D:(ci + 1) * D],
                                         in_=w3[off + poff:off + poff + cw, :]
                                         ).then_inc(GD.h, 16)
                            GD.n += 16
                            poff += cw
                    d_gd_now = [GD.n]
                    d_ld_now = [LD.n]

                    for g in range(ng_th):
                        poff = 0
                        for ci, cw in enumerate(chunks):
                            par = ci % 2
                            Z0, Z1 = (P_P0, P_P1) if par == 0 else (P_M, P_BC)
                            Z0n, Z1n = (("P_P0", "P_P1") if par == 0
                                        else ("P_M", "P_BC"))
                            ZS = zsil[par]

                            @block.tensor
                            def _(t, g=g, ci=ci, cw=cw, poff=poff,
                                  Z0=Z0, Z1=Z1, Z0n=Z0n, Z1n=Z1n):
                                t.wait_ge(GD.h, d_gd_now[0])
                                t.wait_ge(LD.h, d_ld_now[0])
                                bwait(t, Z0n, Z1n)
                                last = None
                                for fc in range(8):
                                    rhs = h2fs[fc][:, g * GS:(g + 1) * GS]
                                    t.matmul(Z0[0:cw, :],
                                             w1d[:, fc * 1408 + poff:
                                                  fc * 1408 + poff + cw],
                                             rhs, start=(fc == 0), stop=(fc == 7))
                                    last = t.matmul(
                                        Z1[0:cw, :],
                                        w2d[:, fc * 1408 + poff:
                                             fc * 1408 + poff + cw],
                                        rhs, start=(fc == 0), stop=(fc == 7))
                                last.then_inc(PE.h, 1)
                                PE.n += 1

                            @block.scalar
                            def _(s, ci=ci, cw=cw, Z0=Z0, ZS=ZS, par=par):
                                s.wait_ge(PE.h, PE.n)
                                bwait(s, "ZS%d" % par)
                                s.activation(ZS[0:cw, :], Z0[0:cw, :], AF.Silu
                                             ).then_inc(AC.h, 1)
                                AC.n += 1
                            bfree(Z0n, AC)

                            @block.vector
                            def _(v, ci=ci, cw=cw, Z1=Z1, ZS=ZS):
                                v.wait_ge(AC.h, AC.n)
                                v.tensor_tensor(zbuf[0:cw, ci * GS:(ci + 1) * GS],
                                                Z1[0:cw, :], ZS[0:cw, :],
                                                ALU.mult).then_inc(DV.h, 1)
                                DV.n += 1
                            bfree(Z1n, DV)
                            bfree("ZS%d" % par, DV)
                            poff += cw

                        z_done_dv = DV.n
                        for tl in range(NTG):
                            tlg = g * NTG + tl
                            par = tl % 2
                            W0, W1_ = (P_A, P_B) if par == 0 else (P_N, P_TRF)
                            W0n, W1n = (("P_A", "P_B") if par == 0
                                        else ("P_N", "P_TRF"))

                            @block.tensor
                            def _(t, g=g, tl=tl, chunks=chunks,
                                  W0=W0, W1_=W1_, W0n=W0n, W1n=W1n,
                                  z_done_dv=z_done_dv):
                                t.wait_ge(DV.h, z_done_dv)
                                bwait(t, W0n, W1n)
                                nchk = len(chunks)
                                last = None
                                for ci, cw in enumerate(chunks):
                                    lhs = zbuf[0:cw, ci * GS + tl * 128:
                                               ci * GS + (tl + 1) * 128]
                                    t.matmul(W0[:, 0:512], lhs,
                                             w3d[0:cw, ci * D:ci * D + 512],
                                             start=(ci == 0), stop=(ci == nchk - 1))
                                    last = t.matmul(
                                        W1_[:, 0:512], lhs,
                                        w3d[0:cw, ci * D + 512:(ci + 1) * D],
                                        start=(ci == 0), stop=(ci == nchk - 1))
                                last.then_inc(PE.h, 1)
                                PE.n += 1

                            @block.vector
                            def _(v, tlg=tlg, W0=W0, W1_=W1_):
                                v.wait_ge(PE.h, PE.n)
                                v.tensor_tensor(s1_sb[:, tlg * D:tlg * D + 512],
                                                W0[:, 0:512],
                                                s1_sb[:, tlg * D:tlg * D + 512],
                                                ALU.add)
                                v.tensor_tensor(s1_sb[:, tlg * D + 512:(tlg + 1) * D],
                                                W1_[:, 0:512],
                                                s1_sb[:, tlg * D + 512:(tlg + 1) * D],
                                                ALU.add).then_inc(DV.h, 1)
                                DV.n += 1
                            bfree(W0n, DV)
                            bfree(W1n, DV)

                for tl in range(nt_th):
                    tt = toff // 128 + tl

                    @block.sync
                    def _(q, tt=tt, th=th, tl=tl):
                        if th == 0 and tl == 0:
                            q.wait_ge(LD.h, LD.n)
                        q.wait_ge(DV.h, DV.n)
                        q.dma_start(out=gxt[:], in_=sc_g[tt * 128:(tt + 1) * 128, :]
                                    ).then_inc(LD.h, 16)
                        q.dma_start(out=xrt[:], in_=x_in[tt * 128:(tt + 1) * 128, :]
                                    ).then_inc(LD.h, 16)
                        LD.n += 32
                    e_ld = LD.n

                    @block.vector
                    def _(v, tl=tl, e_ld=e_ld):
                        v.wait_ge(LD.h, LD.n)
                        v.wait_ge(OUT.h, OUT.n)
                        v.tensor_mul(rt[:], s1_sb[:, tl * D:(tl + 1) * D], gxt[:])
                        v.tensor_add(rt[:], rt[:], xrt[:]).then_inc(DV.h, 1)
                        DV.n += 1

                    @block.gpsimd
                    def _(gq, tt=tt):
                        gq.wait_ge(DV.h, DV.n)
                        gq.dma_start(out=out_t[tt * 128:(tt + 1) * 128, :],
                                     in_=rt[:]).then_inc(OUT.h, 16)
                        OUT.n += 16

            @block.gpsimd
            def _(gq):
                gq.wait_ge(OUT.h, OUT.n)

            barrier()
            close_scope("D")
        except _StopBuild:
            pass


    return nc


_MASK_CACHE = {}


def _consts(core_id):
    if core_id in _MASK_CACHE:
        return _MASK_CACHE[core_id]
    ones_b = np.ones((1, 128), ml_dtypes.bfloat16)
    ones_f = np.ones((1, 128), np.float32)
    ones2_f = np.zeros((128, 2), np.float32)
    ones2_f[0:64, 0] = 1.0
    ones2_f[64:128, 1] = 1.0
    bcast2_f = np.zeros((2, 128), np.float32)
    bcast2_f[0, 0:64] = 1.0
    bcast2_f[1, 64:128] = 1.0
    p = np.arange(128)[:, None] % 64
    q = np.arange(C)[None, :]
    mask_tri = (q >= p).astype(np.float32)
    id_b = np.eye(128, dtype=ml_dtypes.bfloat16)
    id_f = np.eye(128, dtype=np.float32)
    rmask = np.zeros((128, NCORES), np.float32)
    rmask[:, :core_id] = 1.0
    rmaskbar = 1.0 - rmask
    d = dict(ones_b=ones_b, ones_f=ones_f, ones2_f=ones2_f,
             bcast2_f=bcast2_f, mask_tri=mask_tri, id_b=id_b, id_f=id_f,
             rmask=rmask, rmaskbar=rmaskbar)
    _MASK_CACHE[core_id] = d
    return d


def kernel(**inputs):
    x = np.ascontiguousarray(np.asarray(inputs["L0_attn_x"])[0])
    T = x.shape[0]
    TC = T // NCORES
    w = {k: np.ascontiguousarray(np.asarray(v)) for k, v in inputs.items()}
    b1 = w["L0_ffn_b1"]
    b1r = np.ascontiguousarray(
        b1.reshape(NE, HME // 128, 128).transpose(0, 2, 1))

    base = dict(
        Wq=w["L0_attn_Wq"], Wk=w["L0_attn_Wk"], Wv=w["L0_attn_Wv"],
        Wg=w["L0_attn_Wg"], Wo=w["L0_attn_Wo"],
        w1=w["L0_attn_w1"], w2=w["L0_attn_w2"], w3=w["L0_attn_w3"],
        g_attn=w["L0_attn_g_attn"].reshape(1, D),
        g_ffn=w["L0_attn_g_ffn"].reshape(1, D),
        Wr=w["L0_ffn_Wr"], br=w["L0_ffn_br"].reshape(1, NE),
        W1=w["L0_ffn_W1"], b1r=b1r, W2=w["L0_ffn_W2"], b2=w["L0_ffn_b2"],
    )

    nc = build_nc(TC=TC, stop_after=globals().get("_STOP_AFTER"))
    in_maps = []
    for c in range(NCORES):
        m = dict(base)
        m["x"] = np.ascontiguousarray(x[c * TC:(c + 1) * TC])
        m.update(_consts(c))
        in_maps.append(m)

    res = run_bass_kernel_spmd(nc, in_maps, core_ids=list(range(NCORES)))
    globals()["_LAST"] = res.results
    out = np.concatenate([np.asarray(r["out"]) for r in res.results], axis=0)
    return out[None]
